# version 01
# baseline (speedup 1.0000x reference)
import numpy as np
import ml_dtypes
from contextlib import ExitStack
from collections import defaultdict
import concourse.bass as bass
import concourse.mybir as mybir
from concourse.bass_utils import run_bass_kernel_spmd

F32 = mybir.dt.float32
BF16 = mybir.dt.bfloat16
AF = mybir.ActivationFunctionType
ALU = mybir.AluOpType
AX = mybir.AxisListType
BF = ml_dtypes.bfloat16

T = 2048
D = 2048
NCORE = 8
NEG = -1e30
ENGS = ("tensor", "vector", "scalar", "gpsimd", "sync")
CPOOL_N = 32


def _sylvester(n):
    H = np.array([[1.0]], dtype=np.float64)
    while H.shape[0] < n:
        H = np.block([[H, H], [H, -H]])
    return H


_HOST_CONSTS = None


def host_constants():
    global _HOST_CONSTS
    if _HOST_CONSTS is not None:
        return _HOST_CONSTS
    c = {}
    c["identb"] = np.eye(128, dtype=BF)
    c["identf"] = np.eye(128, dtype=np.float32)
    r = np.arange(128)
    c["m_upincl_neg"] = np.where(r[:, None] <= r[None, :], 0.0, NEG).astype(np.float32)
    c["m_upstrict_neg"] = np.where(r[:, None] < r[None, :], 0.0, NEG).astype(np.float32)
    c["m_lowstrict_pos"] = np.where(r[None, :] < r[:, None], 0.0, -NEG).astype(np.float32)
    c["twoI"] = (2.0 * np.eye(128)).astype(np.float32)
    c["IH16"] = np.kron(np.eye(8), _sylvester(16)).astype(BF)
    c["IH64"] = np.kron(np.eye(2), _sylvester(64)).astype(BF)
    c["H128"] = _sylvester(128).astype(BF)
    c["ones_row"] = np.ones((1, 128), dtype=BF)
    c["onesf_row"] = np.ones((1, 128), dtype=np.float32)
    c["onescol_b"] = np.ones((128, 1), dtype=BF)
    _HOST_CONSTS = c
    return c


class Prog:
    def __init__(self, name="prog"):
        self.name = name
        self.nc = bass.Bass("TRN2", target_bir_lowering=False)
        self.es = ExitStack()
        self._sems = {}
        self.tot = defaultdict(int)
        self.seen = {e: defaultdict(int) for e in ENGS}
        self.block = None
        self.cvals = []
        self._cpool = None
        self.snaps = {}

    def sem(self, name):
        if name not in self._sems:
            self._sems[name] = self.es.enter_context(self.nc.semaphore(name))
        return self._sems[name]

    def sbuf(self, name, shape, dt):
        return self.es.enter_context(self.nc.sbuf_tensor(name, shape, dt))

    def psum(self, name, shape, dt=F32):
        return self.es.enter_context(self.nc.psum_tensor(name, shape, dt))

    def dram(self, name, shape, dt, kind="Internal"):
        return self.nc.dram_tensor(name, shape, dt, kind=kind).ap()

    def cscalar(self, v):
        v = float(v)
        if self._cpool is None:
            self._cpool = self.sbuf("cpool", [128, CPOOL_N], F32)
        if v not in self.cvals:
            assert len(self.cvals) < CPOOL_N
            self.cvals.append(v)
            i = self.cvals.index(v)
            self.nc.const_aps.register(F32, v, self._cpool[:, i:i + 1])
        return v

    def inc(self, inst, name, n=1):
        inst.then_inc(self.sem(name), n)
        self.tot[name] += n
        return (name, self.tot[name])

    def tok(self, name):
        return (name, self.tot[name])

    def wait(self, eng, engname, *toks):
        for tk in toks:
            if tk is None:
                continue
            name, val = tk
            if val > 0 and self.seen[engname][name] < val:
                eng.wait_ge(self.sem(name), val)
                self.seen[engname][name] = val

    def snapshot(self, key):
        self.snaps[key] = dict(self.tot)

    def snap_tok(self, key, name):
        if key not in self.snaps:
            return None
        return (name, self.snaps[key].get(name, 0))

    def wait_snap(self, eng, engname, key, names):
        for n in names:
            self.wait(eng, engname, self.snap_tok(key, n))

    def final_wait(self, eng, engname):
        for name in list(self.tot.keys()):
            self.wait(eng, engname, (name, self.tot[name]))


def load_consts(p, names):
    cvals = host_constants()
    out = {}
    aps = {}
    dt_map = {np.dtype(np.float32): F32, np.dtype(BF): BF16}
    for n in names:
        a = cvals[n]
        aps[n] = p.dram("c_" + n, list(a.shape), dt_map[a.dtype], kind="ExternalInput")
        out[n] = p.sbuf("cs_" + n, list(a.shape), dt_map[a.dtype])
    cpool_in = p.dram("cpool_in", [128, CPOOL_N], F32, kind="ExternalInput")

    def emit(g):
        if p._cpool is None:
            p._cpool = p.sbuf("cpool", [128, CPOOL_N], F32)
        last = g.dma_start(out=p._cpool[:], in_=cpool_in[:, :])
        p.inc(last, "cld", 16)
        for n in names:
            last = g.dma_start(out=out[n][:], in_=aps[n][:, :] if aps[n].ndim == 2 else aps[n][:])
            p.inc(last, "cld", 16)
    return out, emit


def const_inputs(p, names):
    cvals = host_constants()
    d = {"c_" + n: cvals[n] for n in names}
    pool = np.zeros((128, CPOOL_N), dtype=np.float32)
    for i, v in enumerate(p.cvals):
        pool[:, i] = v
    d["cpool_in"] = pool
    return d


def emit_ln0(p, csb):
    x_in = p.dram("x_in", [T, D], F32, kind="ExternalInput")
    xacc = p.dram("xacc", [D, T], F32, kind="ExternalOutput")
    xn0 = p.dram("xn0", [T, D], BF16, kind="ExternalOutput")

    p.cscalar(1e-5)
    p.cscalar(-0.5)

    xt = [p.sbuf(f"ln_x{i}", [128, D], F32) for i in range(2)]
    xnb = [p.sbuf(f"ln_xn{i}", [128, D], BF16) for i in range(2)]
    sq = p.sbuf("ln_sq", [128, D], F32)
    cols = [p.sbuf(f"ln_cols{i}", [128, 8], F32) for i in range(2)]
    stg = [p.sbuf(f"ln_stg{i}", [128, 128], F32) for i in range(4)]
    pst = [p.psum(f"ln_pst{i}", [128, 128], F32) for i in range(2)]
    identf = csb["identf"]

    NT = T // 128
    for tc in range(NT):
        b = tc % 2

        @p.block.gpsimd
        def _(g):
            p.wait_snap(g, "gpsimd", tc - 2, [f"lnxn{b}", "lnev0", "lnev1", "s", "v", "xnst"])
            e = g.dma_start(out=xt[b][:], in_=x_in[tc * 128:(tc + 1) * 128, :])
            p.inc(e, f"lnld{b}", 16)

        @p.block.scalar
        def _(s):
            p.wait(s, "scalar", p.tok(f"lnld{b}"), p.tok("cld"))
            e = s.activation(sq[:], xt[b][:], AF.Square, accum_out=cols[b][:, 1:2])
            p.inc(e, "s", 1)

        @p.block.vector
        def _(v):
            p.wait(v, "vector", p.tok(f"lnld{b}"))
            e = v.reduce_sum(cols[b][:, 0:1], xt[b][:], axis=AX.X)
            tk = p.inc(e, "v", 1)
            p.wait(v, "vector", tk)
            e = v.tensor_scalar_mul(cols[b][:, 2:3], cols[b][:, 0:1], -1.0 / D)
            tk = p.inc(e, "v", 1)
            p.wait(v, "vector", tk, p.tok("s"))
            e = v.scalar_tensor_tensor(cols[b][:, 3:4], cols[b][:, 0:1], cols[b][:, 2:3], cols[b][:, 1:2], ALU.mult, ALU.add)
            tk = p.inc(e, "v", 1)
            p.wait(v, "vector", tk)
            e = v.tensor_scalar_mul(cols[b][:, 4:5], cols[b][:, 3:4], 1.0 / D)
            p.inc(e, "v", 1)

        @p.block.scalar
        def _(s):
            p.wait(s, "scalar", p.tok("v"))
            e = s.activation(cols[b][:, 6:7], cols[b][:, 4:5], AF.Ln, bias=1e-5)
            tk = p.inc(e, "s", 1)
            p.wait(s, "scalar", tk)
            e = s.activation(cols[b][:, 5:6], cols[b][:, 6:7], AF.Exp, scale=-0.5)
            p.inc(e, "s", 1)

        @p.block.vector
        def _(v):
            p.wait(v, "vector", p.tok("s"))
            e = v.tensor_scalar(xnb[b][:], xt[b][:], cols[b][:, 2:3], cols[b][:, 5:6], ALU.add, ALU.mult)
            p.inc(e, f"lnxn{b}", 1)

        @p.block.gpsimd
        def _(g):
            p.wait(g, "gpsimd", p.tok(f"lnxn{b}"))
            e = g.dma_start(out=xn0[tc * 128:(tc + 1) * 128, :], in_=xnb[b][:])
            p.inc(e, "xnst", 16)

        for kc in range(D // 128):
            pb = kc % 2
            sb = kc % 4
            idx = tc * 16 + kc

            @p.block.tensor
            def _(t):
                p.wait(t, "tensor", p.tok(f"lnld{b}"), p.tok("cld"))
                if idx >= 2:
                    p.wait(t, "tensor", (f"lnev{pb}", (idx - 2) // 2 + 1))
                e = t.transpose(pst[pb][:, :], xt[b][:, kc * 128:(kc + 1) * 128], identf[:, :])
                p.inc(e, f"lntr{pb}", 1)

            @p.block.vector
            def _(v):
                p.wait(v, "vector", p.tok(f"lntr{pb}"))
                if idx >= 4:
                    p.wait(v, "vector", (f"lnst{sb}", ((idx - 4) // 4 + 1) * 16))
                e = v.tensor_copy(stg[sb][:], pst[pb][:, :])
                p.inc(e, f"lnev{pb}", 1)

            @p.block.sync
            def _(sy):
                p.wait(sy, "sync", p.tok(f"lnev{pb}"))
                e = sy.dma_start(out=xacc[kc * 128:(kc + 1) * 128, tc * 128:(tc + 1) * 128], in_=stg[sb][:])
                p.inc(e, f"lnst{sb}", 16)

        p.snapshot(tc)

    @p.block.sync
    def _(sy):
        p.final_wait(sy, "sync")


def emit_final(p, csb, src_name="xacc_in"):
    xacc = p.dram(src_name, [D, T], F32, kind="ExternalInput")
    out = p.dram("y_out", [T, D], F32, kind="ExternalOutput")

    xt = [p.sbuf(f"fn_x{i}", [128, T], F32) for i in range(2)]
    stg = [p.sbuf(f"fn_stg{i}", [128, 128], F32) for i in range(4)]
    pst = [p.psum(f"fn_pst{i}", [128, 128], F32) for i in range(2)]
    identf = csb["identf"]

    ND = D // 128
    for dc in range(ND):
        b = dc % 2

        @p.block.gpsimd
        def _(g):
            p.wait_snap(g, "gpsimd", dc - 2, ["fnev0", "fnev1"])
            e = g.dma_start(out=xt[b][:], in_=xacc[dc * 128:(dc + 1) * 128, :])
            p.inc(e, f"fnld{b}", 16)

        for tc in range(T // 128):
            pb = tc % 2
            sb = tc % 4
            idx = dc * 16 + tc

            @p.block.tensor
            def _(t):
                p.wait(t, "tensor", p.tok(f"fnld{b}"), p.tok("cld"))
                if idx >= 2:
                    p.wait(t, "tensor", (f"fnev{pb}", (idx - 2) // 2 + 1))
                e = t.transpose(pst[pb][:, :], xt[b][:, tc * 128:(tc + 1) * 128], identf[:, :])
                p.inc(e, f"fntr{pb}", 1)

            @p.block.vector
            def _(v):
                p.wait(v, "vector", p.tok(f"fntr{pb}"))
                if idx >= 4:
                    p.wait(v, "vector", (f"fnst{sb}", ((idx - 4) // 4 + 1) * 16))
                e = v.tensor_copy(stg[sb][:], pst[pb][:, :])
                p.inc(e, f"fnev{pb}", 1)

            @p.block.sync
            def _(sy):
                p.wait(sy, "sync", p.tok(f"fnev{pb}"))
                e = sy.dma_start(out=out[tc * 128:(tc + 1) * 128, dc * 128:(dc + 1) * 128], in_=stg[sb][:])
                p.inc(e, f"fnst{sb}", 16)

        p.snapshot(dc)

    @p.block.sync
    def _(sy):
        p.final_wait(sy, "sync")


def build_ln0():
    p = Prog("ln0")
    with p.es:
        csb, cemit = load_consts(p, ["identf"])
        with p.nc.Block() as blk:
            p.block = blk

            @blk.gpsimd
            def _(g):
                cemit(g)
            emit_ln0(p, csb)
    return p


def build_final():
    p = Prog("final")
    with p.es:
        csb, cemit = load_consts(p, ["identf"])
        with p.nc.Block() as blk:
            p.block = blk

            @blk.gpsimd
            def _(g):
                cemit(g)
            emit_final(p, csb)
    return p


def emit_prep_cast(p, g, src_ap, dst_ap, tokname, chunk_rows=4096):
    n = src_ap.shape[0]
    r = 0
    while r < n:
        m = min(chunk_rows, n - r)
        e = g.dma_start(out=dst_ap[r:r + m, :], in_=src_ap[r:r + m, :])
        p.inc(e, tokname, 16)
        r += m


def emit_load_xn(p, g_or_sy, xn_ap, xn_sb, tokname, tm=True):
    for dc in range(D // 128):
        if tm:
            e = g_or_sy.dma_start(out=xn_sb[:, dc * T:(dc + 1) * T],
                                  in_=xn_ap[0:T, dc * 128:(dc + 1) * 128], transpose=True)
        else:
            e = g_or_sy.dma_start(out=xn_sb[:, dc * T:(dc + 1) * T],
                                  in_=xn_ap[dc * 128:(dc + 1) * 128, :])
        p.inc(e, tokname, 16)


def emit_prep_cast(p, g, src_ap, dst_ap, tokname, chunk_rows=4096):
    n = src_ap.shape[0]
    r = 0
    while r < n:
        m = min(chunk_rows, n - r)
        e = g.dma_start(out=dst_ap[r:r + m, :], in_=src_ap[r:r + m, :])
        p.inc(e, tokname, 16)
        r += m


def emit_load_xn(p, eng, xn_ap, xn_sb, tokname, tm=True):
    for dc in range(D // 128):
        if tm:
            e = eng.dma_start(out=xn_sb[:, dc * T:(dc + 1) * T],
                              in_=xn_ap[0:T, dc * 128:(dc + 1) * 128], transpose=True)
        else:
            e = eng.dma_start(out=xn_sb[:, dc * T:(dc + 1) * T],
                              in_=xn_ap[dc * 128:(dc + 1) * 128, :])
        p.inc(e, tokname, 16)


class PsumPool:
    def __init__(self, p, name, n, shape=(128, 512)):
        self.p = p
        self.tiles = [p.psum(f"{name}{i}", list(shape)) for i in range(n)]
        self.free_tok = [None] * n
        self.i = 0

    def acquire(self):
        i = self.i
        self.i = (self.i + 1) % len(self.tiles)
        return i, self.tiles[i], self.free_tok[i]

    def release(self, i, tok):
        self.free_tok[i] = tok


def emit_l0_moe(p, csb):
    NH = 1024
    HS = 2048
    xn0 = p.dram("xn0_in", [T, D], BF16, kind="ExternalInput")
    xacc_in = p.dram("xacc_in", [D, T], F32, kind="ExternalInput")
    xacc = p.dram("xacc", [D, T], F32, kind="ExternalOutput")
    gate_w = p.dram("gate_w", [8, D], F32, kind="ExternalInput")
    w_gate = p.dram("w_gate", [8, NH, D], F32, kind="ExternalInput")
    w_up = p.dram("w_up", [8, NH, D], F32, kind="ExternalInput")
    w_down = p.dram("w_down", [8, D, NH], F32, kind="ExternalInput")
    sh_gate = p.dram("sh_gate", [HS, D], F32, kind="ExternalInput")
    sh_up = p.dram("sh_up", [HS, D], F32, kind="ExternalInput")
    sh_down = p.dram("sh_down", [D, HS], F32, kind="ExternalInput")
    se_gate_w = p.dram("se_gate_w", [1, D], F32, kind="ExternalInput")

    wg_bf = p.dram("wg_bf", [8 * NH, D], BF16)
    wu_bf = p.dram("wu_bf", [8 * NH, D], BF16)
    wd_bf = p.dram("wd_bf", [8 * D, NH], BF16)
    shg_bf = p.dram("shg_bf", [HS, D], BF16)
    shu_bf = p.dram("shu_bf", [HS, D], BF16)
    shd_bf = p.dram("shd_bf", [D, HS], BF16)

    identf = csb["identf"]
    ones_row = csb["ones_row"]

    xn_sb = p.sbuf("moe_xn", [128, 16 * T], BF16)
    gsc_sb = p.sbuf("moe_gsc", [9, D], F32)
    gwT = p.sbuf("moe_gwT", [128, 16 * 9], BF16)
    sc = p.sbuf("moe_sc", [128, 18], F32)
    eqb = p.sbuf("moe_eqb", [128, 32], F32)
    combtm = p.sbuf("moe_combtm", [128, 16], F32)
    combT = p.sbuf("moe_combT", [16, T], F32)
    crow1 = p.sbuf("moe_crow1", [1, T], BF16)
    wub = p.sbuf("moe_wub", [128, 16 * 512], BF16)
    wgb = p.sbuf("moe_wgb", [128, 16 * 512], BF16)
    wdb = p.sbuf("moe_wdb", [128, 16 * 512], BF16)
    h_sb = p.sbuf("moe_h", [128, 16 * 1024], BF16)
    sgs = [p.sbuf(f"moe_sg{i}", [128, 512], F32) for i in range(2)]
    tps = [p.sbuf(f"moe_tp{i}", [128, 512], F32) for i in range(2)]
    stg = [p.sbuf(f"moe_stg{i}", [128, 512], F32) for i in range(4)]

    pool = PsumPool(p, "moe_pp", 5)
    cps = [p.psum(f"moe_cps{i}", [128, 512], F32) for i in range(2)]
    arena = p.psum("moe_ar", [128, 512], F32)

    @p.block.gpsimd
    def _(g):
        e = g.dma_start(out=gsc_sb[0:8, :], in_=gate_w[:, :])
        p.inc(e, "gsc", 16)
        e = g.dma_start(out=gsc_sb[8:9, :], in_=se_gate_w[:, :])
        p.inc(e, "gsc", 16)
        emit_prep_cast(p, g, w_gate.rearrange("e h d -> (e h) d"), wg_bf, "pw_ug")
        emit_prep_cast(p, g, w_up.rearrange("e h d -> (e h) d"), wu_bf, "pw_ug")
        emit_prep_cast(p, g, sh_gate, shg_bf, "pw_ug")
        emit_prep_cast(p, g, sh_up, shu_bf, "pw_ug")
        emit_prep_cast(p, g, w_down.rearrange("e d h -> (e d) h"), wd_bf, "pw_d")
        emit_prep_cast(p, g, sh_down, shd_bf, "pw_d")
        for i in range(4):
            e = g.dma_start(out=xacc[i * 512:(i + 1) * 512, :], in_=xacc_in[i * 512:(i + 1) * 512, :])
            p.inc(e, "cpacc", 16)

    @p.block.sync
    def _(sy):
        emit_load_xn(p, sy, xn0, xn_sb, "xnld", tm=True)

    for kc in range(16):
        @p.block.tensor
        def _(t):
            p.wait(t, "tensor", p.tok("gsc"), p.tok("cld"))
            if kc >= 1:
                p.wait(t, "tensor", p.tok("gscev"))
            e = t.transpose(arena[:, 0:9], gsc_sb[0:9, kc * 128:(kc + 1) * 128], identf[0:9, 0:9])
            p.inc(e, "gsctr", 1)

        @p.block.vector
        def _(v):
            p.wait(v, "vector", p.tok("gsctr"))
            e = v.tensor_copy(gwT[:, kc * 9:(kc + 1) * 9], arena[0:128, 0:9] if False else arena[:, 0:9])
            p.inc(e, "gscev", 1)

    for tc in range(16):
        @p.block.tensor
        def _(t):
            p.wait(t, "tensor", p.tok("xnld"), p.tok("gscev"))
            if tc > 0:
                p.wait(t, "tensor", p.tok("ctrev"))
            e = None
            for kc in range(16):
                e = t.matmul(arena[:, 0:9], xn_sb[:, kc * T + tc * 128: kc * T + tc * 128 + 128],
                             gwT[:, kc * 9:(kc + 1) * 9], start=(kc == 0), stop=(kc == 15))
            p.inc(e, "scmm", 1)

        @p.block.vector
        def _(v):
            p.wait(v, "vector", p.tok("scmm"))
            e = v.tensor_copy(sc[:, 0:9], arena[:, 0:9])
            tk = p.inc(e, "scev", 1)
            p.wait(v, "vector", tk)
            e = v.reduce_max(sc[:, 12:13], sc[:, 0:8], axis=AX.X)
            tk = p.inc(e, "v", 1)
            p.wait(v, "vector", tk)
            e = v.tensor_scalar(eqb[:, 0:8], sc[:, 0:8], sc[:, 12:13], None, ALU.is_equal)
            tk = p.inc(e, "v", 1)
            p.wait(v, "vector", tk)
            e = v.scalar_tensor_tensor(eqb[:, 8:16], eqb[:, 0:8], float(NEG), sc[:, 0:8], ALU.mult, ALU.add)
            tk = p.inc(e, "v", 1)
            p.wait(v, "vector", tk)
            e = v.reduce_max(sc[:, 13:14], eqb[:, 8:16], axis=AX.X)
            tk = p.inc(e, "v", 1)
            p.wait(v, "vector", tk)
            e = v.tensor_scalar(eqb[:, 16:24], eqb[:, 8:16], sc[:, 13:14], None, ALU.is_equal)
            tk = p.inc(e, "v", 1)
            p.wait(v, "vector", tk)
            e = v.tensor_sub(sc[:, 14:15], sc[:, 12:13], sc[:, 13:14])
            p.inc(e, "sigin", 1)

        @p.block.scalar
        def _(s):
            p.wait(s, "scalar", p.tok("sigin"))
            e = s.activation(sc[:, 15:16], sc[:, 14:15], AF.Sigmoid)
            p.inc(e, "s", 1)
            e = s.activation(combtm[:, 8:9], sc[:, 8:9], AF.Sigmoid)
            p.inc(e, "s", 1)

        @p.block.vector
        def _(v):
            p.wait(v, "vector", p.tok("s"))
            e = v.tensor_scalar(sc[:, 16:17], sc[:, 15:16], -1.0, 1.0, ALU.mult, ALU.add)
            tk = p.inc(e, "v", 1)
            p.wait(v, "vector", tk)
            e = v.tensor_scalar(eqb[:, 24:32], eqb[:, 0:8], sc[:, 15:16], None, ALU.mult)
            tk = p.inc(e, "v", 1)
            p.wait(v, "vector", tk)
            e = v.scalar_tensor_tensor(combtm[:, 0:8], eqb[:, 16:24], sc[:, 16:17], eqb[:, 24:32], ALU.mult, ALU.add)
            p.inc(e, "combv", 1)
            if tc == 0:
                e = v.memset(combtm[:, 9:16], 0.0)
                p.inc(e, "combv", 1)

        @p.block.tensor
        def _(t):
            p.wait(t, "tensor", p.tok("combv"))
            e = t.transpose(arena[0:16, 128:256], combtm[:, 0:16], identf[:, :])
            p.inc(e, "ctr", 1)

        @p.block.vector
        def _(v):
            p.wait(v, "vector", p.tok("ctr"))
            e = v.tensor_copy(combT[:, tc * 128:(tc + 1) * 128], arena[0:16, 128:256])
            p.inc(e, "ctrev", 1)


    GROUPS = globals().get("MOE_GROUPS") or [([0, 1], False), ([2, 3], False), ([4, 5], False), ([6, 7], False), ([], True)]
    last_upgate_mm = None
    last_down_mm = None
    cps_free = [None, None]
    sg_free = [None, None]
    tp_free = [None, None]
    stg_free = [None] * 4

    for th in range(2):
        for (experts, has_sh) in GROUPS:
            units = []
            for ei, e_ in enumerate(experts):
                units.append((wg_bf, wu_bf, e_ * NH, e_, ei * 8))
            if has_sh:
                units.append((shg_bf, shu_bf, 0, 8, len(experts) * 8))
                units.append((shg_bf, shu_bf, NH, 8, len(experts) * 8 + 8))
            nj = len(units) * 8

            for (gsrc, usrc, rlo, crow_q, jbase) in units:
                @p.block.gpsimd
                def _(g):
                    p.wait(g, "gpsimd", p.tok("ctrev"))
                    if cps_free[0] is not None:
                        p.wait(g, "gpsimd", cps_free[0])
                    if cps_free[1] is not None:
                        p.wait(g, "gpsimd", cps_free[1])
                    e = g.dma_start(out=crow1[0:1, :], in_=combT[crow_q:crow_q + 1, :])
                    p.inc(e, "crowld", 16)

                @p.block.tensor
                def _(t):
                    p.wait(t, "tensor", p.tok("crowld"))
                    for tt in range(2):
                        p.wait(t, "tensor", cps_free[tt])
                        e = t.matmul(cps[tt][:, :], ones_row[:, :],
                                     crow1[0:1, th * 1024 + tt * 512: th * 1024 + tt * 512 + 512],
                                     start=True, stop=True)
                        p.inc(e, "t", 1)

                for oh in range(2):
                    @p.block.sync
                    def _(sy):
                        p.wait(sy, "sync", p.tok("pw_ug"))
                        if last_upgate_mm is not None:
                            p.wait(sy, "sync", last_upgate_mm)
                        if last_down_mm is not None:
                            p.wait(sy, "sync", last_down_mm)
                        for kc in range(16):
                            e = sy.dma_start(out=wgb[:, kc * 512:(kc + 1) * 512],
                                             in_=gsrc[rlo + oh * 512:rlo + oh * 512 + 512,
                                                      kc * 128:(kc + 1) * 128], transpose=True)
                            p.inc(e, "wug", 16)
                            e = sy.dma_start(out=wub[:, kc * 512:(kc + 1) * 512],
                                             in_=usrc[rlo + oh * 512:rlo + oh * 512 + 512,
                                                      kc * 128:(kc + 1) * 128], transpose=True)
                            p.inc(e, "wug", 16)

                    for oc4 in range(4):
                        oc = oh * 4 + oc4
                        for tt in range(2):
                            bg, gps, gfree = pool.acquire()
                            bu, ups, ufree = pool.acquire()
                            col0 = th * 1024 + tt * 512

                            @p.block.tensor
                            def _(t):
                                p.wait(t, "tensor", p.tok("wug"), gfree, ufree)
                                e = None
                                for kc in range(16):
                                    e = t.matmul(gps[:, :], wgb[:, kc * 512 + oc4 * 128: kc * 512 + oc4 * 128 + 128],
                                                 xn_sb[:, kc * T + col0: kc * T + col0 + 512],
                                                 start=(kc == 0), stop=(kc == 15))
                                tkg = p.inc(e, "t", 1)
                                for kc in range(16):
                                    e = t.matmul(ups[:, :], wub[:, kc * 512 + oc4 * 128: kc * 512 + oc4 * 128 + 128],
                                                 xn_sb[:, kc * T + col0: kc * T + col0 + 512],
                                                 start=(kc == 0), stop=(kc == 15))
                                tku = p.inc(e, "t", 1)
                                globals()["_moe_tkg"], globals()["_moe_tku"] = tkg, tku

                            tkg, tku = globals()["_moe_tkg"], globals()["_moe_tku"]

                            @p.block.scalar
                            def _(s):
                                p.wait(s, "scalar", tkg, sg_free[tt])
                                e = s.activation(sgs[tt][:], gps[:, :], AF.Silu)
                                globals()["_moe_tks"] = p.inc(e, "s", 1)

                            tks = globals()["_moe_tks"]

                            @p.block.vector
                            def _(v):
                                p.wait(v, "vector", tks, tku, tp_free[tt])
                                if last_down_mm is not None:
                                    p.wait(v, "vector", last_down_mm)
                                e = v.tensor_tensor(tps[tt][:], sgs[tt][:], ups[:, :], ALU.mult)
                                tkv = p.inc(e, "v", 1)
                                p.wait(v, "vector", tkv)
                                e = v.tensor_tensor(h_sb[:, (jbase + oc) * 1024 + tt * 512: (jbase + oc) * 1024 + tt * 512 + 512],
                                                    tps[tt][:], cps[tt][:, :], ALU.mult)
                                tkh = p.inc(e, "hwr", 1)
                                globals()["_moe_tkv"], globals()["_moe_tkh"] = tkv, tkh

                            tkv, tkh = globals()["_moe_tkv"], globals()["_moe_tkh"]
                            pool.release(bg, tks)
                            pool.release(bu, tkv)
                            sg_free[tt] = tkv
                            tp_free[tt] = tkh
                            cps_free[tt] = tkh
                    last_upgate_mm = p.tok("t")
                last_upgate_mm = p.tok("t")

            h_ready = p.tok("hwr")
            if globals().get("DBG_MOE") and th == 0 and experts[:1] == [0]:
                dbg_h0 = p.dram("dbg_h0", [128, 16 * 1024], F32, kind="ExternalOutput")

                @p.block.gpsimd
                def _(g):
                    p.wait(g, "gpsimd", h_ready)
                    e = g.dma_start(out=dbg_h0[:, :], in_=h_sb[:, :])
                    p.inc(e, "dbg", 16)

            acc_base = p.tok("acc")
            prev_og_mm = last_down_mm
            for og in range(4):
                @p.block.sync
                def _(sy):
                    p.wait(sy, "sync", p.tok("pw_d"))
                    if prev_og_mm is not None:
                        p.wait(sy, "sync", prev_og_mm)
                    jj = 0
                    for (gsrc, usrc, rlo, crow_q, jbase) in units:
                        if gsrc is wg_bf:
                            e_ = rlo // NH
                            for kc in range(8):
                                e = sy.dma_start(out=wdb[:, jj * 512:(jj + 1) * 512],
                                                 in_=wd_bf[e_ * D + og * 512: e_ * D + og * 512 + 512,
                                                           kc * 128:(kc + 1) * 128], transpose=True)
                                p.inc(e, "wd", 16)
                                jj += 1
                        else:
                            for kc in range(rlo // 128, rlo // 128 + 8):
                                e = sy.dma_start(out=wdb[:, jj * 512:(jj + 1) * 512],
                                                 in_=shd_bf[og * 512:(og + 1) * 512, kc * 128:(kc + 1) * 128],
                                                 transpose=True)
                                p.inc(e, "wd", 16)
                                jj += 1

                for ol in range(4):
                    for tt in range(2):
                        bk, ps_, free = pool.acquire()

                        @p.block.tensor
                        def _(t):
                            p.wait(t, "tensor", p.tok("wd"), free, h_ready)
                            e = None
                            for j in range(nj):
                                e = t.matmul(ps_[:, :], wdb[:, j * 512 + ol * 128: j * 512 + ol * 128 + 128],
                                             h_sb[:, j * 1024 + tt * 512: j * 1024 + tt * 512 + 512],
                                             start=(j == 0), stop=(j == nj - 1))
                            globals()["_moe_tkm"] = p.inc(e, "t", 1)

                        tkm = globals()["_moe_tkm"]
                        sb_i = (og * 8 + ol * 2 + tt) % 4

                        @p.block.scalar
                        def _(s):
                            p.wait(s, "scalar", tkm, stg_free[sb_i])
                            e = s.copy(stg[sb_i][:], ps_[:, :])
                            globals()["_moe_tke"] = p.inc(e, "dnev", 1)

                        tke = globals()["_moe_tke"]
                        pool.release(bk, tke)

                        @p.block.gpsimd
                        def _(g):
                            p.wait(g, "gpsimd", tke, p.tok("cpacc"), acc_base)
                            e = g.dma_start(out=xacc[og * 512 + ol * 128: og * 512 + ol * 128 + 128,
                                                     th * 1024 + tt * 512: th * 1024 + tt * 512 + 512],
                                            in_=stg[sb_i][:], accum_op=ALU.add)
                            stg_free[sb_i] = p.inc(e, "acc", 16)
                prev_og_mm = p.tok("t")
            last_down_mm = p.tok("t")

    if globals().get("DBG_MOE"):
        dbg_comb = p.dram("dbg_comb", [16, T], F32, kind="ExternalOutput")
        dbg_h = p.dram("dbg_h", [128, 16 * 1024], F32, kind="ExternalOutput")
        dbg_xn = p.dram("dbg_xn", [128, T], F32, kind="ExternalOutput")

        dbg_sc = p.dram("dbg_sc", [128, 18], F32, kind="ExternalOutput")
        dbg_eqb = p.dram("dbg_eqb", [128, 32], F32, kind="ExternalOutput")

        @p.block.gpsimd
        def _(g):
            p.final_wait(g, "gpsimd")
            e = g.dma_start(out=dbg_comb[:, :], in_=combT[:, :])
            p.inc(e, "dbg", 16)
            e = g.dma_start(out=dbg_xn[:, :], in_=xn_sb[:, 0:T])
            p.inc(e, "dbg", 16)
            e = g.dma_start(out=dbg_sc[:, :], in_=sc[:, :])
            p.inc(e, "dbg", 16)
            e = g.dma_start(out=dbg_eqb[:, :], in_=eqb[:, :])
            p.inc(e, "dbg", 16)

    @p.block.sync
    def _(sy):
        p.final_wait(sy, "sync")


DBG_MOE = False


def build_l0_moe():
    p = Prog("l0moe")
    with p.es:
        csb, cemit = load_consts(p, ["identf", "ones_row"])
        with p.nc.Block() as blk:
            p.block = blk

            @blk.gpsimd
            def _(g):
                cemit(g)
            emit_l0_moe(p, csb)
    return p


def _chain(p, steps):
    prev = None
    first = True
    for engname, fn, extra in steps:
        sec = getattr(p.block, engname)
        box = {}
        bar = [p.tok("cht"), p.tok("chv"), p.tok("chs")] if first else []
        first = False

        def body(eng, fn=fn, extra=extra, prevtok=prev, engname=engname, box=box, bar=bar):
            p.wait(eng, engname, prevtok, *(list(extra) + bar))
            inst = fn(eng)
            if inst is not None:
                n = 16 if engname in ("gpsimd", "sync") else 1
                box["tok"] = p.inc(inst, "ch" + engname[0], n)
            else:
                box["tok"] = prevtok
        sec(body)
        prev = box["tok"]
    return prev


def emit_l0_attn(p, csb):
    NK, DK, DV, PP, KC = 16, 128, 128, 2, 4
    QD = NK * DK
    VD = 2 * NK * DV
    xn0 = p.dram("xn0_in", [T, D], BF16, kind="ExternalInput")
    xacc_in = p.dram("xacc_in", [D, T], F32, kind="ExternalInput")
    xacc = p.dram("xacc", [D, T], F32, kind="ExternalOutput")
    W_qkv = p.dram("W_qkv", [2 * QD + VD, D], F32, kind="ExternalInput")
    W_z = p.dram("W_z", [VD, D], F32, kind="ExternalInput")
    W_b = p.dram("W_b", [32, D], F32, kind="ExternalInput")
    W_a = p.dram("W_a", [32, D], F32, kind="ExternalInput")
    conv_w = p.dram("conv_w", [2 * QD + VD, KC], F32, kind="ExternalInput")
    dt_bias = p.dram("dt_bias", [32], F32, kind="ExternalInput")
    A_log = p.dram("A_log", [32], F32, kind="ExternalInput")
    norm_w = p.dram("norm_w", [128], F32, kind="ExternalInput")
    W_out = p.dram("W_out", [D, VD], F32, kind="ExternalInput")

    wqkv_bf = p.dram("wqkv_bf", [2 * QD + VD, D], BF16)
    wz_bf = p.dram("wz_bf", [VD, D], BF16)
    wout_bf = p.dram("wout_bf", [D, VD], BF16)

    identf = csb["identf"]
    identb = csb["identb"]
    ones_row = csb["ones_row"]
    onesf_row = csb["onesf_row"]
    onescol_b = csb["onescol_b"]
    m_ui = csb["m_upincl_neg"]
    m_us = csb["m_upstrict_neg"]
    m_ls = csb["m_lowstrict_pos"]
    twoI = csb["twoI"]

    for v_ in (1e-6, 1.0 / 128, -0.5, -1.0, 1.0):
        p.cscalar(v_)

    xn_sb = p.sbuf("da_xn", [128, 16 * T], BF16)
    baT = p.sbuf("da_baT", [128, 16 * 64], BF16)
    qrow1 = p.sbuf("da_qrow1", [128, T], F32)
    qrow2 = p.sbuf("da_qrow2", [64, T], F32)
    cols_all = p.sbuf("da_cols", [128, 16 * 192], F32)
    rows_buf = p.sbuf("da_rows", [1, 1024], F32)
    cw_sb = p.sbuf("da_cw", [128, 64 * 4], F32)
    misc = p.sbuf("da_misc", [128, 16], F32)
    nw_rep = p.sbuf("da_nw", [128, 128], F32)
    nw_row = p.sbuf("da_nwrow", [1, 128], F32)
    wq_l = p.sbuf("da_wq", [128, 16 * 128], BF16)
    wk_l = p.sbuf("da_wk", [128, 16 * 128], BF16)
    wv_l = p.sbuf("da_wv", [128, 16 * 256], BF16)
    wz_l = p.sbuf("da_wz", [128, 16 * 256], BF16)
    qraw = p.sbuf("da_qraw", [128, T + 4], BF16)
    qc = p.sbuf("da_qc", [128, T], BF16)
    kc_ = p.sbuf("da_kc", [128, T], BF16)
    vc = [p.sbuf(f"da_vc{i}", [128, T], BF16) for i in range(2)]
    ssrow = p.sbuf("da_ssrow", [1, T], F32)
    zsil = p.sbuf("da_zsil", [128, 2 * T], BF16)
    basb = zsil[0:64, 0:4096].bitcast(F32)
    S_bf = p.sbuf("da_S", [128, 32 * 128], BF16)
    g_sb = p.sbuf("da_g", [128, 4 * T], BF16)
    gv = g_sb[0:32, :].bitcast(F32)
    wodb = p.sbuf("da_wod", [128, 4 * 512], BF16)
    stg = [p.sbuf(f"da_stg{i}", [128, 512], F32) for i in range(2)]
    KK_sb = p.sbuf("da_KK", [128, 128], F32)
    KQ_sb = p.sbuf("da_KQ", [128, 128], F32)
    KT_sb = p.sbuf("da_KT", [128, 128], BF16)
    VT_sb = [p.sbuf(f"da_VT{i}", [128, 128], BF16) for i in range(2)]
    Gt = p.sbuf("da_Gt", [128, 128], F32)
    Et = p.sbuf("da_Et", [128, 128], F32)
    Et2 = p.sbuf("da_Et2", [128, 128], F32)
    MT_sb = p.sbuf("da_MT", [128, 128], BF16)
    nTT = p.sbuf("da_nTT", [128, 128], F32)
    X_sb = p.sbuf("da_X", [128, 128], BF16)
    XT_sb = p.sbuf("da_XT", [128, 128], BF16)
    W2_sb = p.sbuf("da_W2", [128, 128], BF16)
    R_sb = p.sbuf("da_R", [128, 128], BF16)
    U_sb = p.sbuf("da_U", [128, 128], BF16)
    U2_sb = p.sbuf("da_U2", [128, 128], BF16)
    t4_sb = p.sbuf("da_t4", [128, 128], F32)
    O_sb = p.sbuf("da_O", [128, 128], F32)
    On_sb = p.sbuf("da_On", [128, 128], BF16)
    osq = p.sbuf("da_osq", [128, 128], BF16)

    pool = PsumPool(p, "da_pp", 4)
    bk0 = p.psum("da_bk0", [128, 512], F32)
    bk1 = p.psum("da_bk1", [128, 512], F32)
    bk2 = p.psum("da_bk2", [128, 512], F32)
    bk3 = p.psum("da_bk3", [128, 512], F32)

    @p.block.gpsimd
    def _(g):
        emit_prep_cast(p, g, W_qkv, wqkv_bf, "pw_qkv")
        emit_prep_cast(p, g, W_z, wz_bf, "pw_z")
        emit_prep_cast(p, g, W_out, wout_bf, "pw_o")
        e = g.dma_start(out=basb[0:32, :], in_=W_b[:, :])
        p.inc(e, "mld", 16)
        e = g.dma_start(out=basb[32:64, :], in_=W_a[:, :])
        p.inc(e, "mld", 16)
        e = g.dma_start(out=cw_sb[:, :].rearrange("p (c j) -> p c j", j=4), in_=conv_w.rearrange("(c q) j -> q c j", q=128))
        p.inc(e, "mld", 16)
        e = g.dma_start(out=misc[0:32, 0:1], in_=dt_bias[:, None])
        p.inc(e, "mld", 16)
        e = g.dma_start(out=misc[0:32, 1:2], in_=A_log[:, None])
        p.inc(e, "mld", 16)
        e = g.dma_start(out=nw_row[0:1, :], in_=norm_w[None, :])
        p.inc(e, "mld", 16)
        for i in range(4):
            e = g.dma_start(out=xacc[i * 512:(i + 1) * 512, :], in_=xacc_in[i * 512:(i + 1) * 512, :])
            p.inc(e, "cpacc", 16)
        e = g.memset(qraw[:, 0:4], 0.0)
        p.inc(e, "mld", 1)

    @p.block.sync
    def _(sy):
        emit_load_xn(p, sy, xn0, xn_sb, "xnld", tm=True)

    _chain(p, [
        ("tensor", lambda t: t.matmul(bk0[:, 0:128], onesf_row[:, :], nw_row[0:1, :], start=True, stop=True), [p.tok("mld"), p.tok("cld")]),
        ("vector", lambda v: v.tensor_copy(nw_rep[:, :], bk0[:, 0:128]), []),
    ])

    for kcc in range(16):
        _chain(p, [
            ("tensor", lambda t, kcc=kcc: t.transpose(bk0.bitcast(BF16)[:, 512:512 + 64] if False else bk0[:, 128:192], basb[0:64, kcc * 128:(kcc + 1) * 128], identf[0:64, 0:64]), [p.tok("mld"), p.tok("cld")]),
            ("vector", lambda v, kcc=kcc: v.tensor_copy(baT[:, kcc * 64:(kcc + 1) * 64], bk0[:, 128:192]), []),
        ])

    _regions = [gv[:, 0:T], gv[:, T:2 * T],
                zsil[0:32, :].bitcast(F32), S_bf[0:32, :].bitcast(F32)]

    def Ar(s0, s1=None):
        r = _regions[s0]
        return r if s1 is None else r[:, s1[0]:s1[1]]
    for tt in range(4):
        bnk, ps_, free = pool.acquire()

        def mmfn(t, tt=tt, ps_=ps_, free=free):
            p.wait(t, "tensor", free, p.tok("xnld"))
            e = None
            for kcc in range(16):
                e = t.matmul(ps_[0:64, 0:512], baT[:, kcc * 64:(kcc + 1) * 64],
                             xn_sb[:, kcc * T + tt * 512: kcc * T + tt * 512 + 512],
                             start=(kcc == 0), stop=(kcc == 15))
            return e
        tk = _chain(p, [
            ("tensor", mmfn, []),
            ("vector", lambda v, tt=tt, ps_=ps_: v.tensor_copy(Ar(0, (tt * 512, (tt + 1) * 512)), ps_[0:32, 0:512]), []),
            ("vector", lambda v, tt=tt, ps_=ps_: v.tensor_copy(Ar(1, (tt * 512, (tt + 1) * 512)), ps_[32:64, 0:512]), []),
        ])
        pool.release(bnk, tk)

    _chain(p, [
        ("scalar", lambda s: s.activation(Ar(3), Ar(0), AF.Sigmoid), []),
        ("vector", lambda v: v.tensor_scalar_mul(qrow1[64:96, :], Ar(3), -1.0), []),
        ("scalar", lambda s: s.activation(Ar(2), Ar(1), AF.Abs, bias=misc[0:32, 0:1]), []),
        ("scalar", lambda s: s.activation(Ar(2), Ar(2), AF.Exp, scale=-1.0), []),
        ("scalar", lambda s: s.activation(Ar(2), Ar(2), AF.Ln, bias=1.0), []),
        ("vector", lambda v: v.tensor_scalar(Ar(3), Ar(1), misc[0:32, 0:1], 0.0, ALU.add, ALU.max), []),
        ("vector", lambda v: v.tensor_add(Ar(1), Ar(2), Ar(3)), []),
        ("scalar", lambda s: s.activation(misc[0:32, 2:3], misc[0:32, 1:2], AF.Exp), []),
        ("vector", lambda v: v.tensor_scalar_mul(misc[0:32, 3:4], misc[0:32, 2:3], -1.0), []),
        ("vector", lambda v: v.tensor_scalar(Ar(2), Ar(1), misc[0:32, 3:4], None, ALU.mult), []),
    ])
    for c in range(16):
        sl = slice(c * 128, (c + 1) * 128)
        _chain(p, [
            ("vector", lambda v, sl=sl: v.tensor_tensor_scan(qrow1[0:32, sl], Ar(2)[:, sl], Ar(2)[:, sl], 0.0, ALU.add, ALU.bypass), []),
            ("vector", lambda v, sl=sl, c=c: v.tensor_scalar(Ar(3)[:, sl], qrow1[0:32, sl], qrow1[0:32, c * 128 + 127:c * 128 + 128], None, ALU.subtract), []),
            ("scalar", lambda s, sl=sl: s.activation(qrow2[32:64, sl], Ar(3)[:, sl], AF.Exp, scale=-1.0), []),
        ])
    _chain(p, [
        ("scalar", lambda s: s.activation(qrow2[0:32, :], qrow1[0:32, :], AF.Exp), []),
    ])
    for c in range(16):
        _chain(p, [
            ("vector", lambda v, c=c: v.memset(qrow1[32:64, c * 128:c * 128 + 1], 0.0), []),
            ("vector", lambda v, c=c: v.tensor_copy(qrow1[32:64, c * 128 + 1:(c + 1) * 128], qrow1[0:32, c * 128:(c + 1) * 128 - 1]), []),
            ("vector", lambda v, c=c: v.memset(qrow1[96:128, c * 128:c * 128 + 1], 1.0), []),
            ("vector", lambda v, c=c: v.tensor_copy(qrow1[96:128, c * 128 + 1:(c + 1) * 128], qrow2[0:32, c * 128:(c + 1) * 128 - 1]), []),
        ])
    for c in range(16):
        _chain(p, [
            ("tensor", lambda t, c=c: t.transpose(bk0[:, 128:256], qrow1[:, c * 128:(c + 1) * 128], identf[:, :]), []),
            ("vector", lambda v, c=c: v.tensor_copy(cols_all[:, c * 192: c * 192 + 128], bk0[:, 128:256]), []),
            ("tensor", lambda t, c=c: t.transpose(bk0[:, 256:320], qrow2[0:64, c * 128:(c + 1) * 128], identf[0:64, 0:64]), []),
            ("vector", lambda v, c=c: v.tensor_copy(cols_all[:, c * 192 + 128: c * 192 + 192], bk0[:, 256:320]), []),
        ])

    _chain(p, [("vector", lambda v: v.memset(S_bf[:, :], 0.0), [])])

    def col(q, vh, c):
        off = {"lA": 0, "lB": 32, "negb": 64, "elB": 96, "elA": 128, "w": 160}[q]
        return cols_all[:, c * 192 + off + vh: c * 192 + off + vh + 1]

    last_inproj_mm = None
    acc_base = p.tok("acc")
    for h in range(NK):
        hg = h % 2

        @p.block.sync
        def _(sy):
            p.wait(sy, "sync", p.tok("pw_qkv"), p.tok("pw_z"))
            if last_inproj_mm is not None:
                p.wait(sy, "sync", last_inproj_mm)
            for kcc in range(16):
                e = sy.dma_start(out=wq_l[:, kcc * 128:(kcc + 1) * 128],
                                 in_=wqkv_bf[h * 128:(h + 1) * 128, kcc * 128:(kcc + 1) * 128], transpose=True)
                p.inc(e, "wld", 16)
                e = sy.dma_start(out=wk_l[:, kcc * 128:(kcc + 1) * 128],
                                 in_=wqkv_bf[QD + h * 128: QD + (h + 1) * 128, kcc * 128:(kcc + 1) * 128], transpose=True)
                p.inc(e, "wld", 16)
                e = sy.dma_start(out=wv_l[:, kcc * 256:(kcc + 1) * 256],
                                 in_=wqkv_bf[2 * QD + h * 256: 2 * QD + (h + 1) * 256, kcc * 128:(kcc + 1) * 128], transpose=True)
                p.inc(e, "wld", 16)
                e = sy.dma_start(out=wz_l[:, kcc * 256:(kcc + 1) * 256],
                                 in_=wz_bf[h * 256:(h + 1) * 256, kcc * 128:(kcc + 1) * 128], transpose=True)
                p.inc(e, "wld", 16)

        parts = [(qc, wq_l, 0, 128, h), (kc_, wk_l, 0, 128, 16 + h),
                 (vc[0], wv_l, 0, 256, 32 + 2 * h), (vc[1], wv_l, 128, 256, 33 + 2 * h)]
        for (dst, wbuf, coff, wstride, cc) in parts:
            for tt in range(4):
                bnk, ps_, free = pool.acquire()

                def mmfn(t, tt=tt, ps_=ps_, free=free, wbuf=wbuf, coff=coff, wstride=wstride):
                    p.wait(t, "tensor", free, p.tok("wld"), p.tok("xnld"), p.tok("mld"))
                    e = None
                    for kcc in range(16):
                        e = t.matmul(ps_[:, :], wbuf[:, kcc * wstride + coff: kcc * wstride + coff + 128],
                                     xn_sb[:, kcc * T + tt * 512: kcc * T + tt * 512 + 512],
                                     start=(kcc == 0), stop=(kcc == 15))
                    return e
                tk = _chain(p, [
                    ("tensor", mmfn, []),
                    ("vector", lambda v, tt=tt, ps_=ps_: v.tensor_copy(qraw[:, 4 + tt * 512: 4 + (tt + 1) * 512], ps_[:, :]), []),
                ])
                pool.release(bnk, tk)
            steps = [("vector", lambda v, cc=cc: v.tensor_scalar(dst[:, :], qraw[:, 1:1 + T], cw_sb[:, cc * 4 + 0: cc * 4 + 1], None, ALU.mult), [])]
            for j in range(1, 4):
                steps.append(("vector", lambda v, j=j, cc=cc, dst=dst: v.scalar_tensor_tensor(
                    dst[:, :], qraw[:, 1 + j:1 + j + T], cw_sb[:, cc * 4 + j: cc * 4 + j + 1], dst[:, :], ALU.mult, ALU.add), []))
            _chain(p, steps)

        for pi, buf in enumerate((qc, kc_)):
            _chain(p, [("scalar", lambda s, buf=buf: s.activation(qraw[:, 4:4 + T], buf[:, :], AF.Square), [])])
            for tt in range(4):
                _chain(p, [
                    ("tensor", lambda t, tt=tt: t.matmul(bk0[0:1, 0:512], onescol_b[:, :], qraw[:, 4 + tt * 512:4 + (tt + 1) * 512], start=True, stop=True), []),
                    ("vector", lambda v, tt=tt, pi=pi: v.tensor_copy(ssrow[0:1, tt * 512:(tt + 1) * 512], bk0[0:1, 0:512]), []),
                ])
            _chain(p, [
                ("scalar", lambda s, pi=pi: s.activation(ssrow[0:1, :], ssrow[0:1, :], AF.Sqrt), []),
                ("vector", lambda v, pi=pi: v.tensor_scalar_max(ssrow[0:1, :], ssrow[0:1, :], 1e-12), []),
                ("vector", lambda v, pi=pi: v.reciprocal(ssrow[0:1, :], ssrow[0:1, :]), []),
            ])
            for tt in range(4):
                _chain(p, [
                    ("tensor", lambda t, tt=tt, pi=pi: t.matmul(bk1[:, 0:512], onesf_row[:, :], ssrow[0:1, tt * 512:(tt + 1) * 512], start=True, stop=True), []),
                    ("vector", lambda v, tt=tt, buf=buf: v.tensor_tensor(buf[:, tt * 512:(tt + 1) * 512], buf[:, tt * 512:(tt + 1) * 512], bk1[:, 0:512], ALU.mult), []),
                ])

        for zo in range(2):
            for tt in range(4):
                bnk, ps_, free = pool.acquire()

                def mmfn(t, tt=tt, ps_=ps_, free=free, zo=zo):
                    p.wait(t, "tensor", free)
                    e = None
                    for kcc in range(16):
                        e = t.matmul(ps_[:, :], wz_l[:, kcc * 256 + zo * 128: kcc * 256 + zo * 128 + 128],
                                     xn_sb[:, kcc * T + tt * 512: kcc * T + tt * 512 + 512],
                                     start=(kcc == 0), stop=(kcc == 15))
                    return e
                tk = _chain(p, [
                    ("tensor", mmfn, []),
                    ("scalar", lambda s, tt=tt, ps_=ps_, zo=zo: s.activation(
                        zsil[:, zo * T + tt * 512: zo * T + (tt + 1) * 512], ps_[:, :], AF.Silu), []),
                ])
                pool.release(bnk, tk)
        last_inproj_mm = p.tok("cht")

        for c in range(16):
            csl = slice(c * 128, (c + 1) * 128)
            _chain(p, [
                ("tensor", lambda t: t.matmul(bk0[:, 0:128], kc_[:, csl], kc_[:, csl], start=True, stop=True), []),
                ("vector", lambda v: v.tensor_copy(KK_sb[:, :], bk0[:, 0:128]), []),
                ("tensor", lambda t: t.matmul(bk0[:, 128:256], kc_[:, csl], qc[:, csl], start=True, stop=True), []),
                ("vector", lambda v: v.tensor_copy(KQ_sb[:, :], bk0[:, 128:256]), []),
                ("tensor", lambda t: t.transpose(bk0.bitcast(BF16)[:, 512:640], kc_[:, csl], identb[:, :]), []),
                ("vector", lambda v: v.tensor_copy(KT_sb[:, :], bk0.bitcast(BF16)[:, 512:640]), []),
                ("tensor", lambda t: t.transpose(bk0.bitcast(BF16)[:, 640:768], vc[0][:, csl], identb[:, :]), []),
                ("vector", lambda v: v.tensor_copy(VT_sb[0][:, :], bk0.bitcast(BF16)[:, 640:768]), []),
                ("tensor", lambda t: t.transpose(bk0.bitcast(BF16)[:, 768:896], vc[1][:, csl], identb[:, :]), []),
                ("vector", lambda v: v.tensor_copy(VT_sb[1][:, :], bk0.bitcast(BF16)[:, 768:896]), []),
            ])
            for pp_ in range(2):
                vh = 2 * h + pp_

                @p.block.gpsimd
                def _(g, vh=vh, pp_=pp_, c=c):
                    p.wait(g, "gpsimd", p.tok("chv"), p.tok("chs"), p.tok("cht"))
                    e = g.dma_start(out=rows_buf[0:1, pp_ * 512:(pp_ + 1) * 512],
                                    in_=qrow1[vh:vh + 97:32, c * 128:(c + 1) * 128])
                    p.inc(e, "rowld", 16)

            for pp_ in range(2):
                vh = 2 * h + pp_
                ro = pp_ * 512
                lA_c, lB_c, nb_c, elB_c = col("lA", vh, c), col("lB", vh, c), col("negb", vh, c), col("elB", vh, c)
                elA_c, w_c = col("elA", vh, c), col("w", vh, c)
                Ssl = S_bf[:, vh * 128:(vh + 1) * 128]

                def step(engname, fn, extra=()):
                    return (engname, fn, list(extra))
                _chain(p, [
                    step("tensor", lambda t: t.matmul(bk1[:, 0:128], onesf_row[:, :], rows_buf[0:1, ro + 0:ro + 128], start=True, stop=True), [p.tok("rowld")]),
                    step("tensor", lambda t: t.matmul(bk1[:, 128:256], onesf_row[:, :], rows_buf[0:1, ro + 128:ro + 256], start=True, stop=True)),
                    step("tensor", lambda t: t.matmul(bk1[:, 256:384], onesf_row[:, :], rows_buf[0:1, ro + 256:ro + 384], start=True, stop=True)),
                    step("vector", lambda v: v.scalar_tensor_tensor(Gt[:, :], bk1[:, 0:128], lA_c, m_ui[:, :], ALU.subtract, ALU.add)),
                    step("scalar", lambda s: s.activation(Et[:, :], Gt[:, :], AF.Exp)),
                    step("vector", lambda v: v.tensor_tensor(MT_sb[:, :], Et[:, :], KQ_sb[:, :], ALU.mult)),
                    step("vector", lambda v: v.scalar_tensor_tensor(Gt[:, :], bk1[:, 128:256], lA_c, m_us[:, :], ALU.subtract, ALU.add)),
                    step("scalar", lambda s: s.activation(Et[:, :], Gt[:, :], AF.Exp)),
                    step("vector", lambda v: v.tensor_tensor(Et2[:, :], Et[:, :], KK_sb[:, :], ALU.mult)),
                    step("vector", lambda v: v.tensor_tensor(nTT[:, :], Et2[:, :], bk1[:, 256:384], ALU.mult)),
                    step("vector", lambda v: v.tensor_add(XT_sb[:, :], identf[:, :], nTT[:, :])),
                    step("vector", lambda v: v.tensor_sub(R_sb[:, :], identf[:, :], nTT[:, :])),
                    step("vector", lambda v: v.scalar_tensor_tensor(Gt[:, :], bk1[:, 0:128], lB_c, m_ls[:, :], ALU.subtract, ALU.add)),
                    step("scalar", lambda s: s.activation(Et[:, :], Gt[:, :], AF.Exp, scale=-1.0)),
                    step("vector", lambda v: v.tensor_mul(Et[:, :], Et[:, :], KK_sb[:, :])),
                    step("vector", lambda v: v.scalar_tensor_tensor(X_sb[:, :], Et[:, :], nb_c, identf[:, :], ALU.mult, ALU.add)),
                ])
                for it in range(6):
                    _chain(p, [
                        step("tensor", lambda t: t.matmul(bk2[:, 0:128], R_sb[:, :], X_sb[:, :], start=True, stop=True)),
                        step("vector", lambda v: v.scalar_tensor_tensor(W2_sb[:, :], bk2[:, 0:128], -1.0, twoI[:, :], ALU.mult, ALU.add)),
                        step("tensor", lambda t: t.matmul(bk2[:, 128:256], XT_sb[:, :], W2_sb[:, :], start=True, stop=True)),
                        step("tensor", lambda t: t.matmul(bk2[:, 256:384], W2_sb[:, :], XT_sb[:, :], start=True, stop=True)),
                        step("vector", lambda v: v.tensor_copy(X_sb[:, :], bk2[:, 128:256])),
                        step("scalar", lambda s: s.copy(XT_sb[:, :], bk2[:, 256:384])),
                    ])
                _chain(p, [
                    step("tensor", lambda t: t.matmul(bk3[:, 0:128], kc_[:, csl], Ssl, start=True, stop=True)),
                    step("vector", lambda v: v.scalar_tensor_tensor(Gt[:, :], bk3[:, 0:128], elB_c, VT_sb[pp_][:, :], ALU.mult, ALU.subtract)),
                    step("vector", lambda v: v.tensor_scalar(R_sb[:, :], Gt[:, :], nb_c, None, ALU.mult)),
                    step("tensor", lambda t: t.matmul(bk2[:, 384:512], XT_sb[:, :], R_sb[:, :], start=True, stop=True)),
                    step("vector", lambda v: v.tensor_copy(U_sb[:, :], bk2[:, 384:512])),
                    step("vector", lambda v: v.tensor_scalar(U2_sb[:, :], bk2[:, 384:512], w_c, None, ALU.mult)),
                    step("tensor", lambda t: t.matmul(bk3[:, 128:256], qc[:, csl], Ssl, start=True, stop=True)),
                    step("vector", lambda v: v.tensor_scalar(t4_sb[:, :], bk3[:, 128:256], elA_c, None, ALU.mult)),
                    step("tensor", lambda t: t.matmul(bk3[:, 256:384], KT_sb[:, :], U2_sb[:, :], start=True, stop=True)),
                    step("vector", lambda v: v.tensor_scalar(misc[:, 4:5], w_c, elA_c, None, ALU.mult)),
                    step("vector", lambda v: v.scalar_tensor_tensor(Ssl, Ssl, misc[:, 4:5], bk3[:, 256:384], ALU.mult, ALU.add)),
                    step("tensor", lambda t: t.matmul(bk3[:, 384:512], MT_sb[:, :], U_sb[:, :], start=True, stop=True)),
                    step("vector", lambda v: v.tensor_add(O_sb[:, :], bk3[:, 384:512], t4_sb[:, :])),
                    step("scalar", lambda s: s.activation(osq[:, :], O_sb[:, :], AF.Square, accum_out=misc[:, 5:6])),
                    step("scalar", lambda s: s.activation(misc[:, 6:7], misc[:, 5:6], AF.Ln, scale=1.0 / 128, bias=1e-6)),
                    step("scalar", lambda s: s.activation(misc[:, 7:8], misc[:, 6:7], AF.Exp, scale=-0.5)),
                    step("vector", lambda v: v.scalar_tensor_tensor(On_sb[:, :], O_sb[:, :], misc[:, 7:8], nw_rep[:, :], ALU.mult, ALU.mult)),
                    step("tensor", lambda t: t.transpose(bk3.bitcast(BF16)[:, 0:128], On_sb[:, :], identb[:, :])),
                    step("vector", lambda v: v.tensor_tensor(g_sb[:, (hg * 2 + pp_) * T + c * 128: (hg * 2 + pp_) * T + (c + 1) * 128],
                                                             bk3.bitcast(BF16)[:, 0:128], zsil[:, pp_ * T + c * 128: pp_ * T + (c + 1) * 128], ALU.mult)),
                ])

        if hg == 1:
            h0 = h - 1
            g_ready = p.tok("chv")
            prev_og_mm = None
            for og in range(4):
                @p.block.sync
                def _(sy, og=og, h0=h0):
                    p.wait(sy, "sync", p.tok("pw_o"))
                    if prev_og_mm is not None:
                        p.wait(sy, "sync", prev_og_mm)
                    for j in range(4):
                        vhg = 2 * h0 + j
                        e = sy.dma_start(out=wodb[:, j * 512:(j + 1) * 512],
                                         in_=wout_bf[og * 512:(og + 1) * 512, vhg * 128:(vhg + 1) * 128], transpose=True)
                        p.inc(e, "wod", 16)

                for ol in range(4):
                    for tt in range(4):
                        bnk, ps_, free = pool.acquire()

                        def mmfn(t, ps_=ps_, free=free, ol=ol, tt=tt, g_ready=g_ready):
                            p.wait(t, "tensor", p.tok("wod"), free, g_ready)
                            e = None
                            for j in range(4):
                                e = t.matmul(ps_[:, :], wodb[:, j * 512 + ol * 128: j * 512 + ol * 128 + 128],
                                             g_sb[:, j * T + tt * 512: j * T + (tt + 1) * 512],
                                             start=(j == 0), stop=(j == 3))
                            return e
                        sb_i = (ol * 4 + tt) % 2

                        def evfn(s, sb_i=sb_i, ps_=ps_):
                            p.wait(s, "scalar", stg_free_da[sb_i])
                            return s.copy(stg[sb_i][:], ps_[:, :])
                        tke = _chain(p, [("tensor", mmfn, []), ("scalar", evfn, [])])
                        pool.release(bnk, tke)

                        @p.block.gpsimd
                        def _(g, og=og, ol=ol, tt=tt, sb_i=sb_i, tke=tke):
                            p.wait(g, "gpsimd", tke, p.tok("cpacc"), acc_base)
                            e = g.dma_start(out=xacc[og * 512 + ol * 128: og * 512 + ol * 128 + 128, tt * 512:(tt + 1) * 512],
                                            in_=stg[sb_i][:], accum_op=ALU.add)
                            stg_free_da[sb_i] = p.inc(e, "acc", 16)
                prev_og_mm = p.tok("cht")

    @p.block.sync
    def _(sy):
        p.final_wait(sy, "sync")


stg_free_da = [None, None]


def build_l0_attn():
    global stg_free_da
    stg_free_da = [None, None]
    p = Prog("l0attn")
    with p.es:
        csb, cemit = load_consts(p, ["identf", "identb", "ones_row", "onesf_row", "onescol_b",
                                     "m_upincl_neg", "m_upstrict_neg", "m_lowstrict_pos", "twoI"])
        with p.nc.Block() as blk:
            p.block = blk

            @blk.gpsimd
            def _(g):
                cemit(g)
            emit_l0_attn(p, csb)
    return p


def emit_ln_fm(p, csb):
    xacc_in = p.dram("xacc_in", [D, T], F32, kind="ExternalInput")
    xn1 = p.dram("xn1", [D, T], BF16, kind="ExternalOutput")
    onesf_row = csb["onesf_row"]
    onescol_b = csb["onescol_b"]
    p.cscalar(1e-5)
    p.cscalar(-0.5)

    xbf = p.sbuf("lf_xbf", [128, 16 * T], BF16)
    xf32 = [p.sbuf(f"lf_x{i}", [128, T], F32) for i in range(2)]
    sq = p.sbuf("lf_sq", [128, 512], BF16)
    rows = p.sbuf("lf_rows", [1, 2 * T], F32)
    rscr = p.sbuf("lf_rscr", [1, 2 * 512], F32)
    tscr = p.sbuf("lf_t", [128, 512], F32)
    xnb = [p.sbuf(f"lf_xn{i}", [128, 512], BF16) for i in range(2)]
    pst = [p.psum(f"lf_pst{i}", [1, 512], F32) for i in range(2)]
    prep = [p.psum(f"lf_prep{i}", [128, 512], F32) for i in range(2)]

    for dc in range(16):
        b = dc % 2

        @p.block.gpsimd
        def _(g, dc=dc, b=b):
            p.wait_snap(g, "gpsimd", dc - 2, ["chv"])
            e = g.dma_start(out=xf32[b][:], in_=xacc_in[dc * 128:(dc + 1) * 128, :])
            p.inc(e, f"lfld{b}", 16)
        _chain(p, [
            ("vector", lambda v, dc=dc, b=b: v.tensor_copy(xbf[:, dc * T:(dc + 1) * T], xf32[b][:]), [p.tok(f"lfld{b}")]),
        ])
        p.snapshot(dc)

    for tt in range(4):
        for dc in range(16):
            _chain(p, [
                ("scalar", lambda s, dc=dc, tt=tt: s.activation(sq[:, :], xbf[:, dc * T + tt * 512: dc * T + tt * 512 + 512], AF.Square), []),
                ("tensor", lambda t, dc=dc, tt=tt: t.matmul(pst[0][0:1, :], onescol_b[:, :],
                                                            xbf[:, dc * T + tt * 512: dc * T + tt * 512 + 512],
                                                            start=(dc == 0), stop=(dc == 15)), []),
                ("tensor", lambda t, dc=dc: t.matmul(pst[1][0:1, :], onescol_b[:, :], sq[:, :],
                                                     start=(dc == 0), stop=(dc == 15)), []),
            ])
        _chain(p, [
            ("vector", lambda v, tt=tt: v.tensor_scalar_mul(rows[0:1, tt * 512:(tt + 1) * 512], pst[0][0:1, :], -1.0 / D), []),
            ("vector", lambda v, tt=tt: v.tensor_scalar_mul(rscr[0:1, 0:512], pst[1][0:1, :], 1.0 / D), []),
            ("vector", lambda v, tt=tt: v.tensor_mul(rscr[0:1, 512:1024], rows[0:1, tt * 512:(tt + 1) * 512], rows[0:1, tt * 512:(tt + 1) * 512]), []),
            ("vector", lambda v: v.tensor_sub(rscr[0:1, 0:512], rscr[0:1, 0:512], rscr[0:1, 512:1024]), []),
            ("scalar", lambda s: s.activation(rscr[0:1, 0:512], rscr[0:1, 0:512], AF.Ln, bias=1e-5), []),
            ("scalar", lambda s, tt=tt: s.activation(rows[0:1, T + tt * 512: T + (tt + 1) * 512], rscr[0:1, 0:512], AF.Exp, scale=-0.5), []),
        ])
        _chain(p, [
            ("tensor", lambda t, tt=tt: t.matmul(prep[0][:, :], onesf_row[:, :], rows[0:1, tt * 512:(tt + 1) * 512], start=True, stop=True), []),
            ("tensor", lambda t, tt=tt: t.matmul(prep[1][:, :], onesf_row[:, :], rows[0:1, T + tt * 512: T + (tt + 1) * 512], start=True, stop=True), []),
        ])
        for dc in range(16):
            b = dc % 2

            def xfn1(v, dc=dc, tt=tt, b=b):
                p.wait(v, "vector", p.tok(f"lfst{b}"))
                return v.tensor_tensor(tscr[:, :], xbf[:, dc * T + tt * 512: dc * T + tt * 512 + 512], prep[0][:, :], ALU.add)
            tk = _chain(p, [
                ("vector", xfn1, []),
                ("vector", lambda v, b=b: v.tensor_tensor(xnb[b][:, :], tscr[:, :], prep[1][:, :], ALU.mult), []),
            ])

            @p.block.gpsimd
            def _(g, dc=dc, tt=tt, b=b, tk=tk):
                p.wait(g, "gpsimd", tk)
                e = g.dma_start(out=xn1[dc * 128:(dc + 1) * 128, tt * 512:(tt + 1) * 512], in_=xnb[b][:, :])
                p.inc(e, f"lfst{b}", 16)

    @p.block.sync
    def _(sy):
        p.final_wait(sy, "sync")


def build_ln_fm():
    p = Prog("lnfm")
    with p.es:
        csb, cemit = load_consts(p, ["onesf_row", "onescol_b"])
        with p.nc.Block() as blk:
            p.block = blk

            @blk.gpsimd
            def _(g):
                cemit(g)
            emit_ln_fm(p, csb)
    return p


def emit_l1_attn(p, csb):
    DI, DS, NH, HD = 4096, 128, 64, 64
    CVD = DI + 2 * DS
    DIP = 2 * DI + 2 * DS + NH
    xn1 = p.dram("xn1_in", [D, T], BF16, kind="ExternalInput")
    xacc_in = p.dram("xacc_in", [D, T], F32, kind="ExternalInput")
    xacc = p.dram("xacc", [D, T], F32, kind="ExternalOutput")
    in_proj_w = p.dram("in_proj_w", [DIP, D], F32, kind="ExternalInput")
    conv_w = p.dram("conv_w", [CVD, 4], F32, kind="ExternalInput")
    conv_b = p.dram("conv_b", [CVD], F32, kind="ExternalInput")
    dt_bias = p.dram("dt_bias", [NH], F32, kind="ExternalInput")
    A_log = p.dram("A_log", [NH], F32, kind="ExternalInput")
    Dvec = p.dram("Dvec", [NH], F32, kind="ExternalInput")
    norm_weight = p.dram("norm_weight", [DI], F32, kind="ExternalInput")
    out_proj_w = p.dram("out_proj_w", [D, DI], F32, kind="ExternalInput")

    wip_bf = p.dram("wip_bf", [8704, D], BF16)
    wop_bf = p.dram("wop_bf", [D, DI], BF16)
    z_sp = p.dram("z_sp", [DI, T], BF16)
    xc_sp = p.dram("xc_sp", [CVD, T], BF16)
    yn_sp = p.dram("yn_sp", [DI, T], BF16)

    identf = csb["identf"]
    identb = csb["identb"]
    onesf_row = csb["onesf_row"]
    m_ui = csb["m_upincl_neg"]
    for v_ in (1e-5, 1.0 / DI, -0.5, -1.0, 1.0, 0.0):
        p.cscalar(v_)

    xn_sb = p.sbuf("ma_xn", [128, 16 * T], BF16)
    wob = xn_sb[:, 0:16384]
    ynt = xn_sb[:, 16384:32768]
    wipb = p.sbuf("ma_wipb", [128, 16 * 512], BF16)
    zbuf = p.sbuf("ma_zbuf", [128, D], BF16)
    stg = [p.sbuf(f"ma_stg{i}", [128, 512], F32) for i in range(2)]
    y_TM = p.sbuf("ma_y", [128, 4096], F32)
    dt_raw = y_TM[0:64, 0:T]
    dscr = y_TM[0:64, T:2 * T]
    mrow1 = p.sbuf("ma_mrow1", [128, T], F32)
    mrow2 = p.sbuf("ma_mrow2", [128, T], F32)
    cols_m = p.sbuf("ma_cols", [128, 16 * 256], F32)
    rows_all = p.sbuf("ma_rows", [1, 16 * 128], F32)
    cw_sb = p.sbuf("ma_cw", [128, 34 * 4], F32)
    cb_sb = p.sbuf("ma_cb", [128, 34], F32)
    cbT_sb = p.sbuf("ma_cbT", [34, 128], F32)
    misc = p.sbuf("ma_misc", [128, 16], F32)
    nw_rep = p.sbuf("ma_nw", [128, 4096], BF16)
    nwrow = p.sbuf("ma_nwrow", [1, 512], F32)
    Drep = p.sbuf("ma_Drep", [128, NH], F32)
    Drow = p.sbuf("ma_Drow", [1, NH], F32)
    xch = p.sbuf("ma_xch", [128, 32 * 128], BF16)
    zch = p.sbuf("ma_zch", [128, 32 * 128], BF16)
    xcraw = xch[:, 0:T + 4]
    xcv = zch[:, 0:T]
    bc_sb = p.sbuf("ma_bc", [128, 256], BF16)
    BC_sb = p.sbuf("ma_BC", [128, 128], F32)
    BTM_sb = p.sbuf("ma_BTM", [128, 128], BF16)
    Btil = p.sbuf("ma_Btil", [128, 128], BF16)
    MT_sb = p.sbuf("ma_MT", [128, 128], BF16)
    Gm = p.sbuf("ma_Gm", [128, 128], F32)
    Em = p.sbuf("ma_Em", [128, 128], F32)
    Xdt_sb = p.sbuf("ma_Xdt", [128, 64], BF16)
    Ydsb = p.sbuf("ma_Yd", [128, 64], F32)
    t5 = p.sbuf("ma_t5", [128, 64], F32)
    zT = p.sbuf("ma_zT", [128, 4096], BF16)
    yn_TM = p.sbuf("ma_ynTM", [128, 4096], BF16)
    ynF = zT
    state = p.sbuf("ma_state", [128, NH * HD], BF16)

    pool = PsumPool(p, "ma_pp", 4)
    mb0 = p.psum("ma_mb0", [128, 512], F32)
    mb1 = p.psum("ma_mb1", [128, 512], F32)
    mb2 = p.psum("ma_mb2", [128, 512], F32)
    mb3 = p.psum("ma_mb3", [128, 512], F32)

    @p.block.gpsimd
    def _(g):
        emit_prep_cast(p, g, in_proj_w, wip_bf[0:DIP, :], "pw_ip")
        emit_prep_cast(p, g, out_proj_w, wop_bf, "pw_op")
        e = g.memset(zbuf[:, :], 0.0)
        p.inc(e, "mld", 1)
        g.wait_ge(p.sem("mld"), 1)
        e = g.dma_start(out=wip_bf[8512:8640, :], in_=zbuf[:, :])
        p.inc(e, "pw_ip", 16)
        e = g.dma_start(out=wip_bf[8640:8704, :], in_=zbuf[0:64, :])
        p.inc(e, "pw_ip", 16)
        e = g.dma_start(out=cw_sb[:, :].rearrange("p (c j) -> p c j", j=4),
                        in_=conv_w[0:4352, :].rearrange("(c q) j -> q c j", q=128))
        p.inc(e, "mld", 16)
        e = g.dma_start(out=cbT_sb[0:34, :], in_=conv_b[:].rearrange("(c q) -> c q", q=128))
        p.inc(e, "mld", 16)
        e = g.dma_start(out=misc[0:NH, 0:1], in_=dt_bias[:, None])
        p.inc(e, "mld", 16)
        e = g.dma_start(out=misc[0:NH, 1:2], in_=A_log[:, None])
        p.inc(e, "mld", 16)
        e = g.dma_start(out=Drow[0:1, :], in_=Dvec[None, :])
        p.inc(e, "mld", 16)
        for i in range(4):
            e = g.dma_start(out=xacc[i * 512:(i + 1) * 512, :], in_=xacc_in[i * 512:(i + 1) * 512, :])
            p.inc(e, "cpacc", 16)

    @p.block.sync
    def _(sy):
        emit_load_xn(p, sy, xn1, xn_sb, "xnld", tm=False)

    _chain(p, [
        ("tensor", lambda t: t.transpose(mb0[:, 0:34], cbT_sb[0:34, :], identf[0:34, 0:34]), [p.tok("mld"), p.tok("cld")]),
        ("vector", lambda v: v.tensor_copy(cb_sb[:, :], mb0[:, 0:34]), []),
    ])

    for a in range(8):
        @p.block.gpsimd
        def _(g, a=a):
            p.wait(g, "gpsimd", p.tok("cht"))
            e = g.dma_start(out=nwrow[0:1, :], in_=norm_weight[a * 512:(a + 1) * 512][None, :])
            p.inc(e, "mld2", 16)
        _chain(p, [
            ("tensor", lambda t, a=a: t.matmul(mb0[:, :], onesf_row[:, :], nwrow[0:1, :], start=True, stop=True), [p.tok("mld2"), p.tok("cld")]),
            ("vector", lambda v, a=a: v.tensor_copy(nw_rep[:, a * 512:(a + 1) * 512], mb0[:, :]), []),
        ])
    _chain(p, [
        ("tensor", lambda t: t.matmul(mb0[:, 0:NH], onesf_row[:, :], Drow[0:1, :], start=True, stop=True), [p.tok("mld")]),
        ("vector", lambda v: v.tensor_copy(Drep[:, :], mb0[:, 0:NH]), []),
    ])

    last_og_mm = None
    for og in range(17):
        @p.block.sync
        def _(sy, og=og):
            p.wait(sy, "sync", p.tok("pw_ip"))
            if last_og_mm is not None:
                p.wait(sy, "sync", last_og_mm)
            for kcc in range(16):
                e = sy.dma_start(out=wipb[:, kcc * 512:(kcc + 1) * 512],
                                 in_=wip_bf[og * 512:(og + 1) * 512, kcc * 128:(kcc + 1) * 128], transpose=True)
                p.inc(e, "wipld", 16)

        for ol in range(4):
            ocg = og * 4 + ol
            if ocg >= 67:
                continue
            is_dt = (ocg == 66)
            is_z = (ocg < 32)
            cc = ocg - 32 if (not is_z and not is_dt) else None
            for tt in range(4):
                bnk, ps_, free = pool.acquire()

                def mmfn(t, ps_=ps_, free=free, og=og, ol=ol, tt=tt):
                    p.wait(t, "tensor", free, p.tok("wipld"), p.tok("xnld"), p.tok("mld"))
                    e = None
                    for kcc in range(16):
                        e = t.matmul(ps_[:, :], wipb[:, kcc * 512 + ol * 128: kcc * 512 + ol * 128 + 128],
                                     xn_sb[:, kcc * T + tt * 512: kcc * T + tt * 512 + 512],
                                     start=(kcc == 0), stop=(kcc == 15))
                    return e
                if is_z:
                    def evfn(s, ps_=ps_, ocg=ocg, tt=tt):
                        p.wait(s, "scalar", stg_free_ma[tt % 2])
                        return s.activation(stg[tt % 2][:, :].bitcast(BF16)[:, 0:512], ps_[:, :], AF.Silu)
                    tk = _chain(p, [("tensor", mmfn, []), ("scalar", evfn, [])])
                    pool.release(bnk, tk)

                    @p.block.gpsimd
                    def _(g, ocg=ocg, tt=tt, tk=tk):
                        p.wait(g, "gpsimd", tk)
                        e = g.dma_start(out=z_sp[ocg * 128:(ocg + 1) * 128, tt * 512:(tt + 1) * 512],
                                        in_=stg[tt % 2][:, :].bitcast(BF16)[:, 0:512])
                        stg_free_ma[tt % 2] = p.inc(e, "zspst", 16)
                elif is_dt:
                    tk = _chain(p, [
                        ("tensor", mmfn, []),
                        ("vector", lambda v, ps_=ps_, tt=tt: v.tensor_copy(dt_raw[:, tt * 512:(tt + 1) * 512], ps_[0:64, :]), []),
                    ])
                    pool.release(bnk, tk)
                else:
                    tk = _chain(p, [
                        ("tensor", mmfn, []),
                        ("vector", lambda v, ps_=ps_, tt=tt: v.tensor_copy(xcraw[:, 4 + tt * 512: 4 + (tt + 1) * 512], ps_[:, :]), []),
                    ])
                    pool.release(bnk, tk)
            if cc is not None:
                steps = [("gpsimd", lambda g: g.memset(xcraw[:, 0:4], 0.0), [])]
                steps.append(("vector", lambda v, cc=cc: v.tensor_scalar(xcv[:, :], xcraw[:, 1:1 + T], cw_sb[:, cc * 4: cc * 4 + 1], None, ALU.mult), []))
                for j in range(1, 4):
                    steps.append(("vector", lambda v, j=j, cc=cc: v.scalar_tensor_tensor(
                        xcv[:, :], xcraw[:, 1 + j:1 + j + T], cw_sb[:, cc * 4 + j: cc * 4 + j + 1], xcv[:, :], ALU.mult, ALU.add), []))
                steps.append(("scalar", lambda s, cc=cc: s.activation(xcv[:, :], xcv[:, :], AF.Silu, bias=cb_sb[:, cc:cc + 1]), []))
                tk = _chain(p, steps)

                @p.block.gpsimd
                def _(g, cc=cc, tk=tk):
                    p.wait(g, "gpsimd", tk)
                    e = g.dma_start(out=xc_sp[cc * 128:(cc + 1) * 128, :], in_=xcv[:, :])
                    p.inc(e, "xcspst", 16)
        last_og_mm = p.tok("cht")

    _chain(p, [
        ("scalar", lambda s: s.activation(dscr[:, :], dt_raw[:, :], AF.Abs, bias=misc[0:NH, 0:1]), [p.tok("xcspst")]),
        ("scalar", lambda s: s.activation(dscr[:, :], dscr[:, :], AF.Exp, scale=-1.0), []),
        ("scalar", lambda s: s.activation(dscr[:, :], dscr[:, :], AF.Ln, bias=1.0), []),
        ("vector", lambda v: v.tensor_scalar(dt_raw[:, :], dt_raw[:, :], misc[0:NH, 0:1], 0.0, ALU.add, ALU.max), []),
        ("vector", lambda v: v.tensor_add(dt_raw[:, :], dt_raw[:, :], dscr[:, :]), []),
        ("vector", lambda v: v.tensor_copy(mrow1[0:64, :], dt_raw[:, :]), []),
        ("scalar", lambda s: s.activation(misc[0:NH, 2:3], misc[0:NH, 1:2], AF.Exp), []),
        ("vector", lambda v: v.tensor_scalar_mul(misc[0:NH, 3:4], misc[0:NH, 2:3], -1.0), []),
        ("vector", lambda v: v.tensor_scalar(dscr[:, :], dt_raw[:, :], misc[0:NH, 3:4], None, ALU.mult), []),
    ])
    for c in range(16):
        sl = slice(c * 128, (c + 1) * 128)
        _chain(p, [
            ("vector", lambda v, sl=sl: v.tensor_tensor_scan(mrow1[64:128, sl], dscr[:, sl], dscr[:, sl], 0.0, ALU.add, ALU.bypass), []),
            ("vector", lambda v, sl=sl, c=c: v.tensor_scalar(dt_raw[:, sl], mrow1[64:128, sl], mrow1[64:128, c * 128 + 127:c * 128 + 128], None, ALU.subtract), []),
            ("scalar", lambda s, sl=sl: s.activation(mrow2[0:64, sl], dt_raw[:, sl], AF.Exp, scale=-1.0), []),
        ])
    _chain(p, [
        ("scalar", lambda s: s.activation(mrow2[64:128, :], mrow1[64:128, :], AF.Exp), []),
    ])
    for c in range(16):
        _chain(p, [
            ("tensor", lambda t, c=c: t.transpose(mb0[:, 0:128], mrow1[:, c * 128:(c + 1) * 128], identf[:, :]), []),
            ("vector", lambda v, c=c: v.tensor_copy(cols_m[:, c * 256: c * 256 + 128], mb0[:, 0:128]), []),
            ("tensor", lambda t, c=c: t.transpose(mb0[:, 128:256], mrow2[:, c * 128:(c + 1) * 128], identf[:, :]), []),
            ("vector", lambda v, c=c: v.tensor_copy(cols_m[:, c * 256 + 128: c * 256 + 256], mb0[:, 128:256]), []),
        ])
    _chain(p, [("vector", lambda v: v.memset(state[:, :], 0.0), [])])

    def mcol(q, h, c):
        off = {"dt": 0, "lA": 64, "ds": 128, "elA": 192}[q]
        return cols_m[:, c * 256 + off + h: c * 256 + off + h + 1]

    for c in range(16):
        @p.block.sync
        def _(sy, c=c):
            p.wait(sy, "sync", p.tok("xcspst"), p.tok("zspst"))
            p.wait(sy, "sync", p.tok("cht"), p.tok("chv"), p.tok("chs"))
            e = sy.dma_start(out=xch[:, :].rearrange("p (a t) -> p a t", t=128),
                             in_=xc_sp[0:DI, c * 128:(c + 1) * 128].rearrange("(a p) t -> p a t", p=128))
            p.inc(e, "chld", 16)
            e = sy.dma_start(out=zch[:, :].rearrange("p (a t) -> p a t", t=128),
                             in_=z_sp[0:DI, c * 128:(c + 1) * 128].rearrange("(a p) t -> p a t", p=128))
            p.inc(e, "chld", 16)
            e = sy.dma_start(out=bc_sb[:, :].rearrange("p (a t) -> p a t", t=128),
                             in_=xc_sp[DI:CVD, c * 128:(c + 1) * 128].rearrange("(a p) t -> p a t", p=128))
            p.inc(e, "chld", 16)


        _chain(p, [
            ("tensor", lambda t: t.matmul(mb0[:, 0:128], bc_sb[:, 0:128], bc_sb[:, 128:256], start=True, stop=True), [p.tok("chld")]),
            ("vector", lambda v: v.tensor_copy(BC_sb[:, :], mb0[:, 0:128]), []),
            ("tensor", lambda t: t.transpose(mb0.bitcast(BF16)[:, 512:640], bc_sb[:, 0:128], identb[:, :]), []),
            ("vector", lambda v: v.tensor_copy(BTM_sb[:, :], mb0.bitcast(BF16)[:, 512:640]), []),
        ])
        for h in range(NH):
            a = h // 2
            pb = (h % 2) * 64
            if h % 16 == 0:
                @p.block.gpsimd
                def _(g, c=c, h=h):
                    p.wait(g, "gpsimd", p.tok("cht"), p.tok("chv"))
                    e = g.dma_start(out=rows_all[0:1, :], in_=mrow1[64 + h:64 + h + 16, c * 128:(c + 1) * 128])
                    p.inc(e, "rowld", 16)
            dt_c, lA_c = mcol("dt", h, c), mcol("lA", h, c)
            ds_c, elA_c = mcol("ds", h, c), mcol("elA", h, c)
            _chain(p, [
                ("tensor", lambda t, a=a, pb=pb: t.transpose(mb1.bitcast(BF16)[:, 0:64], xch[pb:pb + 64, a * 128:(a + 1) * 128], identb[pb:pb + 64, pb:pb + 64]), [p.tok("rowld")]),
                ("vector", lambda v, dt_c=dt_c: v.tensor_scalar(Xdt_sb[:, :], mb1.bitcast(BF16)[:, 0:64], dt_c, None, ALU.mult), []),
                ("tensor", lambda t, h=h: t.matmul(mb1[:, 128:256], onesf_row[:, :], rows_all[0:1, (h % 16) * 128:(h % 16) * 128 + 128], start=True, stop=True), []),
                ("vector", lambda v, lA_c=lA_c: v.scalar_tensor_tensor(Gm[:, :], mb1[:, 128:256], lA_c, m_ui[:, :], ALU.subtract, ALU.add), []),
                ("scalar", lambda s: s.activation(Em[:, :], Gm[:, :], AF.Exp), []),
                ("vector", lambda v: v.tensor_tensor(MT_sb[:, :], Em[:, :], BC_sb[:, :], ALU.mult), []),
                ("tensor", lambda t: t.matmul(mb2[:, 0:64], MT_sb[:, :], Xdt_sb[:, :], start=True, stop=True), []),
                ("scalar", lambda s: s.copy(Ydsb[:, :], mb2[:, 0:64]), []),
                ("tensor", lambda t, h=h: t.matmul(mb3[:, 0:64], bc_sb[:, 128:256], state[:, h * 64:(h + 1) * 64], start=True, stop=True), []),
                ("vector", lambda v, elA_c=elA_c: v.scalar_tensor_tensor(t5[:, :], mb3[:, 0:64], elA_c, Ydsb[:, :], ALU.mult, ALU.add), []),
                ("vector", lambda v, h=h: v.scalar_tensor_tensor(y_TM[:, h * 64:(h + 1) * 64], mb1.bitcast(BF16)[:, 0:64], Drep[:, h:h + 1], t5[:, :], ALU.mult, ALU.add), []),
                ("vector", lambda v, ds_c=ds_c: v.tensor_scalar(Btil[:, :], BTM_sb[:, :], ds_c, None, ALU.mult), []),
                ("tensor", lambda t: t.matmul(mb3[:, 128:192], Btil[:, :], Xdt_sb[:, :], start=True, stop=True), []),
                ("vector", lambda v, ds_c=ds_c, elA_c=elA_c: v.tensor_scalar(misc[:, 4:5], ds_c, elA_c, None, ALU.mult), []),
                ("vector", lambda v, h=h: v.scalar_tensor_tensor(state[:, h * 64:(h + 1) * 64], state[:, h * 64:(h + 1) * 64], misc[:, 4:5], mb3[:, 128:192], ALU.mult, ALU.add), []),
            ])
        for a in range(32):
            _chain(p, [
                ("tensor", lambda t, a=a: t.transpose(mb0.bitcast(BF16)[:, 0:128], zch[:, a * 128:(a + 1) * 128], identb[:, :]), []),
                ("vector", lambda v, a=a: v.tensor_copy(zT[:, a * 128:(a + 1) * 128], mb0.bitcast(BF16)[:, 0:128]), []),
            ])
        _chain(p, [
            ("vector", lambda v: v.tensor_tensor(y_TM[:, :], y_TM[:, :], zT[:, :], ALU.mult), []),
            ("scalar", lambda s: s.activation(zT[:, :], y_TM[:, :], AF.Square, accum_out=misc[:, 5:6]), []),
            ("scalar", lambda s: s.activation(misc[:, 6:7], misc[:, 5:6], AF.Ln, scale=1.0 / DI, bias=1e-5), []),
            ("scalar", lambda s: s.activation(misc[:, 7:8], misc[:, 6:7], AF.Exp, scale=-0.5), []),
            ("vector", lambda v: v.scalar_tensor_tensor(yn_TM[:, :], y_TM[:, :], misc[:, 7:8], nw_rep[:, :], ALU.mult, ALU.mult), []),
        ])
        for a in range(32):
            _chain(p, [
                ("tensor", lambda t, a=a: t.transpose(mb0.bitcast(BF16)[:, 0:128], yn_TM[:, a * 128:(a + 1) * 128], identb[:, :]), []),
                ("vector", lambda v, a=a: v.tensor_copy(ynF[:, a * 128:(a + 1) * 128], mb0.bitcast(BF16)[:, 0:128]), []),
            ])
        tk = p.tok("chv")

        @p.block.gpsimd
        def _(g, c=c, tk=tk):
            p.wait(g, "gpsimd", tk)
            e = g.dma_start(out=yn_sp[0:DI, c * 128:(c + 1) * 128].rearrange("(a p) t -> p a t", p=128),
                            in_=ynF[:, :].rearrange("p (a t) -> p a t", t=128))
            p.inc(e, "ynspst", 16)
    yn_ready = p.tok("ynspst")
    prev_og = None
    for og in range(4):
        @p.block.sync
        def _(sy, og=og):
            p.wait(sy, "sync", p.tok("pw_op"))
            if prev_og is not None:
                p.wait(sy, "sync", prev_og)
            for kcc in range(32):
                e = sy.dma_start(out=wob[:, kcc * 512:(kcc + 1) * 512],
                                 in_=wop_bf[og * 512:(og + 1) * 512, kcc * 128:(kcc + 1) * 128], transpose=True)
                p.inc(e, "wopld", 16)

        for tt in range(4):
            @p.block.sync
            def _(sy, tt=tt, og=og):
                p.wait(sy, "sync", p.tok("ynspst"))
                if og > 0 or tt > 0:
                    p.wait(sy, "sync", p.tok("cht"))
                e = sy.dma_start(out=ynt[:, :].rearrange("p (a t) -> p a t", t=512),
                                 in_=yn_sp[0:DI, tt * 512:(tt + 1) * 512].rearrange("(a p) t -> p a t", p=128))
                p.inc(e, "yntld", 16)

            for ol in range(4):
                bnk, ps_, free = pool.acquire()

                def mmfn(t, ps_=ps_, free=free, og=og, ol=ol, tt=tt):
                    p.wait(t, "tensor", free, p.tok("wopld"), p.tok("yntld"))
                    e = None
                    for kcc in range(32):
                        e = t.matmul(ps_[:, :], wob[:, kcc * 512 + ol * 128: kcc * 512 + ol * 128 + 128],
                                     ynt[:, kcc * 512:(kcc + 1) * 512],
                                     start=(kcc == 0), stop=(kcc == 31))
                    return e

                def evfn(s, ps_=ps_, ol=ol, tt=tt):
                    p.wait(s, "scalar", stg_free_ma[(ol + tt) % 2])
                    return s.copy(stg[(ol + tt) % 2][:, :], ps_[:, :])
                tk2 = _chain(p, [("tensor", mmfn, []), ("scalar", evfn, [])])
                pool.release(bnk, tk2)

                @p.block.gpsimd
                def _(g, og=og, ol=ol, tt=tt, tk2=tk2):
                    p.wait(g, "gpsimd", tk2, p.tok("cpacc"))
                    e = g.dma_start(out=xacc[og * 512 + ol * 128: og * 512 + ol * 128 + 128, tt * 512:(tt + 1) * 512],
                                    in_=stg[(ol + tt) % 2][:, :], accum_op=ALU.add)
                    stg_free_ma[(ol + tt) % 2] = p.inc(e, "acc", 16)
        prev_og = p.tok("cht")

    @p.block.sync
    def _(sy):
        p.final_wait(sy, "sync")


stg_free_ma = [None, None]


def build_l1_attn():
    global stg_free_ma
    stg_free_ma = [None, None]
    p = Prog("l1attn")
    with p.es:
        csb, cemit = load_consts(p, ["identf", "identb", "onesf_row", "m_upincl_neg"])
        with p.nc.Block() as blk:
            p.block = blk

            @blk.gpsimd
            def _(g):
                cemit(g)
            emit_l1_attn(p, csb)
    return p


def emit_l1_ffn(p, csb):
    INTER = 8192
    xn1 = p.dram("xn1_in", [D, T], BF16, kind="ExternalInput")
    xacc_in = p.dram("xacc_in", [D, T], F32, kind="ExternalInput")
    xacc = p.dram("xacc", [D, T], F32, kind="ExternalOutput")
    SU_ug = p.dram("SU_ug", [D], F32, kind="ExternalInput")
    SU_dp = p.dram("SU_dp", [INTER], F32, kind="ExternalInput")
    Wscale_ug = p.dram("Wscale_ug", [2 * INTER], F32, kind="ExternalInput")
    Wscale_dp = p.dram("Wscale_dp", [D], F32, kind="ExternalInput")
    W_up = p.dram("W_up", [INTER, D], F32, kind="ExternalInput")
    W_gate = p.dram("W_gate", [INTER, D], F32, kind="ExternalInput")
    W_down = p.dram("W_down", [D, INTER], F32, kind="ExternalInput")

    wu_bf = p.dram("wu_bf", [INTER, D], BF16)
    wg_bf = p.dram("wg_bf", [INTER, D], BF16)
    wd_bf = p.dram("wd_bf", [D, INTER], BF16)
    hadA1 = p.dram("hadA1", [D, T], BF16)
    hadB1 = p.dram("hadB1", [D, T], BF16)
    hadA2 = p.dram("hadA2", [INTER, T], BF16)
    hadB2 = p.dram("hadB2", [INTER, T], BF16)

    identf = csb["identf"]
    IH16 = csb["IH16"]
    IH64 = csb["IH64"]
    H128 = csb["H128"]
    RS2 = 1.0 / float(np.sqrt(2048.0))
    RS8 = 1.0 / float(np.sqrt(8192.0))

    xn_sb = p.sbuf("qf_xn", [128, 16 * T], BF16)
    y2t = xn_sb[:, 0:32768]
    wgb = p.sbuf("qf_wgb", [128, 16 * 512], BF16)
    wub = p.sbuf("qf_wub", [128, 16 * 512], BF16)
    wdob = p.sbuf("qf_wdob", [128, 64 * 256], BF16)
    hbuf = p.sbuf("qf_hbuf", [128, T], BF16)
    tabT = p.sbuf("qf_tabT", [128, 128], F32)
    su_ug = p.sbuf("qf_suug", [128, 16], F32)
    su_dp = p.sbuf("qf_sudp", [128, 64], F32)
    wsc = p.sbuf("qf_wsc", [128, 128], F32)
    wsc_dp = p.sbuf("qf_wscdp", [128, 16], F32)
    sup = p.sbuf("qf_sup", [128, 64], F32)
    sgt = p.sbuf("qf_sgt", [128, 64], F32)
    sdp = p.sbuf("qf_sdp", [128, 16], F32)
    sgs = [p.sbuf(f"qf_sg{i}", [128, 512], F32) for i in range(2)]
    stg = [p.sbuf(f"qf_stg{i}", [128, 512], F32) for i in range(2)]
    evb = [p.sbuf(f"qf_ev{i}", [128, 512], BF16) for i in range(2)]

    pool = PsumPool(p, "qf_pp", 6)

    @p.block.gpsimd
    def _(g):
        emit_prep_cast(p, g, W_up, wu_bf, "pw_u")
        emit_prep_cast(p, g, W_gate, wg_bf, "pw_g")
        emit_prep_cast(p, g, W_down, wd_bf, "pw_d")
        e = g.dma_start(out=tabT[0:16, 0:128], in_=SU_ug[:].rearrange("(c q) -> c q", q=128))
        p.inc(e, "tld1", 16)
        for i in range(4):
            e = g.dma_start(out=xacc[i * 512:(i + 1) * 512, :], in_=xacc_in[i * 512:(i + 1) * 512, :])
            p.inc(e, "cpacc", 16)

    bnk0, ps0, _f = pool.acquire()
    _chain(p, [
        ("tensor", lambda t: t.transpose(ps0[:, 0:16], tabT[0:16, 0:128], identf[0:16, 0:16]), [p.tok("tld1"), p.tok("cld")]),
        ("vector", lambda v: v.tensor_copy(su_ug[:, :], ps0[:, 0:16]), []),
    ])

    @p.block.gpsimd
    def _(g):
        p.wait(g, "gpsimd", p.tok("chv"))
        e = g.dma_start(out=tabT[0:64, 0:128], in_=SU_dp[:].rearrange("(c q) -> c q", q=128))
        p.inc(e, "tld2", 16)
    _chain(p, [
        ("tensor", lambda t: t.transpose(ps0[:, 0:64], tabT[0:64, 0:128], identf[0:64, 0:64]), [p.tok("tld2")]),
        ("vector", lambda v: v.tensor_copy(su_dp[:, :], ps0[:, 0:64]), []),
    ])

    @p.block.gpsimd
    def _(g):
        p.wait(g, "gpsimd", p.tok("chv"))
        e = g.dma_start(out=tabT[0:128, 0:128], in_=Wscale_ug[:].rearrange("(c q) -> c q", q=128))
        p.inc(e, "tld3", 16)
    _chain(p, [
        ("tensor", lambda t: t.transpose(ps0[:, 0:128], tabT[:, 0:128], identf[:, :]), [p.tok("tld3")]),
        ("vector", lambda v: v.tensor_copy(wsc[:, :], ps0[:, 0:128]), []),
    ])

    @p.block.gpsimd
    def _(g):
        p.wait(g, "gpsimd", p.tok("chv"))
        e = g.dma_start(out=tabT[0:16, 0:128], in_=Wscale_dp[:].rearrange("(c q) -> c q", q=128))
        p.inc(e, "tld4", 16)
    _chain(p, [
        ("tensor", lambda t: t.transpose(ps0[:, 0:16], tabT[0:16, 0:128], identf[0:16, 0:16]), [p.tok("tld4")]),
        ("vector", lambda v: v.tensor_copy(wsc_dp[:, :], ps0[:, 0:16]), []),
        ("vector", lambda v: v.tensor_mul(sup[:, :], wsc[:, 0:64], su_dp[:, :]), []),
        ("vector", lambda v: v.tensor_scalar_mul(sup[:, :], sup[:, :], RS2), []),
        ("vector", lambda v: v.tensor_scalar_mul(sgt[:, :], wsc[:, 64:128], RS2), []),
        ("vector", lambda v: v.tensor_scalar_mul(sdp[:, :], wsc_dp[:, :], RS8), []),
    ])
    pool.release(bnk0, p.tok("chv"))

    @p.block.sync
    def _(sy):
        emit_load_xn(p, sy, xn1, xn_sb, "xnld", tm=False)

    for dc in range(16):
        _chain(p, [
            ("vector", lambda v, dc=dc: v.tensor_scalar(xn_sb[:, dc * T:(dc + 1) * T], xn_sb[:, dc * T:(dc + 1) * T],
                                                        su_ug[:, dc:dc + 1], None, ALU.mult), [p.tok("xnld")]),
        ])
        for tt in range(4):
            bnk, ps_, free = pool.acquire()
            b = (dc * 4 + tt) % 2

            def mmfn(t, dc=dc, tt=tt, ps_=ps_, free=free):
                p.wait(t, "tensor", free)
                return t.matmul(ps_[:, :], IH16[:, :], xn_sb[:, dc * T + tt * 512: dc * T + tt * 512 + 512],
                                start=True, stop=True)

            def evfn(v, ps_=ps_, b=b):
                p.wait(v, "vector", evfree_qf[b])
                return v.tensor_copy(evb[b][:, :], ps_[:, :])
            tk = _chain(p, [("tensor", mmfn, []), ("vector", evfn, [])])
            pool.release(bnk, tk)

            @p.block.gpsimd
            def _(g, dc=dc, tt=tt, b=b, tk=tk):
                p.wait(g, "gpsimd", tk)
                e = g.dma_start(out=hadA1[dc * 128:(dc + 1) * 128, tt * 512:(tt + 1) * 512], in_=evb[b][:, :])
                evfree_qf[b] = p.inc(e, "hA1", 16)

    for q in range(16):
        for tt in range(4):
            bnk, ps_, free = pool.acquire()
            b = (q * 4 + tt) % 2

            @p.block.sync
            def _(sy, q=q, tt=tt):
                p.wait(sy, "sync", p.tok("hA1"))
                p.wait(sy, "sync", p.tok("b1ld"))
                p.wait(sy, "sync", p.tok("cht"))
                e = sy.dma_start(out=vq_qf[0][:, :], in_=hadA1[q:D:16, tt * 512:(tt + 1) * 512])
                p.inc(e, "b1ld", 16)

            def mmfn(t, ps_=ps_, free=free):
                p.wait(t, "tensor", free, p.tok("b1ld"))
                return t.matmul(ps_[:, :], H128[:, :], vq_qf[0][:, :], start=True, stop=True)

            def evfn(v, ps_=ps_, b=b):
                p.wait(v, "vector", evfree_qf[b])
                return v.tensor_copy(evb[b][:, :], ps_[:, :])
            tk = _chain(p, [("tensor", mmfn, []), ("vector", evfn, [])])
            pool.release(bnk, tk)

            @p.block.gpsimd
            def _(g, q=q, tt=tt, b=b, tk=tk):
                p.wait(g, "gpsimd", tk)
                e = g.dma_start(out=hadB1[q:D:16, tt * 512:(tt + 1) * 512], in_=evb[b][:, :])
                evfree_qf[b] = p.inc(e, "hB1", 16)

    @p.block.sync
    def _(sy):
        p.wait(sy, "sync", p.tok("hB1"), p.tok("cht"), p.tok("chv"))
        for dc in range(16):
            e = sy.dma_start(out=xn_sb[:, dc * T:(dc + 1) * T], in_=hadB1[dc * 128:(dc + 1) * 128, :])
            p.inc(e, "y1ld", 16)

    last_og = None
    for og in range(16):
        @p.block.sync
        def _(sy, og=og):
            p.wait(sy, "sync", p.tok("pw_u"), p.tok("pw_g"))
            if last_og is not None:
                p.wait(sy, "sync", last_og)
            for kcc in range(16):
                e = sy.dma_start(out=wgb[:, kcc * 512:(kcc + 1) * 512],
                                 in_=wg_bf[og * 512:(og + 1) * 512, kcc * 128:(kcc + 1) * 128], transpose=True)
                p.inc(e, "wugld", 16)
                e = sy.dma_start(out=wub[:, kcc * 512:(kcc + 1) * 512],
                                 in_=wu_bf[og * 512:(og + 1) * 512, kcc * 128:(kcc + 1) * 128], transpose=True)
                p.inc(e, "wugld", 16)

        for ol in range(4):
            oc = og * 4 + ol
            for tt in range(4):
                bg, gps, gfree = pool.acquire()
                bu, ups, ufree = pool.acquire()

                def mmfn(t, gps=gps, ups=ups, gfree=gfree, ufree=ufree, ol=ol, tt=tt):
                    p.wait(t, "tensor", gfree, ufree, p.tok("wugld"), p.tok("y1ld"))
                    e = None
                    for kcc in range(16):
                        e = t.matmul(gps[:, :], wgb[:, kcc * 512 + ol * 128: kcc * 512 + ol * 128 + 128],
                                     xn_sb[:, kcc * T + tt * 512: kcc * T + tt * 512 + 512],
                                     start=(kcc == 0), stop=(kcc == 15))
                    globals()["_qf_g"] = p.inc(e, "cht", 1)
                    for kcc in range(16):
                        e = t.matmul(ups[:, :], wub[:, kcc * 512 + ol * 128: kcc * 512 + ol * 128 + 128],
                                     xn_sb[:, kcc * T + tt * 512: kcc * T + tt * 512 + 512],
                                     start=(kcc == 0), stop=(kcc == 15))
                    return e
                b = tt % 2

                def sgfn(s, gps=gps, oc=oc, b=b):
                    p.wait(s, "scalar", sgfree_qf[b])
                    return s.activation(sgs[b][:, :], gps[:, :], AF.Silu, scale=sgt[:, oc % 64: oc % 64 + 1] if False else sgt[:, oc // 1 % 64: oc // 1 % 64 + 1])

                def hfn(v, ups=ups, oc=oc, tt=tt, b=b):
                    return v.scalar_tensor_tensor(hbuf[:, tt * 512:(tt + 1) * 512], ups[:, :],
                                                  sup[:, oc % 64: oc % 64 + 1], sgs[b][:, :], ALU.mult, ALU.mult)
                tk = _chain(p, [("tensor", mmfn, []), ("scalar", sgfn, []), ("vector", hfn, [])])
                pool.release(bg, tk)
                pool.release(bu, tk)
                sgfree_qf[b] = tk
            for tt in range(4):
                bnk, ps_, free = pool.acquire()
                b = tt % 2

                def mmfn2(t, tt=tt, ps_=ps_, free=free):
                    p.wait(t, "tensor", free)
                    return t.matmul(ps_[:, :], IH64[:, :], hbuf[:, tt * 512:(tt + 1) * 512], start=True, stop=True)

                def evfn2(v, ps_=ps_, b=b):
                    p.wait(v, "vector", evfree_qf[b])
                    return v.tensor_copy(evb[b][:, :], ps_[:, :])
                tk = _chain(p, [("tensor", mmfn2, []), ("vector", evfn2, [])])
                pool.release(bnk, tk)

                @p.block.gpsimd
                def _(g, oc=oc, tt=tt, b=b, tk=tk):
                    p.wait(g, "gpsimd", tk)
                    e = g.dma_start(out=hadA2[oc * 128:(oc + 1) * 128, tt * 512:(tt + 1) * 512], in_=evb[b][:, :])
                    evfree_qf[b] = p.inc(e, "hA2", 16)
        last_og = p.tok("cht")

    for q in range(64):
        for tt in range(4):
            bnk, ps_, free = pool.acquire()
            b = (q * 4 + tt) % 2

            @p.block.sync
            def _(sy, q=q, tt=tt):
                p.wait(sy, "sync", p.tok("hA2"), p.tok("cht"))
                e = sy.dma_start(out=vq_qf[0][:, :], in_=hadA2[q:INTER:64, tt * 512:(tt + 1) * 512])
                p.inc(e, "b2ld", 16)

            def mmfn(t, ps_=ps_, free=free):
                p.wait(t, "tensor", free, p.tok("b2ld"))
                return t.matmul(ps_[:, :], H128[:, :], vq_qf[0][:, :], start=True, stop=True)

            def evfn(v, ps_=ps_, b=b):
                p.wait(v, "vector", evfree_qf[b])
                return v.tensor_copy(evb[b][:, :], ps_[:, :])
            tk = _chain(p, [("tensor", mmfn, []), ("vector", evfn, [])])
            pool.release(bnk, tk)

            @p.block.gpsimd
            def _(g, q=q, tt=tt, b=b, tk=tk):
                p.wait(g, "gpsimd", tk)
                e = g.dma_start(out=hadB2[q:INTER:64, tt * 512:(tt + 1) * 512], in_=evb[b][:, :])
                evfree_qf[b] = p.inc(e, "hB2", 16)

    acc_base = p.tok("acc")
    prev_og = None
    for og in range(8):
        @p.block.sync
        def _(sy, og=og):
            p.wait(sy, "sync", p.tok("pw_d"))
            if prev_og is not None:
                p.wait(sy, "sync", prev_og)
            for kcc in range(64):
                e = sy.dma_start(out=wdob[:, kcc * 256:(kcc + 1) * 256],
                                 in_=wd_bf[og * 256:(og + 1) * 256, kcc * 128:(kcc + 1) * 128], transpose=True)
                p.inc(e, "wdld", 16)

        for tt in range(4):
            @p.block.sync
            def _(sy, tt=tt, og=og):
                p.wait(sy, "sync", p.tok("hB2"), p.tok("cht"))
                for kcc in range(64):
                    e = sy.dma_start(out=y2t[:, kcc * 512:(kcc + 1) * 512],
                                     in_=hadB2[kcc * 128:(kcc + 1) * 128, tt * 512:(tt + 1) * 512])
                    p.inc(e, "y2ld", 16)

            for ol in range(2):
                oc = og * 2 + ol
                bnk, ps_, free = pool.acquire()

                def mmfn(t, ps_=ps_, free=free, ol=ol):
                    p.wait(t, "tensor", free, p.tok("wdld"), p.tok("y2ld"))
                    e = None
                    for kcc in range(64):
                        e = t.matmul(ps_[:, :], wdob[:, kcc * 256 + ol * 128: kcc * 256 + ol * 128 + 128],
                                     y2t[:, kcc * 512:(kcc + 1) * 512],
                                     start=(kcc == 0), stop=(kcc == 63))
                    return e
                sb_i = (tt + ol) % 2

                def evfn(s, ps_=ps_, oc=oc, sb_i=sb_i):
                    p.wait(s, "scalar", dnfree_qf[sb_i])
                    return s.activation(stg[sb_i][:, :], ps_[:, :], AF.Copy, scale=sdp[:, oc // 1 % 16: oc // 1 % 16 + 1]) if False else s.mul(stg[sb_i][:, :], ps_[:, :], sdp[:, oc % 16: oc % 16 + 1])
                tk = _chain(p, [("tensor", mmfn, []), ("scalar", evfn, [])])
                pool.release(bnk, tk)

                @p.block.gpsimd
                def _(g, oc=oc, tt=tt, sb_i=sb_i, tk=tk):
                    p.wait(g, "gpsimd", tk, p.tok("cpacc"), acc_base)
                    e = g.dma_start(out=xacc[oc * 128:(oc + 1) * 128, tt * 512:(tt + 1) * 512],
                                    in_=stg[sb_i][:, :], accum_op=ALU.add)
                    dnfree_qf[sb_i] = p.inc(e, "acc", 16)
        prev_og = p.tok("cht")

    @p.block.sync
    def _(sy):
        p.final_wait(sy, "sync")


evfree_qf = [None, None]
sgfree_qf = [None, None]
dnfree_qf = [None, None]
vq_qf = [None]


def build_l1_ffn():
    global evfree_qf, sgfree_qf, dnfree_qf
    evfree_qf = [None, None]
    sgfree_qf = [None, None]
    dnfree_qf = [None, None]
    p = Prog("l1ffn")
    with p.es:
        csb, cemit = load_consts(p, ["identf", "IH16", "IH64", "H128"])
        vq_qf[0] = p.sbuf("qf_vq", [128, 512], BF16)
        with p.nc.Block() as blk:
            p.block = blk

            @blk.gpsimd
            def _(g):
                cemit(g)
            emit_l1_ffn(p, csb)
    return p


_CN = {
    "ln0": ["identf"],
    "l0attn": ["identf", "identb", "ones_row", "onesf_row", "onescol_b",
               "m_upincl_neg", "m_upstrict_neg", "m_lowstrict_pos", "twoI"],
    "l0moe": ["identf", "ones_row"],
    "lnfm": ["onesf_row", "onescol_b"],
    "l1attn": ["identf", "identb", "onesf_row", "m_upincl_neg"],
    "l1ffn": ["identf", "IH16", "IH64", "H128"],
    "final": ["identf"],
}

_PROGS = {}


def _prog(name):
    if name not in _PROGS:
        _PROGS[name] = {
            "ln0": build_ln0, "l0attn": build_l0_attn, "l0moe": build_l0_moe,
            "lnfm": build_ln_fm, "l1attn": build_l1_attn, "l1ffn": build_l1_ffn,
            "final": build_final,
        }[name]()
    return _PROGS[name]


def kernel(**inputs):
    cores = list(range(NCORE))
    x = np.ascontiguousarray(np.asarray(inputs["L0_attn_x"], dtype=np.float32))

    def run(name, per_core, shared):
        p = _prog(name)
        ci = const_inputs(p, _CN[name])
        in_maps = []
        for b in range(NCORE):
            m = dict(shared)
            m.update(ci)
            m.update({k: v[b] for k, v in per_core.items()})
            in_maps.append(m)
        res = run_bass_kernel_spmd(p.nc, in_maps, core_ids=cores)
        return res.results

    r = run("ln0", {"x_in": [x[b] for b in range(NCORE)]}, {})
    xacc = [r[b]["xacc"] for b in range(NCORE)]
    xn0 = [r[b]["xn0"] for b in range(NCORE)]

    r = run("l0attn", {"xn0_in": xn0, "xacc_in": xacc}, {
        "W_qkv": inputs["L0_attn_W_qkv"], "W_z": inputs["L0_attn_W_z"],
        "W_b": inputs["L0_attn_W_b"], "W_a": inputs["L0_attn_W_a"],
        "conv_w": inputs["L0_attn_conv_w"], "dt_bias": inputs["L0_attn_dt_bias"],
        "A_log": inputs["L0_attn_A_log"], "norm_w": inputs["L0_attn_norm_w"],
        "W_out": inputs["L0_attn_W_out"]})
    xacc = [r[b]["xacc"] for b in range(NCORE)]

    r = run("l0moe", {"xn0_in": xn0, "xacc_in": xacc}, {
        "gate_w": inputs["L0_ffn_gate_w"], "w_gate": inputs["L0_ffn_w_gate"],
        "w_up": inputs["L0_ffn_w_up"], "w_down": inputs["L0_ffn_w_down"],
        "sh_gate": inputs["L0_ffn_sh_gate"], "sh_up": inputs["L0_ffn_sh_up"],
        "sh_down": inputs["L0_ffn_sh_down"], "se_gate_w": inputs["L0_ffn_se_gate_w"]})
    xacc = [r[b]["xacc"] for b in range(NCORE)]

    r = run("lnfm", {"xacc_in": xacc}, {})
    xn1 = [r[b]["xn1"] for b in range(NCORE)]

    r = run("l1attn", {"xn1_in": xn1, "xacc_in": xacc}, {
        "in_proj_w": inputs["L1_attn_in_proj_w"], "conv_w": inputs["L1_attn_conv_w"],
        "conv_b": inputs["L1_attn_conv_b"], "dt_bias": inputs["L1_attn_dt_bias"],
        "A_log": inputs["L1_attn_A_log"], "Dvec": inputs["L1_attn_D"],
        "norm_weight": inputs["L1_attn_norm_weight"], "out_proj_w": inputs["L1_attn_out_proj_w"]})
    xacc = [r[b]["xacc"] for b in range(NCORE)]

    r = run("l1ffn", {"xn1_in": xn1, "xacc_in": xacc}, {
        "SU_ug": inputs["L1_ffn_SU_ug"], "SU_dp": inputs["L1_ffn_SU_dp"],
        "Wscale_ug": inputs["L1_ffn_Wscale_ug"], "Wscale_dp": inputs["L1_ffn_Wscale_dp"],
        "W_up": inputs["L1_ffn_W_up"], "W_gate": inputs["L1_ffn_W_gate"],
        "W_down": inputs["L1_ffn_W_down"]})
    xacc = [r[b]["xacc"] for b in range(NCORE)]

    r = run("final", {"xacc_in": xacc}, {})
    out = np.stack([np.asarray(r[b]["y_out"], dtype=np.float32) for b in range(NCORE)], axis=0)
    return out


# version 02
# speedup vs baseline: 1.0070x; 1.0070x over previous
import numpy as np
import ml_dtypes
from contextlib import ExitStack
from collections import defaultdict
import concourse.bass as bass
import concourse.mybir as mybir
from concourse.bass_utils import run_bass_kernel_spmd

F32 = mybir.dt.float32
BF16 = mybir.dt.bfloat16
AF = mybir.ActivationFunctionType
ALU = mybir.AluOpType
AX = mybir.AxisListType
BF = ml_dtypes.bfloat16

T = 2048
D = 2048
NCORE = 8
NEG = -1e30
ENGS = ("tensor", "vector", "scalar", "gpsimd", "sync")
CPOOL_N = 32


def _sylvester(n):
    H = np.array([[1.0]], dtype=np.float64)
    while H.shape[0] < n:
        H = np.block([[H, H], [H, -H]])
    return H


_HOST_CONSTS = None


def host_constants():
    global _HOST_CONSTS
    if _HOST_CONSTS is not None:
        return _HOST_CONSTS
    c = {}
    c["identb"] = np.eye(128, dtype=BF)
    c["identf"] = np.eye(128, dtype=np.float32)
    r = np.arange(128)
    c["m_upincl_neg"] = np.where(r[:, None] <= r[None, :], 0.0, NEG).astype(np.float32)
    c["m_upstrict_neg"] = np.where(r[:, None] < r[None, :], 0.0, NEG).astype(np.float32)
    c["m_lowstrict_pos"] = np.where(r[None, :] < r[:, None], 0.0, -NEG).astype(np.float32)
    c["twoI"] = (2.0 * np.eye(128)).astype(np.float32)
    c["IH16"] = np.kron(np.eye(8), _sylvester(16)).astype(BF)
    c["IH64"] = np.kron(np.eye(2), _sylvester(64)).astype(BF)
    c["H128"] = _sylvester(128).astype(BF)
    c["ones_row"] = np.ones((1, 128), dtype=BF)
    c["onesf_row"] = np.ones((1, 128), dtype=np.float32)
    c["onescol_b"] = np.ones((128, 1), dtype=BF)
    _HOST_CONSTS = c
    return c


class Prog:
    def __init__(self, name="prog"):
        self.name = name
        self.nc = bass.Bass("TRN2", target_bir_lowering=False)
        self.es = ExitStack()
        self._frames = []
        self._sems = {}
        self.tot = defaultdict(int)
        self.seen = {e: defaultdict(int) for e in ENGS}
        self.block = None
        self.cvals = []
        self._cpool = None
        self.snaps = {}
        self.prefix = ""
        self.io = {}

    def push_frame(self, prefix=""):
        self._frames.append(ExitStack())
        self.prefix = prefix

    def pop_frame(self):
        self._frames.pop().close()
        self.prefix = ""

    def barrier(self):
        for engname in ENGS:
            sec = getattr(self.block, engname)

            def b(eng, engname=engname):
                self.final_wait(eng, engname)
            sec(b)

    def sem(self, name):
        if name not in self._sems:
            self._sems[name] = self.es.enter_context(self.nc.semaphore(name))
        return self._sems[name]

    def _stack(self):
        return self._frames[-1] if self._frames else self.es

    def sbuf(self, name, shape, dt):
        return self._stack().enter_context(self.nc.sbuf_tensor(self.prefix + name, shape, dt))

    def psum(self, name, shape, dt=F32):
        return self._stack().enter_context(self.nc.psum_tensor(self.prefix + name, shape, dt))

    def dram(self, name, shape, dt, kind="Internal"):
        if name in self.io:
            return self.io[name]
        pfx = self.prefix if kind == "Internal" else ""
        return self.nc.dram_tensor(pfx + name, shape, dt, kind=kind).ap()

    def cscalar(self, v):
        v = float(v)
        if self._cpool is None:
            self._cpool = self.es.enter_context(self.nc.sbuf_tensor("cpool", [128, CPOOL_N], F32))
        if v not in self.cvals:
            assert len(self.cvals) < CPOOL_N
            self.cvals.append(v)
            i = self.cvals.index(v)
            self.nc.const_aps.register(F32, v, self._cpool[:, i:i + 1])
        return v

    def inc(self, inst, name, n=1):
        inst.then_inc(self.sem(name), n)
        self.tot[name] += n
        return (name, self.tot[name])

    def tok(self, name):
        return (name, self.tot[name])

    def wait(self, eng, engname, *toks):
        for tk in toks:
            if tk is None:
                continue
            name, val = tk
            if val > 0 and self.seen[engname][name] < val:
                eng.wait_ge(self.sem(name), val)
                self.seen[engname][name] = val

    def snapshot(self, key):
        self.snaps[key] = dict(self.tot)

    def snap_tok(self, key, name):
        if key not in self.snaps:
            return None
        return (name, self.snaps[key].get(name, 0))

    def wait_snap(self, eng, engname, key, names):
        for n in names:
            self.wait(eng, engname, self.snap_tok(key, n))

    def final_wait(self, eng, engname):
        for name in list(self.tot.keys()):
            self.wait(eng, engname, (name, self.tot[name]))


def load_consts(p, names):
    cvals = host_constants()
    out = {}
    aps = {}
    dt_map = {np.dtype(np.float32): F32, np.dtype(BF): BF16}
    for n in names:
        a = cvals[n]
        aps[n] = p.dram("c_" + n, list(a.shape), dt_map[a.dtype], kind="ExternalInput")
        out[n] = p.sbuf("cs_" + n, list(a.shape), dt_map[a.dtype])
    cpool_in = p.dram("cpool_in", [128, CPOOL_N], F32, kind="ExternalInput")

    def emit(g):
        if p._cpool is None:
            p._cpool = p.es.enter_context(p.nc.sbuf_tensor("cpool", [128, CPOOL_N], F32))
        last = g.dma_start(out=p._cpool[:], in_=cpool_in[:, :])
        p.inc(last, "cld", 16)
        for n in names:
            last = g.dma_start(out=out[n][:], in_=aps[n][:, :] if aps[n].ndim == 2 else aps[n][:])
            p.inc(last, "cld", 16)
    return out, emit


def const_inputs(p, names):
    cvals = host_constants()
    d = {"c_" + n: cvals[n] for n in names}
    pool = np.zeros((128, CPOOL_N), dtype=np.float32)
    for i, v in enumerate(p.cvals):
        pool[:, i] = v
    d["cpool_in"] = pool
    return d


def emit_ln0(p, csb):
    x_in = p.dram("x_in", [T, D], F32, kind="ExternalInput")
    xacc = p.dram("xacc", [D, T], F32, kind="ExternalOutput")
    xn0 = p.dram("xn0", [T, D], BF16, kind="ExternalOutput")

    p.cscalar(1e-5)
    p.cscalar(-0.5)

    xt = [p.sbuf(f"ln_x{i}", [128, D], F32) for i in range(2)]
    xnb = [p.sbuf(f"ln_xn{i}", [128, D], BF16) for i in range(2)]
    sq = p.sbuf("ln_sq", [128, D], F32)
    cols = [p.sbuf(f"ln_cols{i}", [128, 8], F32) for i in range(2)]
    stg = [p.sbuf(f"ln_stg{i}", [128, 128], F32) for i in range(4)]
    pst = [p.psum(f"ln_pst{i}", [128, 128], F32) for i in range(2)]
    identf = csb["identf"]

    NT = T // 128
    for tc in range(NT):
        b = tc % 2

        @p.block.gpsimd
        def _(g):
            p.wait_snap(g, "gpsimd", tc - 2, [f"lnxn{b}", "lnev0", "lnev1", "s", "v", "xnst"])
            e = g.dma_start(out=xt[b][:], in_=x_in[tc * 128:(tc + 1) * 128, :])
            p.inc(e, f"lnld{b}", 16)

        @p.block.scalar
        def _(s):
            p.wait(s, "scalar", p.tok(f"lnld{b}"), p.tok("cld"))
            e = s.activation(sq[:], xt[b][:], AF.Square, accum_out=cols[b][:, 1:2])
            p.inc(e, "s", 1)

        @p.block.vector
        def _(v):
            p.wait(v, "vector", p.tok(f"lnld{b}"))
            e = v.reduce_sum(cols[b][:, 0:1], xt[b][:], axis=AX.X)
            tk = p.inc(e, "v", 1)
            p.wait(v, "vector", tk)
            e = v.tensor_scalar_mul(cols[b][:, 2:3], cols[b][:, 0:1], -1.0 / D)
            tk = p.inc(e, "v", 1)
            p.wait(v, "vector", tk, p.tok("s"))
            e = v.scalar_tensor_tensor(cols[b][:, 3:4], cols[b][:, 0:1], cols[b][:, 2:3], cols[b][:, 1:2], ALU.mult, ALU.add)
            tk = p.inc(e, "v", 1)
            p.wait(v, "vector", tk)
            e = v.tensor_scalar_mul(cols[b][:, 4:5], cols[b][:, 3:4], 1.0 / D)
            p.inc(e, "v", 1)

        @p.block.scalar
        def _(s):
            p.wait(s, "scalar", p.tok("v"))
            e = s.activation(cols[b][:, 6:7], cols[b][:, 4:5], AF.Ln, bias=1e-5)
            tk = p.inc(e, "s", 1)
            p.wait(s, "scalar", tk)
            e = s.activation(cols[b][:, 5:6], cols[b][:, 6:7], AF.Exp, scale=-0.5)
            p.inc(e, "s", 1)

        @p.block.vector
        def _(v):
            p.wait(v, "vector", p.tok("s"))
            e = v.tensor_scalar(xnb[b][:], xt[b][:], cols[b][:, 2:3], cols[b][:, 5:6], ALU.add, ALU.mult)
            p.inc(e, f"lnxn{b}", 1)

        @p.block.gpsimd
        def _(g):
            p.wait(g, "gpsimd", p.tok(f"lnxn{b}"))
            e = g.dma_start(out=xn0[tc * 128:(tc + 1) * 128, :], in_=xnb[b][:])
            p.inc(e, "xnst", 16)

        for kc in range(D // 128):
            pb = kc % 2
            sb = kc % 4
            idx = tc * 16 + kc

            @p.block.tensor
            def _(t):
                p.wait(t, "tensor", p.tok(f"lnld{b}"), p.tok("cld"))
                if idx >= 2:
                    p.wait(t, "tensor", (f"lnev{pb}", (idx - 2) // 2 + 1))
                e = t.transpose(pst[pb][:, :], xt[b][:, kc * 128:(kc + 1) * 128], identf[:, :])
                p.inc(e, f"lntr{pb}", 1)

            @p.block.vector
            def _(v):
                p.wait(v, "vector", p.tok(f"lntr{pb}"))
                if idx >= 4:
                    p.wait(v, "vector", (f"lnst{sb}", ((idx - 4) // 4 + 1) * 16))
                e = v.tensor_copy(stg[sb][:], pst[pb][:, :])
                p.inc(e, f"lnev{pb}", 1)

            @p.block.sync
            def _(sy):
                p.wait(sy, "sync", p.tok(f"lnev{pb}"))
                e = sy.dma_start(out=xacc[kc * 128:(kc + 1) * 128, tc * 128:(tc + 1) * 128], in_=stg[sb][:])
                p.inc(e, f"lnst{sb}", 16)

        p.snapshot(tc)

    @p.block.sync
    def _(sy):
        p.final_wait(sy, "sync")


def emit_final(p, csb, src_name="xacc_in"):
    xacc = p.dram(src_name, [D, T], F32, kind="ExternalInput")
    out = p.dram("y_out", [T, D], F32, kind="ExternalOutput")

    xt = [p.sbuf(f"fn_x{i}", [128, T], F32) for i in range(2)]
    stg = [p.sbuf(f"fn_stg{i}", [128, 128], F32) for i in range(4)]
    pst = [p.psum(f"fn_pst{i}", [128, 128], F32) for i in range(2)]
    identf = csb["identf"]

    ND = D // 128
    for dc in range(ND):
        b = dc % 2

        @p.block.gpsimd
        def _(g):
            p.wait_snap(g, "gpsimd", dc - 2, ["fnev0", "fnev1"])
            e = g.dma_start(out=xt[b][:], in_=xacc[dc * 128:(dc + 1) * 128, :])
            p.inc(e, f"fnld{b}", 16)

        for tc in range(T // 128):
            pb = tc % 2
            sb = tc % 4
            idx = dc * 16 + tc

            @p.block.tensor
            def _(t):
                p.wait(t, "tensor", p.tok(f"fnld{b}"), p.tok("cld"))
                if idx >= 2:
                    p.wait(t, "tensor", (f"fnev{pb}", (idx - 2) // 2 + 1))
                e = t.transpose(pst[pb][:, :], xt[b][:, tc * 128:(tc + 1) * 128], identf[:, :])
                p.inc(e, f"fntr{pb}", 1)

            @p.block.vector
            def _(v):
                p.wait(v, "vector", p.tok(f"fntr{pb}"))
                if idx >= 4:
                    p.wait(v, "vector", (f"fnst{sb}", ((idx - 4) // 4 + 1) * 16))
                e = v.tensor_copy(stg[sb][:], pst[pb][:, :])
                p.inc(e, f"fnev{pb}", 1)

            @p.block.sync
            def _(sy):
                p.wait(sy, "sync", p.tok(f"fnev{pb}"))
                e = sy.dma_start(out=out[tc * 128:(tc + 1) * 128, dc * 128:(dc + 1) * 128], in_=stg[sb][:])
                p.inc(e, f"fnst{sb}", 16)

        p.snapshot(dc)

    @p.block.sync
    def _(sy):
        p.final_wait(sy, "sync")


def build_ln0():
    p = Prog("ln0")
    with p.es:
        csb, cemit = load_consts(p, ["identf"])
        with p.nc.Block() as blk:
            p.block = blk

            @blk.gpsimd
            def _(g):
                cemit(g)
            emit_ln0(p, csb)
    return p


def build_final():
    p = Prog("final")
    with p.es:
        csb, cemit = load_consts(p, ["identf"])
        with p.nc.Block() as blk:
            p.block = blk

            @blk.gpsimd
            def _(g):
                cemit(g)
            emit_final(p, csb)
    return p


def emit_prep_cast(p, g, src_ap, dst_ap, tokname, chunk_rows=4096):
    n = src_ap.shape[0]
    r = 0
    while r < n:
        m = min(chunk_rows, n - r)
        e = g.dma_start(out=dst_ap[r:r + m, :], in_=src_ap[r:r + m, :])
        p.inc(e, tokname, 16)
        r += m


def emit_load_xn(p, g_or_sy, xn_ap, xn_sb, tokname, tm=True):
    for dc in range(D // 128):
        if tm:
            e = g_or_sy.dma_start(out=xn_sb[:, dc * T:(dc + 1) * T],
                                  in_=xn_ap[0:T, dc * 128:(dc + 1) * 128], transpose=True)
        else:
            e = g_or_sy.dma_start(out=xn_sb[:, dc * T:(dc + 1) * T],
                                  in_=xn_ap[dc * 128:(dc + 1) * 128, :])
        p.inc(e, tokname, 16)


def emit_prep_cast(p, g, src_ap, dst_ap, tokname, chunk_rows=4096):
    n = src_ap.shape[0]
    r = 0
    while r < n:
        m = min(chunk_rows, n - r)
        e = g.dma_start(out=dst_ap[r:r + m, :], in_=src_ap[r:r + m, :])
        p.inc(e, tokname, 16)
        r += m


def emit_load_xn(p, eng, xn_ap, xn_sb, tokname, tm=True):
    for dc in range(D // 128):
        if tm:
            e = eng.dma_start(out=xn_sb[:, dc * T:(dc + 1) * T],
                              in_=xn_ap[0:T, dc * 128:(dc + 1) * 128], transpose=True)
        else:
            e = eng.dma_start(out=xn_sb[:, dc * T:(dc + 1) * T],
                              in_=xn_ap[dc * 128:(dc + 1) * 128, :])
        p.inc(e, tokname, 16)


class PsumPool:
    def __init__(self, p, name, n, shape=(128, 512)):
        self.p = p
        self.tiles = [p.psum(f"{name}{i}", list(shape)) for i in range(n)]
        self.free_tok = [None] * n
        self.i = 0

    def acquire(self):
        i = self.i
        self.i = (self.i + 1) % len(self.tiles)
        return i, self.tiles[i], self.free_tok[i]

    def release(self, i, tok):
        self.free_tok[i] = tok


def emit_l0_moe(p, csb):
    NH = 1024
    HS = 2048
    xn0 = p.dram("xn0_in", [T, D], BF16, kind="ExternalInput")
    xacc_in = p.dram("xacc_in", [D, T], F32, kind="ExternalInput")
    xacc = p.dram("xacc", [D, T], F32, kind="ExternalOutput")
    gate_w = p.dram("gate_w", [8, D], F32, kind="ExternalInput")
    w_gate = p.dram("w_gate", [8, NH, D], F32, kind="ExternalInput")
    w_up = p.dram("w_up", [8, NH, D], F32, kind="ExternalInput")
    w_down = p.dram("w_down", [8, D, NH], F32, kind="ExternalInput")
    sh_gate = p.dram("sh_gate", [HS, D], F32, kind="ExternalInput")
    sh_up = p.dram("sh_up", [HS, D], F32, kind="ExternalInput")
    sh_down = p.dram("sh_down", [D, HS], F32, kind="ExternalInput")
    se_gate_w = p.dram("se_gate_w", [1, D], F32, kind="ExternalInput")

    wg_bf = p.dram("wg_bf", [8 * NH, D], BF16)
    wu_bf = p.dram("wu_bf", [8 * NH, D], BF16)
    wd_bf = p.dram("wd_bf", [8 * D, NH], BF16)
    shg_bf = p.dram("shg_bf", [HS, D], BF16)
    shu_bf = p.dram("shu_bf", [HS, D], BF16)
    shd_bf = p.dram("shd_bf", [D, HS], BF16)

    identf = csb["identf"]
    ones_row = csb["ones_row"]

    xn_sb = p.sbuf("moe_xn", [128, 16 * T], BF16)
    gsc_sb = p.sbuf("moe_gsc", [9, D], F32)
    gwT = p.sbuf("moe_gwT", [128, 16 * 9], BF16)
    sc = p.sbuf("moe_sc", [128, 18], F32)
    eqb = p.sbuf("moe_eqb", [128, 32], F32)
    combtm = p.sbuf("moe_combtm", [128, 16], F32)
    combT = p.sbuf("moe_combT", [16, T], F32)
    crow1 = p.sbuf("moe_crow1", [1, T], BF16)
    wub = p.sbuf("moe_wub", [128, 16 * 512], BF16)
    wgb = p.sbuf("moe_wgb", [128, 16 * 512], BF16)
    wdb = p.sbuf("moe_wdb", [128, 16 * 512], BF16)
    h_sb = p.sbuf("moe_h", [128, 16 * 1024], BF16)
    sgs = [p.sbuf(f"moe_sg{i}", [128, 512], F32) for i in range(2)]
    tps = [p.sbuf(f"moe_tp{i}", [128, 512], F32) for i in range(2)]
    stg = [p.sbuf(f"moe_stg{i}", [128, 512], F32) for i in range(4)]

    pool = PsumPool(p, "moe_pp", 5)
    cps = [p.psum(f"moe_cps{i}", [128, 512], F32) for i in range(2)]
    arena = p.psum("moe_ar", [128, 512], F32)

    @p.block.gpsimd
    def _(g):
        e = g.dma_start(out=gsc_sb[0:8, :], in_=gate_w[:, :])
        p.inc(e, "gsc", 16)
        e = g.dma_start(out=gsc_sb[8:9, :], in_=se_gate_w[:, :])
        p.inc(e, "gsc", 16)
        emit_prep_cast(p, g, w_gate.rearrange("e h d -> (e h) d"), wg_bf, "pw_ug")
        emit_prep_cast(p, g, w_up.rearrange("e h d -> (e h) d"), wu_bf, "pw_ug")
        emit_prep_cast(p, g, sh_gate, shg_bf, "pw_ug")
        emit_prep_cast(p, g, sh_up, shu_bf, "pw_ug")
        emit_prep_cast(p, g, w_down.rearrange("e d h -> (e d) h"), wd_bf, "pw_d")
        emit_prep_cast(p, g, sh_down, shd_bf, "pw_d")
        for i in range(4):
            e = g.dma_start(out=xacc[i * 512:(i + 1) * 512, :], in_=xacc_in[i * 512:(i + 1) * 512, :])
            p.inc(e, "cpacc", 16)

    @p.block.sync
    def _(sy):
        emit_load_xn(p, sy, xn0, xn_sb, "xnld", tm=True)

    for kc in range(16):
        @p.block.tensor
        def _(t):
            p.wait(t, "tensor", p.tok("gsc"), p.tok("cld"))
            if kc >= 1:
                p.wait(t, "tensor", p.tok("gscev"))
            e = t.transpose(arena[:, 0:9], gsc_sb[0:9, kc * 128:(kc + 1) * 128], identf[0:9, 0:9])
            p.inc(e, "gsctr", 1)

        @p.block.vector
        def _(v):
            p.wait(v, "vector", p.tok("gsctr"))
            e = v.tensor_copy(gwT[:, kc * 9:(kc + 1) * 9], arena[0:128, 0:9] if False else arena[:, 0:9])
            p.inc(e, "gscev", 1)

    for tc in range(16):
        @p.block.tensor
        def _(t):
            p.wait(t, "tensor", p.tok("xnld"), p.tok("gscev"))
            if tc > 0:
                p.wait(t, "tensor", p.tok("ctrev"))
            e = None
            for kc in range(16):
                e = t.matmul(arena[:, 0:9], xn_sb[:, kc * T + tc * 128: kc * T + tc * 128 + 128],
                             gwT[:, kc * 9:(kc + 1) * 9], start=(kc == 0), stop=(kc == 15))
            p.inc(e, "scmm", 1)

        @p.block.vector
        def _(v):
            p.wait(v, "vector", p.tok("scmm"))
            e = v.tensor_copy(sc[:, 0:9], arena[:, 0:9])
            tk = p.inc(e, "scev", 1)
            p.wait(v, "vector", tk)
            e = v.reduce_max(sc[:, 12:13], sc[:, 0:8], axis=AX.X)
            tk = p.inc(e, "v", 1)
            p.wait(v, "vector", tk)
            e = v.tensor_scalar(eqb[:, 0:8], sc[:, 0:8], sc[:, 12:13], None, ALU.is_equal)
            tk = p.inc(e, "v", 1)
            p.wait(v, "vector", tk)
            e = v.scalar_tensor_tensor(eqb[:, 8:16], eqb[:, 0:8], float(NEG), sc[:, 0:8], ALU.mult, ALU.add)
            tk = p.inc(e, "v", 1)
            p.wait(v, "vector", tk)
            e = v.reduce_max(sc[:, 13:14], eqb[:, 8:16], axis=AX.X)
            tk = p.inc(e, "v", 1)
            p.wait(v, "vector", tk)
            e = v.tensor_scalar(eqb[:, 16:24], eqb[:, 8:16], sc[:, 13:14], None, ALU.is_equal)
            tk = p.inc(e, "v", 1)
            p.wait(v, "vector", tk)
            e = v.tensor_sub(sc[:, 14:15], sc[:, 12:13], sc[:, 13:14])
            p.inc(e, "sigin", 1)

        @p.block.scalar
        def _(s):
            p.wait(s, "scalar", p.tok("sigin"))
            e = s.activation(sc[:, 15:16], sc[:, 14:15], AF.Sigmoid)
            p.inc(e, "s", 1)
            e = s.activation(combtm[:, 8:9], sc[:, 8:9], AF.Sigmoid)
            p.inc(e, "s", 1)

        @p.block.vector
        def _(v):
            p.wait(v, "vector", p.tok("s"))
            e = v.tensor_scalar(sc[:, 16:17], sc[:, 15:16], -1.0, 1.0, ALU.mult, ALU.add)
            tk = p.inc(e, "v", 1)
            p.wait(v, "vector", tk)
            e = v.tensor_scalar(eqb[:, 24:32], eqb[:, 0:8], sc[:, 15:16], None, ALU.mult)
            tk = p.inc(e, "v", 1)
            p.wait(v, "vector", tk)
            e = v.scalar_tensor_tensor(combtm[:, 0:8], eqb[:, 16:24], sc[:, 16:17], eqb[:, 24:32], ALU.mult, ALU.add)
            p.inc(e, "combv", 1)
            if tc == 0:
                e = v.memset(combtm[:, 9:16], 0.0)
                p.inc(e, "combv", 1)

        @p.block.tensor
        def _(t):
            p.wait(t, "tensor", p.tok("combv"))
            e = t.transpose(arena[0:16, 128:256], combtm[:, 0:16], identf[:, :])
            p.inc(e, "ctr", 1)

        @p.block.vector
        def _(v):
            p.wait(v, "vector", p.tok("ctr"))
            e = v.tensor_copy(combT[:, tc * 128:(tc + 1) * 128], arena[0:16, 128:256])
            p.inc(e, "ctrev", 1)


    GROUPS = globals().get("MOE_GROUPS") or [([0, 1], False), ([2, 3], False), ([4, 5], False), ([6, 7], False), ([], True)]
    last_upgate_mm = None
    last_down_mm = None
    cps_free = [None, None]
    sg_free = [None, None]
    tp_free = [None, None]
    stg_free = [None] * 4

    for th in range(2):
        for (experts, has_sh) in GROUPS:
            units = []
            for ei, e_ in enumerate(experts):
                units.append((wg_bf, wu_bf, e_ * NH, e_, ei * 8))
            if has_sh:
                units.append((shg_bf, shu_bf, 0, 8, len(experts) * 8))
                units.append((shg_bf, shu_bf, NH, 8, len(experts) * 8 + 8))
            nj = len(units) * 8

            for (gsrc, usrc, rlo, crow_q, jbase) in units:
                @p.block.gpsimd
                def _(g):
                    p.wait(g, "gpsimd", p.tok("ctrev"))
                    if cps_free[0] is not None:
                        p.wait(g, "gpsimd", cps_free[0])
                    if cps_free[1] is not None:
                        p.wait(g, "gpsimd", cps_free[1])
                    e = g.dma_start(out=crow1[0:1, :], in_=combT[crow_q:crow_q + 1, :])
                    p.inc(e, "crowld", 16)

                @p.block.tensor
                def _(t):
                    p.wait(t, "tensor", p.tok("crowld"))
                    for tt in range(2):
                        p.wait(t, "tensor", cps_free[tt])
                        e = t.matmul(cps[tt][:, :], ones_row[:, :],
                                     crow1[0:1, th * 1024 + tt * 512: th * 1024 + tt * 512 + 512],
                                     start=True, stop=True)
                        p.inc(e, "t", 1)

                for oh in range(2):
                    @p.block.sync
                    def _(sy):
                        p.wait(sy, "sync", p.tok("pw_ug"))
                        if last_upgate_mm is not None:
                            p.wait(sy, "sync", last_upgate_mm)
                        if last_down_mm is not None:
                            p.wait(sy, "sync", last_down_mm)
                        for kc in range(16):
                            e = sy.dma_start(out=wgb[:, kc * 512:(kc + 1) * 512],
                                             in_=gsrc[rlo + oh * 512:rlo + oh * 512 + 512,
                                                      kc * 128:(kc + 1) * 128], transpose=True)
                            p.inc(e, "wug", 16)
                            e = sy.dma_start(out=wub[:, kc * 512:(kc + 1) * 512],
                                             in_=usrc[rlo + oh * 512:rlo + oh * 512 + 512,
                                                      kc * 128:(kc + 1) * 128], transpose=True)
                            p.inc(e, "wug", 16)

                    for oc4 in range(4):
                        oc = oh * 4 + oc4
                        for tt in range(2):
                            bg, gps, gfree = pool.acquire()
                            bu, ups, ufree = pool.acquire()
                            col0 = th * 1024 + tt * 512

                            @p.block.tensor
                            def _(t):
                                p.wait(t, "tensor", p.tok("wug"), gfree, ufree)
                                e = None
                                for kc in range(16):
                                    e = t.matmul(gps[:, :], wgb[:, kc * 512 + oc4 * 128: kc * 512 + oc4 * 128 + 128],
                                                 xn_sb[:, kc * T + col0: kc * T + col0 + 512],
                                                 start=(kc == 0), stop=(kc == 15))
                                tkg = p.inc(e, "t", 1)
                                for kc in range(16):
                                    e = t.matmul(ups[:, :], wub[:, kc * 512 + oc4 * 128: kc * 512 + oc4 * 128 + 128],
                                                 xn_sb[:, kc * T + col0: kc * T + col0 + 512],
                                                 start=(kc == 0), stop=(kc == 15))
                                tku = p.inc(e, "t", 1)
                                globals()["_moe_tkg"], globals()["_moe_tku"] = tkg, tku

                            tkg, tku = globals()["_moe_tkg"], globals()["_moe_tku"]

                            @p.block.scalar
                            def _(s):
                                p.wait(s, "scalar", tkg, sg_free[tt])
                                e = s.activation(sgs[tt][:], gps[:, :], AF.Silu)
                                globals()["_moe_tks"] = p.inc(e, "s", 1)

                            tks = globals()["_moe_tks"]

                            @p.block.vector
                            def _(v):
                                p.wait(v, "vector", tks, tku, tp_free[tt])
                                if last_down_mm is not None:
                                    p.wait(v, "vector", last_down_mm)
                                e = v.tensor_tensor(tps[tt][:], sgs[tt][:], ups[:, :], ALU.mult)
                                tkv = p.inc(e, "v", 1)
                                p.wait(v, "vector", tkv)
                                e = v.tensor_tensor(h_sb[:, (jbase + oc) * 1024 + tt * 512: (jbase + oc) * 1024 + tt * 512 + 512],
                                                    tps[tt][:], cps[tt][:, :], ALU.mult)
                                tkh = p.inc(e, "hwr", 1)
                                globals()["_moe_tkv"], globals()["_moe_tkh"] = tkv, tkh

                            tkv, tkh = globals()["_moe_tkv"], globals()["_moe_tkh"]
                            pool.release(bg, tks)
                            pool.release(bu, tkv)
                            sg_free[tt] = tkv
                            tp_free[tt] = tkh
                            cps_free[tt] = tkh
                    last_upgate_mm = p.tok("t")
                last_upgate_mm = p.tok("t")

            h_ready = p.tok("hwr")
            if globals().get("DBG_MOE") and th == 0 and experts[:1] == [0]:
                dbg_h0 = p.dram("dbg_h0", [128, 16 * 1024], F32, kind="ExternalOutput")

                @p.block.gpsimd
                def _(g):
                    p.wait(g, "gpsimd", h_ready)
                    e = g.dma_start(out=dbg_h0[:, :], in_=h_sb[:, :])
                    p.inc(e, "dbg", 16)

            acc_base = p.tok("acc")
            prev_og_mm = last_down_mm
            for og in range(4):
                @p.block.sync
                def _(sy):
                    p.wait(sy, "sync", p.tok("pw_d"))
                    if prev_og_mm is not None:
                        p.wait(sy, "sync", prev_og_mm)
                    jj = 0
                    for (gsrc, usrc, rlo, crow_q, jbase) in units:
                        if gsrc is wg_bf:
                            e_ = rlo // NH
                            for kc in range(8):
                                e = sy.dma_start(out=wdb[:, jj * 512:(jj + 1) * 512],
                                                 in_=wd_bf[e_ * D + og * 512: e_ * D + og * 512 + 512,
                                                           kc * 128:(kc + 1) * 128], transpose=True)
                                p.inc(e, "wd", 16)
                                jj += 1
                        else:
                            for kc in range(rlo // 128, rlo // 128 + 8):
                                e = sy.dma_start(out=wdb[:, jj * 512:(jj + 1) * 512],
                                                 in_=shd_bf[og * 512:(og + 1) * 512, kc * 128:(kc + 1) * 128],
                                                 transpose=True)
                                p.inc(e, "wd", 16)
                                jj += 1

                for ol in range(4):
                    for tt in range(2):
                        bk, ps_, free = pool.acquire()

                        @p.block.tensor
                        def _(t):
                            p.wait(t, "tensor", p.tok("wd"), free, h_ready)
                            e = None
                            for j in range(nj):
                                e = t.matmul(ps_[:, :], wdb[:, j * 512 + ol * 128: j * 512 + ol * 128 + 128],
                                             h_sb[:, j * 1024 + tt * 512: j * 1024 + tt * 512 + 512],
                                             start=(j == 0), stop=(j == nj - 1))
                            globals()["_moe_tkm"] = p.inc(e, "t", 1)

                        tkm = globals()["_moe_tkm"]
                        sb_i = (og * 8 + ol * 2 + tt) % 4

                        @p.block.scalar
                        def _(s):
                            p.wait(s, "scalar", tkm, stg_free[sb_i])
                            e = s.copy(stg[sb_i][:], ps_[:, :])
                            globals()["_moe_tke"] = p.inc(e, "dnev", 1)

                        tke = globals()["_moe_tke"]
                        pool.release(bk, tke)

                        @p.block.gpsimd
                        def _(g):
                            p.wait(g, "gpsimd", tke, p.tok("cpacc"), acc_base)
                            e = g.dma_start(out=xacc[og * 512 + ol * 128: og * 512 + ol * 128 + 128,
                                                     th * 1024 + tt * 512: th * 1024 + tt * 512 + 512],
                                            in_=stg[sb_i][:], accum_op=ALU.add)
                            stg_free[sb_i] = p.inc(e, "acc", 16)
                prev_og_mm = p.tok("t")
            last_down_mm = p.tok("t")

    if globals().get("DBG_MOE"):
        dbg_comb = p.dram("dbg_comb", [16, T], F32, kind="ExternalOutput")
        dbg_h = p.dram("dbg_h", [128, 16 * 1024], F32, kind="ExternalOutput")
        dbg_xn = p.dram("dbg_xn", [128, T], F32, kind="ExternalOutput")

        dbg_sc = p.dram("dbg_sc", [128, 18], F32, kind="ExternalOutput")
        dbg_eqb = p.dram("dbg_eqb", [128, 32], F32, kind="ExternalOutput")

        @p.block.gpsimd
        def _(g):
            p.final_wait(g, "gpsimd")
            e = g.dma_start(out=dbg_comb[:, :], in_=combT[:, :])
            p.inc(e, "dbg", 16)
            e = g.dma_start(out=dbg_xn[:, :], in_=xn_sb[:, 0:T])
            p.inc(e, "dbg", 16)
            e = g.dma_start(out=dbg_sc[:, :], in_=sc[:, :])
            p.inc(e, "dbg", 16)
            e = g.dma_start(out=dbg_eqb[:, :], in_=eqb[:, :])
            p.inc(e, "dbg", 16)

    @p.block.sync
    def _(sy):
        p.final_wait(sy, "sync")


DBG_MOE = False


def build_l0_moe():
    p = Prog("l0moe")
    with p.es:
        csb, cemit = load_consts(p, ["identf", "ones_row"])
        with p.nc.Block() as blk:
            p.block = blk

            @blk.gpsimd
            def _(g):
                cemit(g)
            emit_l0_moe(p, csb)
    return p


def _chain(p, steps):
    prev = None
    first = True
    for engname, fn, extra in steps:
        sec = getattr(p.block, engname)
        box = {}
        bar = [p.tok("cht"), p.tok("chv"), p.tok("chs")] if first else []
        first = False

        def body(eng, fn=fn, extra=extra, prevtok=prev, engname=engname, box=box, bar=bar):
            p.wait(eng, engname, prevtok, *(list(extra) + bar))
            inst = fn(eng)
            if inst is not None:
                n = 16 if engname in ("gpsimd", "sync") else 1
                box["tok"] = p.inc(inst, "ch" + engname[0], n)
            else:
                box["tok"] = prevtok
        sec(body)
        prev = box["tok"]
    return prev


def emit_l0_attn(p, csb):
    NK, DK, DV, PP, KC = 16, 128, 128, 2, 4
    QD = NK * DK
    VD = 2 * NK * DV
    xn0 = p.dram("xn0_in", [T, D], BF16, kind="ExternalInput")
    xacc_in = p.dram("xacc_in", [D, T], F32, kind="ExternalInput")
    xacc = p.dram("xacc", [D, T], F32, kind="ExternalOutput")
    W_qkv = p.dram("W_qkv", [2 * QD + VD, D], F32, kind="ExternalInput")
    W_z = p.dram("W_z", [VD, D], F32, kind="ExternalInput")
    W_b = p.dram("W_b", [32, D], F32, kind="ExternalInput")
    W_a = p.dram("W_a", [32, D], F32, kind="ExternalInput")
    conv_w = p.dram("conv_w", [2 * QD + VD, KC], F32, kind="ExternalInput")
    dt_bias = p.dram("dt_bias", [32], F32, kind="ExternalInput")
    A_log = p.dram("A_log", [32], F32, kind="ExternalInput")
    norm_w = p.dram("norm_w", [128], F32, kind="ExternalInput")
    W_out = p.dram("W_out", [D, VD], F32, kind="ExternalInput")

    wqkv_bf = p.dram("wqkv_bf", [2 * QD + VD, D], BF16)
    wz_bf = p.dram("wz_bf", [VD, D], BF16)
    wout_bf = p.dram("wout_bf", [D, VD], BF16)

    identf = csb["identf"]
    identb = csb["identb"]
    ones_row = csb["ones_row"]
    onesf_row = csb["onesf_row"]
    onescol_b = csb["onescol_b"]
    m_ui = csb["m_upincl_neg"]
    m_us = csb["m_upstrict_neg"]
    m_ls = csb["m_lowstrict_pos"]
    twoI = csb["twoI"]

    for v_ in (1e-6, 1.0 / 128, -0.5, -1.0, 1.0):
        p.cscalar(v_)

    xn_sb = p.sbuf("da_xn", [128, 16 * T], BF16)
    baT = p.sbuf("da_baT", [128, 16 * 64], BF16)
    qrow1 = p.sbuf("da_qrow1", [128, T], F32)
    qrow2 = p.sbuf("da_qrow2", [64, T], F32)
    cols_all = p.sbuf("da_cols", [128, 16 * 192], F32)
    rows_buf = p.sbuf("da_rows", [1, 1024], F32)
    cw_sb = p.sbuf("da_cw", [128, 64 * 4], F32)
    misc = p.sbuf("da_misc", [128, 16], F32)
    nw_rep = p.sbuf("da_nw", [128, 128], F32)
    nw_row = p.sbuf("da_nwrow", [1, 128], F32)
    wq_l = p.sbuf("da_wq", [128, 16 * 128], BF16)
    wk_l = p.sbuf("da_wk", [128, 16 * 128], BF16)
    wv_l = p.sbuf("da_wv", [128, 16 * 256], BF16)
    wz_l = p.sbuf("da_wz", [128, 16 * 256], BF16)
    qraw = p.sbuf("da_qraw", [128, T + 4], BF16)
    qc = p.sbuf("da_qc", [128, T], BF16)
    kc_ = p.sbuf("da_kc", [128, T], BF16)
    vc = [p.sbuf(f"da_vc{i}", [128, T], BF16) for i in range(2)]
    ssrow = p.sbuf("da_ssrow", [1, T], F32)
    zsil = p.sbuf("da_zsil", [128, 2 * T], BF16)
    basb = zsil[0:64, 0:4096].bitcast(F32)
    S_bf = p.sbuf("da_S", [128, 32 * 128], BF16)
    g_sb = p.sbuf("da_g", [128, 4 * T], BF16)
    gv = g_sb[0:32, :].bitcast(F32)
    wodb = p.sbuf("da_wod", [128, 4 * 512], BF16)
    stg = [p.sbuf(f"da_stg{i}", [128, 512], F32) for i in range(2)]
    KK_sb = p.sbuf("da_KK", [128, 128], F32)
    KQ_sb = p.sbuf("da_KQ", [128, 128], F32)
    KT_sb = p.sbuf("da_KT", [128, 128], BF16)
    VT_sb = [p.sbuf(f"da_VT{i}", [128, 128], BF16) for i in range(2)]
    Gt = p.sbuf("da_Gt", [128, 128], F32)
    Et = p.sbuf("da_Et", [128, 128], F32)
    Et2 = p.sbuf("da_Et2", [128, 128], F32)
    MT_sb = p.sbuf("da_MT", [128, 128], BF16)
    nTT = p.sbuf("da_nTT", [128, 128], F32)
    X_sb = p.sbuf("da_X", [128, 128], BF16)
    XT_sb = p.sbuf("da_XT", [128, 128], BF16)
    W2_sb = p.sbuf("da_W2", [128, 128], BF16)
    R_sb = p.sbuf("da_R", [128, 128], BF16)
    U_sb = p.sbuf("da_U", [128, 128], BF16)
    U2_sb = p.sbuf("da_U2", [128, 128], BF16)
    t4_sb = p.sbuf("da_t4", [128, 128], F32)
    O_sb = p.sbuf("da_O", [128, 128], F32)
    On_sb = p.sbuf("da_On", [128, 128], BF16)
    osq = p.sbuf("da_osq", [128, 128], BF16)

    pool = PsumPool(p, "da_pp", 4)
    bk0 = p.psum("da_bk0", [128, 512], F32)
    bk1 = p.psum("da_bk1", [128, 512], F32)
    bk2 = p.psum("da_bk2", [128, 512], F32)
    bk3 = p.psum("da_bk3", [128, 512], F32)

    @p.block.gpsimd
    def _(g):
        emit_prep_cast(p, g, W_qkv, wqkv_bf, "pw_qkv")
        emit_prep_cast(p, g, W_z, wz_bf, "pw_z")
        emit_prep_cast(p, g, W_out, wout_bf, "pw_o")
        e = g.dma_start(out=basb[0:32, :], in_=W_b[:, :])
        p.inc(e, "mld", 16)
        e = g.dma_start(out=basb[32:64, :], in_=W_a[:, :])
        p.inc(e, "mld", 16)
        e = g.dma_start(out=cw_sb[:, :].rearrange("p (c j) -> p c j", j=4), in_=conv_w.rearrange("(c q) j -> q c j", q=128))
        p.inc(e, "mld", 16)
        e = g.dma_start(out=misc[0:32, 0:1], in_=dt_bias[:, None])
        p.inc(e, "mld", 16)
        e = g.dma_start(out=misc[0:32, 1:2], in_=A_log[:, None])
        p.inc(e, "mld", 16)
        e = g.dma_start(out=nw_row[0:1, :], in_=norm_w[None, :])
        p.inc(e, "mld", 16)
        for i in range(4):
            e = g.dma_start(out=xacc[i * 512:(i + 1) * 512, :], in_=xacc_in[i * 512:(i + 1) * 512, :])
            p.inc(e, "cpacc", 16)
        e = g.memset(qraw[:, 0:4], 0.0)
        p.inc(e, "mld", 1)

    @p.block.sync
    def _(sy):
        emit_load_xn(p, sy, xn0, xn_sb, "xnld", tm=True)

    _chain(p, [
        ("tensor", lambda t: t.matmul(bk0[:, 0:128], onesf_row[:, :], nw_row[0:1, :], start=True, stop=True), [p.tok("mld"), p.tok("cld")]),
        ("vector", lambda v: v.tensor_copy(nw_rep[:, :], bk0[:, 0:128]), []),
    ])

    for kcc in range(16):
        _chain(p, [
            ("tensor", lambda t, kcc=kcc: t.transpose(bk0.bitcast(BF16)[:, 512:512 + 64] if False else bk0[:, 128:192], basb[0:64, kcc * 128:(kcc + 1) * 128], identf[0:64, 0:64]), [p.tok("mld"), p.tok("cld")]),
            ("vector", lambda v, kcc=kcc: v.tensor_copy(baT[:, kcc * 64:(kcc + 1) * 64], bk0[:, 128:192]), []),
        ])

    _regions = [gv[:, 0:T], gv[:, T:2 * T],
                zsil[0:32, :].bitcast(F32), S_bf[0:32, :].bitcast(F32)]

    def Ar(s0, s1=None):
        r = _regions[s0]
        return r if s1 is None else r[:, s1[0]:s1[1]]
    for tt in range(4):
        bnk, ps_, free = pool.acquire()

        def mmfn(t, tt=tt, ps_=ps_, free=free):
            p.wait(t, "tensor", free, p.tok("xnld"))
            e = None
            for kcc in range(16):
                e = t.matmul(ps_[0:64, 0:512], baT[:, kcc * 64:(kcc + 1) * 64],
                             xn_sb[:, kcc * T + tt * 512: kcc * T + tt * 512 + 512],
                             start=(kcc == 0), stop=(kcc == 15))
            return e
        tk = _chain(p, [
            ("tensor", mmfn, []),
            ("vector", lambda v, tt=tt, ps_=ps_: v.tensor_copy(Ar(0, (tt * 512, (tt + 1) * 512)), ps_[0:32, 0:512]), []),
            ("vector", lambda v, tt=tt, ps_=ps_: v.tensor_copy(Ar(1, (tt * 512, (tt + 1) * 512)), ps_[32:64, 0:512]), []),
        ])
        pool.release(bnk, tk)

    _chain(p, [
        ("scalar", lambda s: s.activation(Ar(3), Ar(0), AF.Sigmoid), []),
        ("vector", lambda v: v.tensor_scalar_mul(qrow1[64:96, :], Ar(3), -1.0), []),
        ("scalar", lambda s: s.activation(Ar(2), Ar(1), AF.Abs, bias=misc[0:32, 0:1]), []),
        ("scalar", lambda s: s.activation(Ar(2), Ar(2), AF.Exp, scale=-1.0), []),
        ("scalar", lambda s: s.activation(Ar(2), Ar(2), AF.Ln, bias=1.0), []),
        ("vector", lambda v: v.tensor_scalar(Ar(3), Ar(1), misc[0:32, 0:1], 0.0, ALU.add, ALU.max), []),
        ("vector", lambda v: v.tensor_add(Ar(1), Ar(2), Ar(3)), []),
        ("scalar", lambda s: s.activation(misc[0:32, 2:3], misc[0:32, 1:2], AF.Exp), []),
        ("vector", lambda v: v.tensor_scalar_mul(misc[0:32, 3:4], misc[0:32, 2:3], -1.0), []),
        ("vector", lambda v: v.tensor_scalar(Ar(2), Ar(1), misc[0:32, 3:4], None, ALU.mult), []),
    ])
    for c in range(16):
        sl = slice(c * 128, (c + 1) * 128)
        _chain(p, [
            ("vector", lambda v, sl=sl: v.tensor_tensor_scan(qrow1[0:32, sl], Ar(2)[:, sl], Ar(2)[:, sl], 0.0, ALU.add, ALU.bypass), []),
            ("vector", lambda v, sl=sl, c=c: v.tensor_scalar(Ar(3)[:, sl], qrow1[0:32, sl], qrow1[0:32, c * 128 + 127:c * 128 + 128], None, ALU.subtract), []),
            ("scalar", lambda s, sl=sl: s.activation(qrow2[32:64, sl], Ar(3)[:, sl], AF.Exp, scale=-1.0), []),
        ])
    _chain(p, [
        ("scalar", lambda s: s.activation(qrow2[0:32, :], qrow1[0:32, :], AF.Exp), []),
    ])
    for c in range(16):
        _chain(p, [
            ("vector", lambda v, c=c: v.memset(qrow1[32:64, c * 128:c * 128 + 1], 0.0), []),
            ("vector", lambda v, c=c: v.tensor_copy(qrow1[32:64, c * 128 + 1:(c + 1) * 128], qrow1[0:32, c * 128:(c + 1) * 128 - 1]), []),
            ("vector", lambda v, c=c: v.memset(qrow1[96:128, c * 128:c * 128 + 1], 1.0), []),
            ("vector", lambda v, c=c: v.tensor_copy(qrow1[96:128, c * 128 + 1:(c + 1) * 128], qrow2[0:32, c * 128:(c + 1) * 128 - 1]), []),
        ])
    for c in range(16):
        _chain(p, [
            ("tensor", lambda t, c=c: t.transpose(bk0[:, 128:256], qrow1[:, c * 128:(c + 1) * 128], identf[:, :]), []),
            ("vector", lambda v, c=c: v.tensor_copy(cols_all[:, c * 192: c * 192 + 128], bk0[:, 128:256]), []),
            ("tensor", lambda t, c=c: t.transpose(bk0[:, 256:320], qrow2[0:64, c * 128:(c + 1) * 128], identf[0:64, 0:64]), []),
            ("vector", lambda v, c=c: v.tensor_copy(cols_all[:, c * 192 + 128: c * 192 + 192], bk0[:, 256:320]), []),
        ])

    _chain(p, [("vector", lambda v: v.memset(S_bf[:, :], 0.0), [])])

    def col(q, vh, c):
        off = {"lA": 0, "lB": 32, "negb": 64, "elB": 96, "elA": 128, "w": 160}[q]
        return cols_all[:, c * 192 + off + vh: c * 192 + off + vh + 1]

    last_inproj_mm = None
    acc_base = p.tok("acc")
    for h in range(NK):
        hg = h % 2

        @p.block.sync
        def _(sy):
            p.wait(sy, "sync", p.tok("pw_qkv"), p.tok("pw_z"))
            if last_inproj_mm is not None:
                p.wait(sy, "sync", last_inproj_mm)
            for kcc in range(16):
                e = sy.dma_start(out=wq_l[:, kcc * 128:(kcc + 1) * 128],
                                 in_=wqkv_bf[h * 128:(h + 1) * 128, kcc * 128:(kcc + 1) * 128], transpose=True)
                p.inc(e, "wld", 16)
                e = sy.dma_start(out=wk_l[:, kcc * 128:(kcc + 1) * 128],
                                 in_=wqkv_bf[QD + h * 128: QD + (h + 1) * 128, kcc * 128:(kcc + 1) * 128], transpose=True)
                p.inc(e, "wld", 16)
                e = sy.dma_start(out=wv_l[:, kcc * 256:(kcc + 1) * 256],
                                 in_=wqkv_bf[2 * QD + h * 256: 2 * QD + (h + 1) * 256, kcc * 128:(kcc + 1) * 128], transpose=True)
                p.inc(e, "wld", 16)
                e = sy.dma_start(out=wz_l[:, kcc * 256:(kcc + 1) * 256],
                                 in_=wz_bf[h * 256:(h + 1) * 256, kcc * 128:(kcc + 1) * 128], transpose=True)
                p.inc(e, "wld", 16)

        parts = [(qc, wq_l, 0, 128, h), (kc_, wk_l, 0, 128, 16 + h),
                 (vc[0], wv_l, 0, 256, 32 + 2 * h), (vc[1], wv_l, 128, 256, 33 + 2 * h)]
        for (dst, wbuf, coff, wstride, cc) in parts:
            for tt in range(4):
                bnk, ps_, free = pool.acquire()

                def mmfn(t, tt=tt, ps_=ps_, free=free, wbuf=wbuf, coff=coff, wstride=wstride):
                    p.wait(t, "tensor", free, p.tok("wld"), p.tok("xnld"), p.tok("mld"))
                    e = None
                    for kcc in range(16):
                        e = t.matmul(ps_[:, :], wbuf[:, kcc * wstride + coff: kcc * wstride + coff + 128],
                                     xn_sb[:, kcc * T + tt * 512: kcc * T + tt * 512 + 512],
                                     start=(kcc == 0), stop=(kcc == 15))
                    return e
                tk = _chain(p, [
                    ("tensor", mmfn, []),
                    ("vector", lambda v, tt=tt, ps_=ps_: v.tensor_copy(qraw[:, 4 + tt * 512: 4 + (tt + 1) * 512], ps_[:, :]), []),
                ])
                pool.release(bnk, tk)
            steps = [("vector", lambda v, cc=cc: v.tensor_scalar(dst[:, :], qraw[:, 1:1 + T], cw_sb[:, cc * 4 + 0: cc * 4 + 1], None, ALU.mult), [])]
            for j in range(1, 4):
                steps.append(("vector", lambda v, j=j, cc=cc, dst=dst: v.scalar_tensor_tensor(
                    dst[:, :], qraw[:, 1 + j:1 + j + T], cw_sb[:, cc * 4 + j: cc * 4 + j + 1], dst[:, :], ALU.mult, ALU.add), []))
            _chain(p, steps)

        for pi, buf in enumerate((qc, kc_)):
            _chain(p, [("scalar", lambda s, buf=buf: s.activation(qraw[:, 4:4 + T], buf[:, :], AF.Square), [])])
            for tt in range(4):
                _chain(p, [
                    ("tensor", lambda t, tt=tt: t.matmul(bk0[0:1, 0:512], onescol_b[:, :], qraw[:, 4 + tt * 512:4 + (tt + 1) * 512], start=True, stop=True), []),
                    ("vector", lambda v, tt=tt, pi=pi: v.tensor_copy(ssrow[0:1, tt * 512:(tt + 1) * 512], bk0[0:1, 0:512]), []),
                ])
            _chain(p, [
                ("scalar", lambda s, pi=pi: s.activation(ssrow[0:1, :], ssrow[0:1, :], AF.Sqrt), []),
                ("vector", lambda v, pi=pi: v.tensor_scalar_max(ssrow[0:1, :], ssrow[0:1, :], 1e-12), []),
                ("vector", lambda v, pi=pi: v.reciprocal(ssrow[0:1, :], ssrow[0:1, :]), []),
            ])
            for tt in range(4):
                _chain(p, [
                    ("tensor", lambda t, tt=tt, pi=pi: t.matmul(bk1[:, 0:512], onesf_row[:, :], ssrow[0:1, tt * 512:(tt + 1) * 512], start=True, stop=True), []),
                    ("vector", lambda v, tt=tt, buf=buf: v.tensor_tensor(buf[:, tt * 512:(tt + 1) * 512], buf[:, tt * 512:(tt + 1) * 512], bk1[:, 0:512], ALU.mult), []),
                ])

        for zo in range(2):
            for tt in range(4):
                bnk, ps_, free = pool.acquire()

                def mmfn(t, tt=tt, ps_=ps_, free=free, zo=zo):
                    p.wait(t, "tensor", free)
                    e = None
                    for kcc in range(16):
                        e = t.matmul(ps_[:, :], wz_l[:, kcc * 256 + zo * 128: kcc * 256 + zo * 128 + 128],
                                     xn_sb[:, kcc * T + tt * 512: kcc * T + tt * 512 + 512],
                                     start=(kcc == 0), stop=(kcc == 15))
                    return e
                tk = _chain(p, [
                    ("tensor", mmfn, []),
                    ("scalar", lambda s, tt=tt, ps_=ps_, zo=zo: s.activation(
                        zsil[:, zo * T + tt * 512: zo * T + (tt + 1) * 512], ps_[:, :], AF.Silu), []),
                ])
                pool.release(bnk, tk)
        last_inproj_mm = p.tok("cht")

        for c in range(16):
            csl = slice(c * 128, (c + 1) * 128)
            _chain(p, [
                ("tensor", lambda t: t.matmul(bk0[:, 0:128], kc_[:, csl], kc_[:, csl], start=True, stop=True), []),
                ("vector", lambda v: v.tensor_copy(KK_sb[:, :], bk0[:, 0:128]), []),
                ("tensor", lambda t: t.matmul(bk0[:, 128:256], kc_[:, csl], qc[:, csl], start=True, stop=True), []),
                ("vector", lambda v: v.tensor_copy(KQ_sb[:, :], bk0[:, 128:256]), []),
                ("tensor", lambda t: t.transpose(bk0.bitcast(BF16)[:, 512:640], kc_[:, csl], identb[:, :]), []),
                ("vector", lambda v: v.tensor_copy(KT_sb[:, :], bk0.bitcast(BF16)[:, 512:640]), []),
                ("tensor", lambda t: t.transpose(bk0.bitcast(BF16)[:, 640:768], vc[0][:, csl], identb[:, :]), []),
                ("vector", lambda v: v.tensor_copy(VT_sb[0][:, :], bk0.bitcast(BF16)[:, 640:768]), []),
                ("tensor", lambda t: t.transpose(bk0.bitcast(BF16)[:, 768:896], vc[1][:, csl], identb[:, :]), []),
                ("vector", lambda v: v.tensor_copy(VT_sb[1][:, :], bk0.bitcast(BF16)[:, 768:896]), []),
            ])
            for pp_ in range(2):
                vh = 2 * h + pp_

                @p.block.gpsimd
                def _(g, vh=vh, pp_=pp_, c=c):
                    p.wait(g, "gpsimd", p.tok("chv"), p.tok("chs"), p.tok("cht"))
                    e = g.dma_start(out=rows_buf[0:1, pp_ * 512:(pp_ + 1) * 512],
                                    in_=qrow1[vh:vh + 97:32, c * 128:(c + 1) * 128])
                    p.inc(e, "rowld", 16)

            for pp_ in range(2):
                vh = 2 * h + pp_
                ro = pp_ * 512
                lA_c, lB_c, nb_c, elB_c = col("lA", vh, c), col("lB", vh, c), col("negb", vh, c), col("elB", vh, c)
                elA_c, w_c = col("elA", vh, c), col("w", vh, c)
                Ssl = S_bf[:, vh * 128:(vh + 1) * 128]

                def step(engname, fn, extra=()):
                    return (engname, fn, list(extra))
                _chain(p, [
                    step("tensor", lambda t: t.matmul(bk1[:, 0:128], onesf_row[:, :], rows_buf[0:1, ro + 0:ro + 128], start=True, stop=True), [p.tok("rowld")]),
                    step("tensor", lambda t: t.matmul(bk1[:, 128:256], onesf_row[:, :], rows_buf[0:1, ro + 128:ro + 256], start=True, stop=True)),
                    step("tensor", lambda t: t.matmul(bk1[:, 256:384], onesf_row[:, :], rows_buf[0:1, ro + 256:ro + 384], start=True, stop=True)),
                    step("vector", lambda v: v.scalar_tensor_tensor(Gt[:, :], bk1[:, 0:128], lA_c, m_ui[:, :], ALU.subtract, ALU.add)),
                    step("scalar", lambda s: s.activation(Et[:, :], Gt[:, :], AF.Exp)),
                    step("vector", lambda v: v.tensor_tensor(MT_sb[:, :], Et[:, :], KQ_sb[:, :], ALU.mult)),
                    step("vector", lambda v: v.scalar_tensor_tensor(Gt[:, :], bk1[:, 128:256], lA_c, m_us[:, :], ALU.subtract, ALU.add)),
                    step("scalar", lambda s: s.activation(Et[:, :], Gt[:, :], AF.Exp)),
                    step("vector", lambda v: v.tensor_tensor(Et2[:, :], Et[:, :], KK_sb[:, :], ALU.mult)),
                    step("vector", lambda v: v.tensor_tensor(nTT[:, :], Et2[:, :], bk1[:, 256:384], ALU.mult)),
                    step("vector", lambda v: v.tensor_add(XT_sb[:, :], identf[:, :], nTT[:, :])),
                    step("vector", lambda v: v.tensor_sub(R_sb[:, :], identf[:, :], nTT[:, :])),
                    step("vector", lambda v: v.scalar_tensor_tensor(Gt[:, :], bk1[:, 0:128], lB_c, m_ls[:, :], ALU.subtract, ALU.add)),
                    step("scalar", lambda s: s.activation(Et[:, :], Gt[:, :], AF.Exp, scale=-1.0)),
                    step("vector", lambda v: v.tensor_mul(Et[:, :], Et[:, :], KK_sb[:, :])),
                    step("vector", lambda v: v.scalar_tensor_tensor(X_sb[:, :], Et[:, :], nb_c, identf[:, :], ALU.mult, ALU.add)),
                ])
                for it in range(6):
                    _chain(p, [
                        step("tensor", lambda t: t.matmul(bk2[:, 0:128], R_sb[:, :], X_sb[:, :], start=True, stop=True)),
                        step("vector", lambda v: v.scalar_tensor_tensor(W2_sb[:, :], bk2[:, 0:128], -1.0, twoI[:, :], ALU.mult, ALU.add)),
                        step("tensor", lambda t: t.matmul(bk2[:, 128:256], XT_sb[:, :], W2_sb[:, :], start=True, stop=True)),
                        step("tensor", lambda t: t.matmul(bk2[:, 256:384], W2_sb[:, :], XT_sb[:, :], start=True, stop=True)),
                        step("vector", lambda v: v.tensor_copy(X_sb[:, :], bk2[:, 128:256])),
                        step("scalar", lambda s: s.copy(XT_sb[:, :], bk2[:, 256:384])),
                    ])
                _chain(p, [
                    step("tensor", lambda t: t.matmul(bk3[:, 0:128], kc_[:, csl], Ssl, start=True, stop=True)),
                    step("vector", lambda v: v.scalar_tensor_tensor(Gt[:, :], bk3[:, 0:128], elB_c, VT_sb[pp_][:, :], ALU.mult, ALU.subtract)),
                    step("vector", lambda v: v.tensor_scalar(R_sb[:, :], Gt[:, :], nb_c, None, ALU.mult)),
                    step("tensor", lambda t: t.matmul(bk2[:, 384:512], XT_sb[:, :], R_sb[:, :], start=True, stop=True)),
                    step("vector", lambda v: v.tensor_copy(U_sb[:, :], bk2[:, 384:512])),
                    step("vector", lambda v: v.tensor_scalar(U2_sb[:, :], bk2[:, 384:512], w_c, None, ALU.mult)),
                    step("tensor", lambda t: t.matmul(bk3[:, 128:256], qc[:, csl], Ssl, start=True, stop=True)),
                    step("vector", lambda v: v.tensor_scalar(t4_sb[:, :], bk3[:, 128:256], elA_c, None, ALU.mult)),
                    step("tensor", lambda t: t.matmul(bk3[:, 256:384], KT_sb[:, :], U2_sb[:, :], start=True, stop=True)),
                    step("vector", lambda v: v.tensor_scalar(misc[:, 4:5], w_c, elA_c, None, ALU.mult)),
                    step("vector", lambda v: v.scalar_tensor_tensor(Ssl, Ssl, misc[:, 4:5], bk3[:, 256:384], ALU.mult, ALU.add)),
                    step("tensor", lambda t: t.matmul(bk3[:, 384:512], MT_sb[:, :], U_sb[:, :], start=True, stop=True)),
                    step("vector", lambda v: v.tensor_add(O_sb[:, :], bk3[:, 384:512], t4_sb[:, :])),
                    step("scalar", lambda s: s.activation(osq[:, :], O_sb[:, :], AF.Square, accum_out=misc[:, 5:6])),
                    step("scalar", lambda s: s.activation(misc[:, 6:7], misc[:, 5:6], AF.Ln, scale=1.0 / 128, bias=1e-6)),
                    step("scalar", lambda s: s.activation(misc[:, 7:8], misc[:, 6:7], AF.Exp, scale=-0.5)),
                    step("vector", lambda v: v.scalar_tensor_tensor(On_sb[:, :], O_sb[:, :], misc[:, 7:8], nw_rep[:, :], ALU.mult, ALU.mult)),
                    step("tensor", lambda t: t.transpose(bk3.bitcast(BF16)[:, 0:128], On_sb[:, :], identb[:, :])),
                    step("vector", lambda v: v.tensor_tensor(g_sb[:, (hg * 2 + pp_) * T + c * 128: (hg * 2 + pp_) * T + (c + 1) * 128],
                                                             bk3.bitcast(BF16)[:, 0:128], zsil[:, pp_ * T + c * 128: pp_ * T + (c + 1) * 128], ALU.mult)),
                ])

        if hg == 1:
            h0 = h - 1
            g_ready = p.tok("chv")
            prev_og_mm = None
            for og in range(4):
                @p.block.sync
                def _(sy, og=og, h0=h0):
                    p.wait(sy, "sync", p.tok("pw_o"))
                    if prev_og_mm is not None:
                        p.wait(sy, "sync", prev_og_mm)
                    for j in range(4):
                        vhg = 2 * h0 + j
                        e = sy.dma_start(out=wodb[:, j * 512:(j + 1) * 512],
                                         in_=wout_bf[og * 512:(og + 1) * 512, vhg * 128:(vhg + 1) * 128], transpose=True)
                        p.inc(e, "wod", 16)

                for ol in range(4):
                    for tt in range(4):
                        bnk, ps_, free = pool.acquire()

                        def mmfn(t, ps_=ps_, free=free, ol=ol, tt=tt, g_ready=g_ready):
                            p.wait(t, "tensor", p.tok("wod"), free, g_ready)
                            e = None
                            for j in range(4):
                                e = t.matmul(ps_[:, :], wodb[:, j * 512 + ol * 128: j * 512 + ol * 128 + 128],
                                             g_sb[:, j * T + tt * 512: j * T + (tt + 1) * 512],
                                             start=(j == 0), stop=(j == 3))
                            return e
                        sb_i = (ol * 4 + tt) % 2

                        def evfn(s, sb_i=sb_i, ps_=ps_):
                            p.wait(s, "scalar", stg_free_da[sb_i])
                            return s.copy(stg[sb_i][:], ps_[:, :])
                        tke = _chain(p, [("tensor", mmfn, []), ("scalar", evfn, [])])
                        pool.release(bnk, tke)

                        @p.block.gpsimd
                        def _(g, og=og, ol=ol, tt=tt, sb_i=sb_i, tke=tke):
                            p.wait(g, "gpsimd", tke, p.tok("cpacc"), acc_base)
                            e = g.dma_start(out=xacc[og * 512 + ol * 128: og * 512 + ol * 128 + 128, tt * 512:(tt + 1) * 512],
                                            in_=stg[sb_i][:], accum_op=ALU.add)
                            stg_free_da[sb_i] = p.inc(e, "acc", 16)
                prev_og_mm = p.tok("cht")

    @p.block.sync
    def _(sy):
        p.final_wait(sy, "sync")


stg_free_da = [None, None]


def build_l0_attn():
    global stg_free_da
    stg_free_da = [None, None]
    p = Prog("l0attn")
    with p.es:
        csb, cemit = load_consts(p, ["identf", "identb", "ones_row", "onesf_row", "onescol_b",
                                     "m_upincl_neg", "m_upstrict_neg", "m_lowstrict_pos", "twoI"])
        with p.nc.Block() as blk:
            p.block = blk

            @blk.gpsimd
            def _(g):
                cemit(g)
            emit_l0_attn(p, csb)
    return p


def emit_ln_fm(p, csb):
    xacc_in = p.dram("xacc_in", [D, T], F32, kind="ExternalInput")
    xn1 = p.dram("xn1", [D, T], BF16, kind="ExternalOutput")
    onesf_row = csb["onesf_row"]
    onescol_b = csb["onescol_b"]
    p.cscalar(1e-5)
    p.cscalar(-0.5)

    xbf = p.sbuf("lf_xbf", [128, 16 * T], BF16)
    xf32 = [p.sbuf(f"lf_x{i}", [128, T], F32) for i in range(2)]
    sq = p.sbuf("lf_sq", [128, 512], BF16)
    rows = p.sbuf("lf_rows", [1, 2 * T], F32)
    rscr = p.sbuf("lf_rscr", [1, 2 * 512], F32)
    tscr = p.sbuf("lf_t", [128, 512], F32)
    xnb = [p.sbuf(f"lf_xn{i}", [128, 512], BF16) for i in range(2)]
    pst = [p.psum(f"lf_pst{i}", [1, 512], F32) for i in range(2)]
    prep = [p.psum(f"lf_prep{i}", [128, 512], F32) for i in range(2)]

    for dc in range(16):
        b = dc % 2

        @p.block.gpsimd
        def _(g, dc=dc, b=b):
            p.wait_snap(g, "gpsimd", dc - 2, ["chv"])
            e = g.dma_start(out=xf32[b][:], in_=xacc_in[dc * 128:(dc + 1) * 128, :])
            p.inc(e, f"lfld{b}", 16)
        _chain(p, [
            ("vector", lambda v, dc=dc, b=b: v.tensor_copy(xbf[:, dc * T:(dc + 1) * T], xf32[b][:]), [p.tok(f"lfld{b}")]),
        ])
        p.snapshot(dc)

    for tt in range(4):
        for dc in range(16):
            _chain(p, [
                ("scalar", lambda s, dc=dc, tt=tt: s.activation(sq[:, :], xbf[:, dc * T + tt * 512: dc * T + tt * 512 + 512], AF.Square), []),
                ("tensor", lambda t, dc=dc, tt=tt: t.matmul(pst[0][0:1, :], onescol_b[:, :],
                                                            xbf[:, dc * T + tt * 512: dc * T + tt * 512 + 512],
                                                            start=(dc == 0), stop=(dc == 15)), []),
                ("tensor", lambda t, dc=dc: t.matmul(pst[1][0:1, :], onescol_b[:, :], sq[:, :],
                                                     start=(dc == 0), stop=(dc == 15)), []),
            ])
        _chain(p, [
            ("vector", lambda v, tt=tt: v.tensor_scalar_mul(rows[0:1, tt * 512:(tt + 1) * 512], pst[0][0:1, :], -1.0 / D), []),
            ("vector", lambda v, tt=tt: v.tensor_scalar_mul(rscr[0:1, 0:512], pst[1][0:1, :], 1.0 / D), []),
            ("vector", lambda v, tt=tt: v.tensor_mul(rscr[0:1, 512:1024], rows[0:1, tt * 512:(tt + 1) * 512], rows[0:1, tt * 512:(tt + 1) * 512]), []),
            ("vector", lambda v: v.tensor_sub(rscr[0:1, 0:512], rscr[0:1, 0:512], rscr[0:1, 512:1024]), []),
            ("scalar", lambda s: s.activation(rscr[0:1, 0:512], rscr[0:1, 0:512], AF.Ln, bias=1e-5), []),
            ("scalar", lambda s, tt=tt: s.activation(rows[0:1, T + tt * 512: T + (tt + 1) * 512], rscr[0:1, 0:512], AF.Exp, scale=-0.5), []),
        ])
        _chain(p, [
            ("tensor", lambda t, tt=tt: t.matmul(prep[0][:, :], onesf_row[:, :], rows[0:1, tt * 512:(tt + 1) * 512], start=True, stop=True), []),
            ("tensor", lambda t, tt=tt: t.matmul(prep[1][:, :], onesf_row[:, :], rows[0:1, T + tt * 512: T + (tt + 1) * 512], start=True, stop=True), []),
        ])
        for dc in range(16):
            b = dc % 2

            def xfn1(v, dc=dc, tt=tt, b=b):
                p.wait(v, "vector", p.tok(f"lfst{b}"))
                return v.tensor_tensor(tscr[:, :], xbf[:, dc * T + tt * 512: dc * T + tt * 512 + 512], prep[0][:, :], ALU.add)
            tk = _chain(p, [
                ("vector", xfn1, []),
                ("vector", lambda v, b=b: v.tensor_tensor(xnb[b][:, :], tscr[:, :], prep[1][:, :], ALU.mult), []),
            ])

            @p.block.gpsimd
            def _(g, dc=dc, tt=tt, b=b, tk=tk):
                p.wait(g, "gpsimd", tk)
                e = g.dma_start(out=xn1[dc * 128:(dc + 1) * 128, tt * 512:(tt + 1) * 512], in_=xnb[b][:, :])
                p.inc(e, f"lfst{b}", 16)

    @p.block.sync
    def _(sy):
        p.final_wait(sy, "sync")


def build_ln_fm():
    p = Prog("lnfm")
    with p.es:
        csb, cemit = load_consts(p, ["onesf_row", "onescol_b"])
        with p.nc.Block() as blk:
            p.block = blk

            @blk.gpsimd
            def _(g):
                cemit(g)
            emit_ln_fm(p, csb)
    return p


def emit_l1_attn(p, csb):
    DI, DS, NH, HD = 4096, 128, 64, 64
    CVD = DI + 2 * DS
    DIP = 2 * DI + 2 * DS + NH
    xn1 = p.dram("xn1_in", [D, T], BF16, kind="ExternalInput")
    xacc_in = p.dram("xacc_in", [D, T], F32, kind="ExternalInput")
    xacc = p.dram("xacc", [D, T], F32, kind="ExternalOutput")
    in_proj_w = p.dram("in_proj_w", [DIP, D], F32, kind="ExternalInput")
    conv_w = p.dram("conv_w", [CVD, 4], F32, kind="ExternalInput")
    conv_b = p.dram("conv_b", [CVD], F32, kind="ExternalInput")
    dt_bias = p.dram("dt_bias", [NH], F32, kind="ExternalInput")
    A_log = p.dram("A_log", [NH], F32, kind="ExternalInput")
    Dvec = p.dram("Dvec", [NH], F32, kind="ExternalInput")
    norm_weight = p.dram("norm_weight", [DI], F32, kind="ExternalInput")
    out_proj_w = p.dram("out_proj_w", [D, DI], F32, kind="ExternalInput")

    wip_bf = p.dram("wip_bf", [8704, D], BF16)
    wop_bf = p.dram("wop_bf", [D, DI], BF16)
    z_sp = p.dram("z_sp", [DI, T], BF16)
    xc_sp = p.dram("xc_sp", [CVD, T], BF16)
    yn_sp = p.dram("yn_sp", [DI, T], BF16)

    identf = csb["identf"]
    identb = csb["identb"]
    onesf_row = csb["onesf_row"]
    m_ui = csb["m_upincl_neg"]
    for v_ in (1e-5, 1.0 / DI, -0.5, -1.0, 1.0, 0.0):
        p.cscalar(v_)

    xn_sb = p.sbuf("ma_xn", [128, 16 * T], BF16)
    wob = xn_sb[:, 0:16384]
    ynt = xn_sb[:, 16384:32768]
    wipb = p.sbuf("ma_wipb", [128, 16 * 512], BF16)
    zbuf = p.sbuf("ma_zbuf", [128, D], BF16)
    stg = [p.sbuf(f"ma_stg{i}", [128, 512], F32) for i in range(2)]
    y_TM = p.sbuf("ma_y", [128, 4096], F32)
    dt_raw = y_TM[0:64, 0:T]
    dscr = y_TM[0:64, T:2 * T]
    mrow1 = p.sbuf("ma_mrow1", [128, T], F32)
    mrow2 = p.sbuf("ma_mrow2", [128, T], F32)
    cols_m = p.sbuf("ma_cols", [128, 16 * 256], F32)
    rows_all = p.sbuf("ma_rows", [1, 16 * 128], F32)
    cw_sb = p.sbuf("ma_cw", [128, 34 * 4], F32)
    cb_sb = p.sbuf("ma_cb", [128, 34], F32)
    cbT_sb = p.sbuf("ma_cbT", [34, 128], F32)
    misc = p.sbuf("ma_misc", [128, 16], F32)
    nw_rep = p.sbuf("ma_nw", [128, 4096], BF16)
    nwrow = p.sbuf("ma_nwrow", [1, 512], F32)
    Drep = p.sbuf("ma_Drep", [128, NH], F32)
    Drow = p.sbuf("ma_Drow", [1, NH], F32)
    xch = p.sbuf("ma_xch", [128, 32 * 128], BF16)
    zch = p.sbuf("ma_zch", [128, 32 * 128], BF16)
    xcraw = xch[:, 0:T + 4]
    xcv = zch[:, 0:T]
    bc_sb = p.sbuf("ma_bc", [128, 256], BF16)
    BC_sb = p.sbuf("ma_BC", [128, 128], F32)
    BTM_sb = p.sbuf("ma_BTM", [128, 128], BF16)
    Btil = p.sbuf("ma_Btil", [128, 128], BF16)
    MT_sb = p.sbuf("ma_MT", [128, 128], BF16)
    Gm = p.sbuf("ma_Gm", [128, 128], F32)
    Em = p.sbuf("ma_Em", [128, 128], F32)
    Xdt_sb = p.sbuf("ma_Xdt", [128, 64], BF16)
    Ydsb = p.sbuf("ma_Yd", [128, 64], F32)
    t5 = p.sbuf("ma_t5", [128, 64], F32)
    zT = p.sbuf("ma_zT", [128, 4096], BF16)
    yn_TM = p.sbuf("ma_ynTM", [128, 4096], BF16)
    ynF = zT
    state = p.sbuf("ma_state", [128, NH * HD], BF16)

    pool = PsumPool(p, "ma_pp", 4)
    mb0 = p.psum("ma_mb0", [128, 512], F32)
    mb1 = p.psum("ma_mb1", [128, 512], F32)
    mb2 = p.psum("ma_mb2", [128, 512], F32)
    mb3 = p.psum("ma_mb3", [128, 512], F32)

    @p.block.gpsimd
    def _(g):
        emit_prep_cast(p, g, in_proj_w, wip_bf[0:DIP, :], "pw_ip")
        emit_prep_cast(p, g, out_proj_w, wop_bf, "pw_op")
        e = g.memset(zbuf[:, :], 0.0)
        p.inc(e, "mld", 1)
        g.wait_ge(p.sem("mld"), 1)
        e = g.dma_start(out=wip_bf[8512:8640, :], in_=zbuf[:, :])
        p.inc(e, "pw_ip", 16)
        e = g.dma_start(out=wip_bf[8640:8704, :], in_=zbuf[0:64, :])
        p.inc(e, "pw_ip", 16)
        e = g.dma_start(out=cw_sb[:, :].rearrange("p (c j) -> p c j", j=4),
                        in_=conv_w[0:4352, :].rearrange("(c q) j -> q c j", q=128))
        p.inc(e, "mld", 16)
        e = g.dma_start(out=cbT_sb[0:34, :], in_=conv_b[:].rearrange("(c q) -> c q", q=128))
        p.inc(e, "mld", 16)
        e = g.dma_start(out=misc[0:NH, 0:1], in_=dt_bias[:, None])
        p.inc(e, "mld", 16)
        e = g.dma_start(out=misc[0:NH, 1:2], in_=A_log[:, None])
        p.inc(e, "mld", 16)
        e = g.dma_start(out=Drow[0:1, :], in_=Dvec[None, :])
        p.inc(e, "mld", 16)
        for i in range(4):
            e = g.dma_start(out=xacc[i * 512:(i + 1) * 512, :], in_=xacc_in[i * 512:(i + 1) * 512, :])
            p.inc(e, "cpacc", 16)

    @p.block.sync
    def _(sy):
        emit_load_xn(p, sy, xn1, xn_sb, "xnld", tm=False)

    _chain(p, [
        ("tensor", lambda t: t.transpose(mb0[:, 0:34], cbT_sb[0:34, :], identf[0:34, 0:34]), [p.tok("mld"), p.tok("cld")]),
        ("vector", lambda v: v.tensor_copy(cb_sb[:, :], mb0[:, 0:34]), []),
    ])

    for a in range(8):
        @p.block.gpsimd
        def _(g, a=a):
            p.wait(g, "gpsimd", p.tok("cht"))
            e = g.dma_start(out=nwrow[0:1, :], in_=norm_weight[a * 512:(a + 1) * 512][None, :])
            p.inc(e, "mld2", 16)
        _chain(p, [
            ("tensor", lambda t, a=a: t.matmul(mb0[:, :], onesf_row[:, :], nwrow[0:1, :], start=True, stop=True), [p.tok("mld2"), p.tok("cld")]),
            ("vector", lambda v, a=a: v.tensor_copy(nw_rep[:, a * 512:(a + 1) * 512], mb0[:, :]), []),
        ])
    _chain(p, [
        ("tensor", lambda t: t.matmul(mb0[:, 0:NH], onesf_row[:, :], Drow[0:1, :], start=True, stop=True), [p.tok("mld")]),
        ("vector", lambda v: v.tensor_copy(Drep[:, :], mb0[:, 0:NH]), []),
    ])

    last_og_mm = None
    for og in range(17):
        @p.block.sync
        def _(sy, og=og):
            p.wait(sy, "sync", p.tok("pw_ip"))
            if last_og_mm is not None:
                p.wait(sy, "sync", last_og_mm)
            for kcc in range(16):
                e = sy.dma_start(out=wipb[:, kcc * 512:(kcc + 1) * 512],
                                 in_=wip_bf[og * 512:(og + 1) * 512, kcc * 128:(kcc + 1) * 128], transpose=True)
                p.inc(e, "wipld", 16)

        for ol in range(4):
            ocg = og * 4 + ol
            if ocg >= 67:
                continue
            is_dt = (ocg == 66)
            is_z = (ocg < 32)
            cc = ocg - 32 if (not is_z and not is_dt) else None
            for tt in range(4):
                bnk, ps_, free = pool.acquire()

                def mmfn(t, ps_=ps_, free=free, og=og, ol=ol, tt=tt):
                    p.wait(t, "tensor", free, p.tok("wipld"), p.tok("xnld"), p.tok("mld"))
                    e = None
                    for kcc in range(16):
                        e = t.matmul(ps_[:, :], wipb[:, kcc * 512 + ol * 128: kcc * 512 + ol * 128 + 128],
                                     xn_sb[:, kcc * T + tt * 512: kcc * T + tt * 512 + 512],
                                     start=(kcc == 0), stop=(kcc == 15))
                    return e
                if is_z:
                    def evfn(s, ps_=ps_, ocg=ocg, tt=tt):
                        p.wait(s, "scalar", stg_free_ma[tt % 2])
                        return s.activation(stg[tt % 2][:, :].bitcast(BF16)[:, 0:512], ps_[:, :], AF.Silu)
                    tk = _chain(p, [("tensor", mmfn, []), ("scalar", evfn, [])])
                    pool.release(bnk, tk)

                    @p.block.gpsimd
                    def _(g, ocg=ocg, tt=tt, tk=tk):
                        p.wait(g, "gpsimd", tk)
                        e = g.dma_start(out=z_sp[ocg * 128:(ocg + 1) * 128, tt * 512:(tt + 1) * 512],
                                        in_=stg[tt % 2][:, :].bitcast(BF16)[:, 0:512])
                        stg_free_ma[tt % 2] = p.inc(e, "zspst", 16)
                elif is_dt:
                    tk = _chain(p, [
                        ("tensor", mmfn, []),
                        ("vector", lambda v, ps_=ps_, tt=tt: v.tensor_copy(dt_raw[:, tt * 512:(tt + 1) * 512], ps_[0:64, :]), []),
                    ])
                    pool.release(bnk, tk)
                else:
                    tk = _chain(p, [
                        ("tensor", mmfn, []),
                        ("vector", lambda v, ps_=ps_, tt=tt: v.tensor_copy(xcraw[:, 4 + tt * 512: 4 + (tt + 1) * 512], ps_[:, :]), []),
                    ])
                    pool.release(bnk, tk)
            if cc is not None:
                steps = [("gpsimd", lambda g: g.memset(xcraw[:, 0:4], 0.0), [])]
                steps.append(("vector", lambda v, cc=cc: v.tensor_scalar(xcv[:, :], xcraw[:, 1:1 + T], cw_sb[:, cc * 4: cc * 4 + 1], None, ALU.mult), []))
                for j in range(1, 4):
                    steps.append(("vector", lambda v, j=j, cc=cc: v.scalar_tensor_tensor(
                        xcv[:, :], xcraw[:, 1 + j:1 + j + T], cw_sb[:, cc * 4 + j: cc * 4 + j + 1], xcv[:, :], ALU.mult, ALU.add), []))
                steps.append(("scalar", lambda s, cc=cc: s.activation(xcv[:, :], xcv[:, :], AF.Silu, bias=cb_sb[:, cc:cc + 1]), []))
                tk = _chain(p, steps)

                @p.block.gpsimd
                def _(g, cc=cc, tk=tk):
                    p.wait(g, "gpsimd", tk)
                    e = g.dma_start(out=xc_sp[cc * 128:(cc + 1) * 128, :], in_=xcv[:, :])
                    p.inc(e, "xcspst", 16)
        last_og_mm = p.tok("cht")

    _chain(p, [
        ("scalar", lambda s: s.activation(dscr[:, :], dt_raw[:, :], AF.Abs, bias=misc[0:NH, 0:1]), [p.tok("xcspst")]),
        ("scalar", lambda s: s.activation(dscr[:, :], dscr[:, :], AF.Exp, scale=-1.0), []),
        ("scalar", lambda s: s.activation(dscr[:, :], dscr[:, :], AF.Ln, bias=1.0), []),
        ("vector", lambda v: v.tensor_scalar(dt_raw[:, :], dt_raw[:, :], misc[0:NH, 0:1], 0.0, ALU.add, ALU.max), []),
        ("vector", lambda v: v.tensor_add(dt_raw[:, :], dt_raw[:, :], dscr[:, :]), []),
        ("vector", lambda v: v.tensor_copy(mrow1[0:64, :], dt_raw[:, :]), []),
        ("scalar", lambda s: s.activation(misc[0:NH, 2:3], misc[0:NH, 1:2], AF.Exp), []),
        ("vector", lambda v: v.tensor_scalar_mul(misc[0:NH, 3:4], misc[0:NH, 2:3], -1.0), []),
        ("vector", lambda v: v.tensor_scalar(dscr[:, :], dt_raw[:, :], misc[0:NH, 3:4], None, ALU.mult), []),
    ])
    for c in range(16):
        sl = slice(c * 128, (c + 1) * 128)
        _chain(p, [
            ("vector", lambda v, sl=sl: v.tensor_tensor_scan(mrow1[64:128, sl], dscr[:, sl], dscr[:, sl], 0.0, ALU.add, ALU.bypass), []),
            ("vector", lambda v, sl=sl, c=c: v.tensor_scalar(dt_raw[:, sl], mrow1[64:128, sl], mrow1[64:128, c * 128 + 127:c * 128 + 128], None, ALU.subtract), []),
            ("scalar", lambda s, sl=sl: s.activation(mrow2[0:64, sl], dt_raw[:, sl], AF.Exp, scale=-1.0), []),
        ])
    _chain(p, [
        ("scalar", lambda s: s.activation(mrow2[64:128, :], mrow1[64:128, :], AF.Exp), []),
    ])
    for c in range(16):
        _chain(p, [
            ("tensor", lambda t, c=c: t.transpose(mb0[:, 0:128], mrow1[:, c * 128:(c + 1) * 128], identf[:, :]), []),
            ("vector", lambda v, c=c: v.tensor_copy(cols_m[:, c * 256: c * 256 + 128], mb0[:, 0:128]), []),
            ("tensor", lambda t, c=c: t.transpose(mb0[:, 128:256], mrow2[:, c * 128:(c + 1) * 128], identf[:, :]), []),
            ("vector", lambda v, c=c: v.tensor_copy(cols_m[:, c * 256 + 128: c * 256 + 256], mb0[:, 128:256]), []),
        ])
    _chain(p, [("vector", lambda v: v.memset(state[:, :], 0.0), [])])

    def mcol(q, h, c):
        off = {"dt": 0, "lA": 64, "ds": 128, "elA": 192}[q]
        return cols_m[:, c * 256 + off + h: c * 256 + off + h + 1]

    for c in range(16):
        @p.block.sync
        def _(sy, c=c):
            p.wait(sy, "sync", p.tok("xcspst"), p.tok("zspst"))
            p.wait(sy, "sync", p.tok("cht"), p.tok("chv"), p.tok("chs"))
            e = sy.dma_start(out=xch[:, :].rearrange("p (a t) -> p a t", t=128),
                             in_=xc_sp[0:DI, c * 128:(c + 1) * 128].rearrange("(a p) t -> p a t", p=128))
            p.inc(e, "chld", 16)
            e = sy.dma_start(out=zch[:, :].rearrange("p (a t) -> p a t", t=128),
                             in_=z_sp[0:DI, c * 128:(c + 1) * 128].rearrange("(a p) t -> p a t", p=128))
            p.inc(e, "chld", 16)
            e = sy.dma_start(out=bc_sb[:, :].rearrange("p (a t) -> p a t", t=128),
                             in_=xc_sp[DI:CVD, c * 128:(c + 1) * 128].rearrange("(a p) t -> p a t", p=128))
            p.inc(e, "chld", 16)


        _chain(p, [
            ("tensor", lambda t: t.matmul(mb0[:, 0:128], bc_sb[:, 0:128], bc_sb[:, 128:256], start=True, stop=True), [p.tok("chld")]),
            ("vector", lambda v: v.tensor_copy(BC_sb[:, :], mb0[:, 0:128]), []),
            ("tensor", lambda t: t.transpose(mb0.bitcast(BF16)[:, 512:640], bc_sb[:, 0:128], identb[:, :]), []),
            ("vector", lambda v: v.tensor_copy(BTM_sb[:, :], mb0.bitcast(BF16)[:, 512:640]), []),
        ])
        for h in range(NH):
            a = h // 2
            pb = (h % 2) * 64
            if h % 16 == 0:
                @p.block.gpsimd
                def _(g, c=c, h=h):
                    p.wait(g, "gpsimd", p.tok("cht"), p.tok("chv"))
                    e = g.dma_start(out=rows_all[0:1, :], in_=mrow1[64 + h:64 + h + 16, c * 128:(c + 1) * 128])
                    p.inc(e, "rowld", 16)
            dt_c, lA_c = mcol("dt", h, c), mcol("lA", h, c)
            ds_c, elA_c = mcol("ds", h, c), mcol("elA", h, c)
            _chain(p, [
                ("tensor", lambda t, a=a, pb=pb: t.transpose(mb1.bitcast(BF16)[:, 0:64], xch[pb:pb + 64, a * 128:(a + 1) * 128], identb[pb:pb + 64, pb:pb + 64]), [p.tok("rowld")]),
                ("vector", lambda v, dt_c=dt_c: v.tensor_scalar(Xdt_sb[:, :], mb1.bitcast(BF16)[:, 0:64], dt_c, None, ALU.mult), []),
                ("tensor", lambda t, h=h: t.matmul(mb1[:, 128:256], onesf_row[:, :], rows_all[0:1, (h % 16) * 128:(h % 16) * 128 + 128], start=True, stop=True), []),
                ("vector", lambda v, lA_c=lA_c: v.scalar_tensor_tensor(Gm[:, :], mb1[:, 128:256], lA_c, m_ui[:, :], ALU.subtract, ALU.add), []),
                ("scalar", lambda s: s.activation(Em[:, :], Gm[:, :], AF.Exp), []),
                ("vector", lambda v: v.tensor_tensor(MT_sb[:, :], Em[:, :], BC_sb[:, :], ALU.mult), []),
                ("tensor", lambda t: t.matmul(mb2[:, 0:64], MT_sb[:, :], Xdt_sb[:, :], start=True, stop=True), []),
                ("scalar", lambda s: s.copy(Ydsb[:, :], mb2[:, 0:64]), []),
                ("tensor", lambda t, h=h: t.matmul(mb3[:, 0:64], bc_sb[:, 128:256], state[:, h * 64:(h + 1) * 64], start=True, stop=True), []),
                ("vector", lambda v, elA_c=elA_c: v.scalar_tensor_tensor(t5[:, :], mb3[:, 0:64], elA_c, Ydsb[:, :], ALU.mult, ALU.add), []),
                ("vector", lambda v, h=h: v.scalar_tensor_tensor(y_TM[:, h * 64:(h + 1) * 64], mb1.bitcast(BF16)[:, 0:64], Drep[:, h:h + 1], t5[:, :], ALU.mult, ALU.add), []),
                ("vector", lambda v, ds_c=ds_c: v.tensor_scalar(Btil[:, :], BTM_sb[:, :], ds_c, None, ALU.mult), []),
                ("tensor", lambda t: t.matmul(mb3[:, 128:192], Btil[:, :], Xdt_sb[:, :], start=True, stop=True), []),
                ("vector", lambda v, ds_c=ds_c, elA_c=elA_c: v.tensor_scalar(misc[:, 4:5], ds_c, elA_c, None, ALU.mult), []),
                ("vector", lambda v, h=h: v.scalar_tensor_tensor(state[:, h * 64:(h + 1) * 64], state[:, h * 64:(h + 1) * 64], misc[:, 4:5], mb3[:, 128:192], ALU.mult, ALU.add), []),
            ])
        for a in range(32):
            _chain(p, [
                ("tensor", lambda t, a=a: t.transpose(mb0.bitcast(BF16)[:, 0:128], zch[:, a * 128:(a + 1) * 128], identb[:, :]), []),
                ("vector", lambda v, a=a: v.tensor_copy(zT[:, a * 128:(a + 1) * 128], mb0.bitcast(BF16)[:, 0:128]), []),
            ])
        _chain(p, [
            ("vector", lambda v: v.tensor_tensor(y_TM[:, :], y_TM[:, :], zT[:, :], ALU.mult), []),
            ("scalar", lambda s: s.activation(zT[:, :], y_TM[:, :], AF.Square, accum_out=misc[:, 5:6]), []),
            ("scalar", lambda s: s.activation(misc[:, 6:7], misc[:, 5:6], AF.Ln, scale=1.0 / DI, bias=1e-5), []),
            ("scalar", lambda s: s.activation(misc[:, 7:8], misc[:, 6:7], AF.Exp, scale=-0.5), []),
            ("vector", lambda v: v.scalar_tensor_tensor(yn_TM[:, :], y_TM[:, :], misc[:, 7:8], nw_rep[:, :], ALU.mult, ALU.mult), []),
        ])
        for a in range(32):
            _chain(p, [
                ("tensor", lambda t, a=a: t.transpose(mb0.bitcast(BF16)[:, 0:128], yn_TM[:, a * 128:(a + 1) * 128], identb[:, :]), []),
                ("vector", lambda v, a=a: v.tensor_copy(ynF[:, a * 128:(a + 1) * 128], mb0.bitcast(BF16)[:, 0:128]), []),
            ])
        tk = p.tok("chv")

        @p.block.gpsimd
        def _(g, c=c, tk=tk):
            p.wait(g, "gpsimd", tk)
            e = g.dma_start(out=yn_sp[0:DI, c * 128:(c + 1) * 128].rearrange("(a p) t -> p a t", p=128),
                            in_=ynF[:, :].rearrange("p (a t) -> p a t", t=128))
            p.inc(e, "ynspst", 16)
    yn_ready = p.tok("ynspst")
    prev_og = None
    for og in range(4):
        @p.block.sync
        def _(sy, og=og):
            p.wait(sy, "sync", p.tok("pw_op"))
            if prev_og is not None:
                p.wait(sy, "sync", prev_og)
            for kcc in range(32):
                e = sy.dma_start(out=wob[:, kcc * 512:(kcc + 1) * 512],
                                 in_=wop_bf[og * 512:(og + 1) * 512, kcc * 128:(kcc + 1) * 128], transpose=True)
                p.inc(e, "wopld", 16)

        for tt in range(4):
            @p.block.sync
            def _(sy, tt=tt, og=og):
                p.wait(sy, "sync", p.tok("ynspst"))
                if og > 0 or tt > 0:
                    p.wait(sy, "sync", p.tok("cht"))
                e = sy.dma_start(out=ynt[:, :].rearrange("p (a t) -> p a t", t=512),
                                 in_=yn_sp[0:DI, tt * 512:(tt + 1) * 512].rearrange("(a p) t -> p a t", p=128))
                p.inc(e, "yntld", 16)

            for ol in range(4):
                bnk, ps_, free = pool.acquire()

                def mmfn(t, ps_=ps_, free=free, og=og, ol=ol, tt=tt):
                    p.wait(t, "tensor", free, p.tok("wopld"), p.tok("yntld"))
                    e = None
                    for kcc in range(32):
                        e = t.matmul(ps_[:, :], wob[:, kcc * 512 + ol * 128: kcc * 512 + ol * 128 + 128],
                                     ynt[:, kcc * 512:(kcc + 1) * 512],
                                     start=(kcc == 0), stop=(kcc == 31))
                    return e

                def evfn(s, ps_=ps_, ol=ol, tt=tt):
                    p.wait(s, "scalar", stg_free_ma[(ol + tt) % 2])
                    return s.copy(stg[(ol + tt) % 2][:, :], ps_[:, :])
                tk2 = _chain(p, [("tensor", mmfn, []), ("scalar", evfn, [])])
                pool.release(bnk, tk2)

                @p.block.gpsimd
                def _(g, og=og, ol=ol, tt=tt, tk2=tk2):
                    p.wait(g, "gpsimd", tk2, p.tok("cpacc"))
                    e = g.dma_start(out=xacc[og * 512 + ol * 128: og * 512 + ol * 128 + 128, tt * 512:(tt + 1) * 512],
                                    in_=stg[(ol + tt) % 2][:, :], accum_op=ALU.add)
                    stg_free_ma[(ol + tt) % 2] = p.inc(e, "acc", 16)
        prev_og = p.tok("cht")

    @p.block.sync
    def _(sy):
        p.final_wait(sy, "sync")


stg_free_ma = [None, None]


def build_l1_attn():
    global stg_free_ma
    stg_free_ma = [None, None]
    p = Prog("l1attn")
    with p.es:
        csb, cemit = load_consts(p, ["identf", "identb", "onesf_row", "m_upincl_neg"])
        with p.nc.Block() as blk:
            p.block = blk

            @blk.gpsimd
            def _(g):
                cemit(g)
            emit_l1_attn(p, csb)
    return p


def emit_l1_ffn(p, csb):
    INTER = 8192
    xn1 = p.dram("xn1_in", [D, T], BF16, kind="ExternalInput")
    xacc_in = p.dram("xacc_in", [D, T], F32, kind="ExternalInput")
    xacc = p.dram("xacc", [D, T], F32, kind="ExternalOutput")
    SU_ug = p.dram("SU_ug", [D], F32, kind="ExternalInput")
    SU_dp = p.dram("SU_dp", [INTER], F32, kind="ExternalInput")
    Wscale_ug = p.dram("Wscale_ug", [2 * INTER], F32, kind="ExternalInput")
    Wscale_dp = p.dram("Wscale_dp", [D], F32, kind="ExternalInput")
    W_up = p.dram("W_up", [INTER, D], F32, kind="ExternalInput")
    W_gate = p.dram("W_gate", [INTER, D], F32, kind="ExternalInput")
    W_down = p.dram("W_down", [D, INTER], F32, kind="ExternalInput")

    wu_bf = p.dram("wu_bf", [INTER, D], BF16)
    wg_bf = p.dram("wg_bf", [INTER, D], BF16)
    wd_bf = p.dram("wd_bf", [D, INTER], BF16)
    hadA1 = p.dram("hadA1", [D, T], BF16)
    hadB1 = p.dram("hadB1", [D, T], BF16)
    hadA2 = p.dram("hadA2", [INTER, T], BF16)
    hadB2 = p.dram("hadB2", [INTER, T], BF16)

    identf = csb["identf"]
    IH16 = csb["IH16"]
    IH64 = csb["IH64"]
    H128 = csb["H128"]
    RS2 = 1.0 / float(np.sqrt(2048.0))
    RS8 = 1.0 / float(np.sqrt(8192.0))

    xn_sb = p.sbuf("qf_xn", [128, 16 * T], BF16)
    y2t = xn_sb[:, 0:32768]
    wgb = p.sbuf("qf_wgb", [128, 16 * 512], BF16)
    wub = p.sbuf("qf_wub", [128, 16 * 512], BF16)
    wdob = p.sbuf("qf_wdob", [128, 64 * 256], BF16)
    hbuf = p.sbuf("qf_hbuf", [128, T], BF16)
    tabT = p.sbuf("qf_tabT", [128, 128], F32)
    su_ug = p.sbuf("qf_suug", [128, 16], F32)
    su_dp = p.sbuf("qf_sudp", [128, 64], F32)
    wsc = p.sbuf("qf_wsc", [128, 128], F32)
    wsc_dp = p.sbuf("qf_wscdp", [128, 16], F32)
    sup = p.sbuf("qf_sup", [128, 64], F32)
    sgt = p.sbuf("qf_sgt", [128, 64], F32)
    sdp = p.sbuf("qf_sdp", [128, 16], F32)
    sgs = [p.sbuf(f"qf_sg{i}", [128, 512], F32) for i in range(2)]
    stg = [p.sbuf(f"qf_stg{i}", [128, 512], F32) for i in range(2)]
    evb = [p.sbuf(f"qf_ev{i}", [128, 512], BF16) for i in range(2)]

    pool = PsumPool(p, "qf_pp", 6)

    @p.block.gpsimd
    def _(g):
        emit_prep_cast(p, g, W_up, wu_bf, "pw_u")
        emit_prep_cast(p, g, W_gate, wg_bf, "pw_g")
        emit_prep_cast(p, g, W_down, wd_bf, "pw_d")
        e = g.dma_start(out=tabT[0:16, 0:128], in_=SU_ug[:].rearrange("(c q) -> c q", q=128))
        p.inc(e, "tld1", 16)
        for i in range(4):
            e = g.dma_start(out=xacc[i * 512:(i + 1) * 512, :], in_=xacc_in[i * 512:(i + 1) * 512, :])
            p.inc(e, "cpacc", 16)

    bnk0, ps0, _f = pool.acquire()
    _chain(p, [
        ("tensor", lambda t: t.transpose(ps0[:, 0:16], tabT[0:16, 0:128], identf[0:16, 0:16]), [p.tok("tld1"), p.tok("cld")]),
        ("vector", lambda v: v.tensor_copy(su_ug[:, :], ps0[:, 0:16]), []),
    ])

    @p.block.gpsimd
    def _(g):
        p.wait(g, "gpsimd", p.tok("chv"))
        e = g.dma_start(out=tabT[0:64, 0:128], in_=SU_dp[:].rearrange("(c q) -> c q", q=128))
        p.inc(e, "tld2", 16)
    _chain(p, [
        ("tensor", lambda t: t.transpose(ps0[:, 0:64], tabT[0:64, 0:128], identf[0:64, 0:64]), [p.tok("tld2")]),
        ("vector", lambda v: v.tensor_copy(su_dp[:, :], ps0[:, 0:64]), []),
    ])

    @p.block.gpsimd
    def _(g):
        p.wait(g, "gpsimd", p.tok("chv"))
        e = g.dma_start(out=tabT[0:128, 0:128], in_=Wscale_ug[:].rearrange("(c q) -> c q", q=128))
        p.inc(e, "tld3", 16)
    _chain(p, [
        ("tensor", lambda t: t.transpose(ps0[:, 0:128], tabT[:, 0:128], identf[:, :]), [p.tok("tld3")]),
        ("vector", lambda v: v.tensor_copy(wsc[:, :], ps0[:, 0:128]), []),
    ])

    @p.block.gpsimd
    def _(g):
        p.wait(g, "gpsimd", p.tok("chv"))
        e = g.dma_start(out=tabT[0:16, 0:128], in_=Wscale_dp[:].rearrange("(c q) -> c q", q=128))
        p.inc(e, "tld4", 16)
    _chain(p, [
        ("tensor", lambda t: t.transpose(ps0[:, 0:16], tabT[0:16, 0:128], identf[0:16, 0:16]), [p.tok("tld4")]),
        ("vector", lambda v: v.tensor_copy(wsc_dp[:, :], ps0[:, 0:16]), []),
        ("vector", lambda v: v.tensor_mul(sup[:, :], wsc[:, 0:64], su_dp[:, :]), []),
        ("vector", lambda v: v.tensor_scalar_mul(sup[:, :], sup[:, :], RS2), []),
        ("vector", lambda v: v.tensor_scalar_mul(sgt[:, :], wsc[:, 64:128], RS2), []),
        ("vector", lambda v: v.tensor_scalar_mul(sdp[:, :], wsc_dp[:, :], RS8), []),
    ])
    pool.release(bnk0, p.tok("chv"))

    @p.block.sync
    def _(sy):
        emit_load_xn(p, sy, xn1, xn_sb, "xnld", tm=False)

    for dc in range(16):
        _chain(p, [
            ("vector", lambda v, dc=dc: v.tensor_scalar(xn_sb[:, dc * T:(dc + 1) * T], xn_sb[:, dc * T:(dc + 1) * T],
                                                        su_ug[:, dc:dc + 1], None, ALU.mult), [p.tok("xnld")]),
        ])
        for tt in range(4):
            bnk, ps_, free = pool.acquire()
            b = (dc * 4 + tt) % 2

            def mmfn(t, dc=dc, tt=tt, ps_=ps_, free=free):
                p.wait(t, "tensor", free)
                return t.matmul(ps_[:, :], IH16[:, :], xn_sb[:, dc * T + tt * 512: dc * T + tt * 512 + 512],
                                start=True, stop=True)

            def evfn(v, ps_=ps_, b=b):
                p.wait(v, "vector", evfree_qf[b])
                return v.tensor_copy(evb[b][:, :], ps_[:, :])
            tk = _chain(p, [("tensor", mmfn, []), ("vector", evfn, [])])
            pool.release(bnk, tk)

            @p.block.gpsimd
            def _(g, dc=dc, tt=tt, b=b, tk=tk):
                p.wait(g, "gpsimd", tk)
                e = g.dma_start(out=hadA1[dc * 128:(dc + 1) * 128, tt * 512:(tt + 1) * 512], in_=evb[b][:, :])
                evfree_qf[b] = p.inc(e, "hA1", 16)

    for q in range(16):
        for tt in range(4):
            bnk, ps_, free = pool.acquire()
            b = (q * 4 + tt) % 2

            @p.block.sync
            def _(sy, q=q, tt=tt):
                p.wait(sy, "sync", p.tok("hA1"))
                p.wait(sy, "sync", p.tok("b1ld"))
                p.wait(sy, "sync", p.tok("cht"))
                e = sy.dma_start(out=vq_qf[0][:, :], in_=hadA1[q:D:16, tt * 512:(tt + 1) * 512])
                p.inc(e, "b1ld", 16)

            def mmfn(t, ps_=ps_, free=free):
                p.wait(t, "tensor", free, p.tok("b1ld"))
                return t.matmul(ps_[:, :], H128[:, :], vq_qf[0][:, :], start=True, stop=True)

            def evfn(v, ps_=ps_, b=b):
                p.wait(v, "vector", evfree_qf[b])
                return v.tensor_copy(evb[b][:, :], ps_[:, :])
            tk = _chain(p, [("tensor", mmfn, []), ("vector", evfn, [])])
            pool.release(bnk, tk)

            @p.block.gpsimd
            def _(g, q=q, tt=tt, b=b, tk=tk):
                p.wait(g, "gpsimd", tk)
                e = g.dma_start(out=hadB1[q:D:16, tt * 512:(tt + 1) * 512], in_=evb[b][:, :])
                evfree_qf[b] = p.inc(e, "hB1", 16)

    @p.block.sync
    def _(sy):
        p.wait(sy, "sync", p.tok("hB1"), p.tok("cht"), p.tok("chv"))
        for dc in range(16):
            e = sy.dma_start(out=xn_sb[:, dc * T:(dc + 1) * T], in_=hadB1[dc * 128:(dc + 1) * 128, :])
            p.inc(e, "y1ld", 16)

    last_og = None
    for og in range(16):
        @p.block.sync
        def _(sy, og=og):
            p.wait(sy, "sync", p.tok("pw_u"), p.tok("pw_g"))
            if last_og is not None:
                p.wait(sy, "sync", last_og)
            for kcc in range(16):
                e = sy.dma_start(out=wgb[:, kcc * 512:(kcc + 1) * 512],
                                 in_=wg_bf[og * 512:(og + 1) * 512, kcc * 128:(kcc + 1) * 128], transpose=True)
                p.inc(e, "wugld", 16)
                e = sy.dma_start(out=wub[:, kcc * 512:(kcc + 1) * 512],
                                 in_=wu_bf[og * 512:(og + 1) * 512, kcc * 128:(kcc + 1) * 128], transpose=True)
                p.inc(e, "wugld", 16)

        for ol in range(4):
            oc = og * 4 + ol
            for tt in range(4):
                bg, gps, gfree = pool.acquire()
                bu, ups, ufree = pool.acquire()

                def mmfn(t, gps=gps, ups=ups, gfree=gfree, ufree=ufree, ol=ol, tt=tt):
                    p.wait(t, "tensor", gfree, ufree, p.tok("wugld"), p.tok("y1ld"))
                    e = None
                    for kcc in range(16):
                        e = t.matmul(gps[:, :], wgb[:, kcc * 512 + ol * 128: kcc * 512 + ol * 128 + 128],
                                     xn_sb[:, kcc * T + tt * 512: kcc * T + tt * 512 + 512],
                                     start=(kcc == 0), stop=(kcc == 15))
                    globals()["_qf_g"] = p.inc(e, "cht", 1)
                    for kcc in range(16):
                        e = t.matmul(ups[:, :], wub[:, kcc * 512 + ol * 128: kcc * 512 + ol * 128 + 128],
                                     xn_sb[:, kcc * T + tt * 512: kcc * T + tt * 512 + 512],
                                     start=(kcc == 0), stop=(kcc == 15))
                    return e
                b = tt % 2

                def sgfn(s, gps=gps, oc=oc, b=b):
                    p.wait(s, "scalar", sgfree_qf[b])
                    return s.activation(sgs[b][:, :], gps[:, :], AF.Silu, scale=sgt[:, oc % 64: oc % 64 + 1] if False else sgt[:, oc // 1 % 64: oc // 1 % 64 + 1])

                def hfn(v, ups=ups, oc=oc, tt=tt, b=b):
                    return v.scalar_tensor_tensor(hbuf[:, tt * 512:(tt + 1) * 512], ups[:, :],
                                                  sup[:, oc % 64: oc % 64 + 1], sgs[b][:, :], ALU.mult, ALU.mult)
                tk = _chain(p, [("tensor", mmfn, []), ("scalar", sgfn, []), ("vector", hfn, [])])
                pool.release(bg, tk)
                pool.release(bu, tk)
                sgfree_qf[b] = tk
            for tt in range(4):
                bnk, ps_, free = pool.acquire()
                b = tt % 2

                def mmfn2(t, tt=tt, ps_=ps_, free=free):
                    p.wait(t, "tensor", free)
                    return t.matmul(ps_[:, :], IH64[:, :], hbuf[:, tt * 512:(tt + 1) * 512], start=True, stop=True)

                def evfn2(v, ps_=ps_, b=b):
                    p.wait(v, "vector", evfree_qf[b])
                    return v.tensor_copy(evb[b][:, :], ps_[:, :])
                tk = _chain(p, [("tensor", mmfn2, []), ("vector", evfn2, [])])
                pool.release(bnk, tk)

                @p.block.gpsimd
                def _(g, oc=oc, tt=tt, b=b, tk=tk):
                    p.wait(g, "gpsimd", tk)
                    e = g.dma_start(out=hadA2[oc * 128:(oc + 1) * 128, tt * 512:(tt + 1) * 512], in_=evb[b][:, :])
                    evfree_qf[b] = p.inc(e, "hA2", 16)
        last_og = p.tok("cht")

    for q in range(64):
        for tt in range(4):
            bnk, ps_, free = pool.acquire()
            b = (q * 4 + tt) % 2

            @p.block.sync
            def _(sy, q=q, tt=tt):
                p.wait(sy, "sync", p.tok("hA2"), p.tok("cht"))
                e = sy.dma_start(out=vq_qf[0][:, :], in_=hadA2[q:INTER:64, tt * 512:(tt + 1) * 512])
                p.inc(e, "b2ld", 16)

            def mmfn(t, ps_=ps_, free=free):
                p.wait(t, "tensor", free, p.tok("b2ld"))
                return t.matmul(ps_[:, :], H128[:, :], vq_qf[0][:, :], start=True, stop=True)

            def evfn(v, ps_=ps_, b=b):
                p.wait(v, "vector", evfree_qf[b])
                return v.tensor_copy(evb[b][:, :], ps_[:, :])
            tk = _chain(p, [("tensor", mmfn, []), ("vector", evfn, [])])
            pool.release(bnk, tk)

            @p.block.gpsimd
            def _(g, q=q, tt=tt, b=b, tk=tk):
                p.wait(g, "gpsimd", tk)
                e = g.dma_start(out=hadB2[q:INTER:64, tt * 512:(tt + 1) * 512], in_=evb[b][:, :])
                evfree_qf[b] = p.inc(e, "hB2", 16)

    acc_base = p.tok("acc")
    prev_og = None
    for og in range(8):
        @p.block.sync
        def _(sy, og=og):
            p.wait(sy, "sync", p.tok("pw_d"))
            if prev_og is not None:
                p.wait(sy, "sync", prev_og)
            for kcc in range(64):
                e = sy.dma_start(out=wdob[:, kcc * 256:(kcc + 1) * 256],
                                 in_=wd_bf[og * 256:(og + 1) * 256, kcc * 128:(kcc + 1) * 128], transpose=True)
                p.inc(e, "wdld", 16)

        for tt in range(4):
            @p.block.sync
            def _(sy, tt=tt, og=og):
                p.wait(sy, "sync", p.tok("hB2"), p.tok("cht"))
                for kcc in range(64):
                    e = sy.dma_start(out=y2t[:, kcc * 512:(kcc + 1) * 512],
                                     in_=hadB2[kcc * 128:(kcc + 1) * 128, tt * 512:(tt + 1) * 512])
                    p.inc(e, "y2ld", 16)

            for ol in range(2):
                oc = og * 2 + ol
                bnk, ps_, free = pool.acquire()

                def mmfn(t, ps_=ps_, free=free, ol=ol):
                    p.wait(t, "tensor", free, p.tok("wdld"), p.tok("y2ld"))
                    e = None
                    for kcc in range(64):
                        e = t.matmul(ps_[:, :], wdob[:, kcc * 256 + ol * 128: kcc * 256 + ol * 128 + 128],
                                     y2t[:, kcc * 512:(kcc + 1) * 512],
                                     start=(kcc == 0), stop=(kcc == 63))
                    return e
                sb_i = (tt + ol) % 2

                def evfn(s, ps_=ps_, oc=oc, sb_i=sb_i):
                    p.wait(s, "scalar", dnfree_qf[sb_i])
                    return s.activation(stg[sb_i][:, :], ps_[:, :], AF.Copy, scale=sdp[:, oc // 1 % 16: oc // 1 % 16 + 1]) if False else s.mul(stg[sb_i][:, :], ps_[:, :], sdp[:, oc % 16: oc % 16 + 1])
                tk = _chain(p, [("tensor", mmfn, []), ("scalar", evfn, [])])
                pool.release(bnk, tk)

                @p.block.gpsimd
                def _(g, oc=oc, tt=tt, sb_i=sb_i, tk=tk):
                    p.wait(g, "gpsimd", tk, p.tok("cpacc"), acc_base)
                    e = g.dma_start(out=xacc[oc * 128:(oc + 1) * 128, tt * 512:(tt + 1) * 512],
                                    in_=stg[sb_i][:, :], accum_op=ALU.add)
                    dnfree_qf[sb_i] = p.inc(e, "acc", 16)
        prev_og = p.tok("cht")

    @p.block.sync
    def _(sy):
        p.final_wait(sy, "sync")


evfree_qf = [None, None]
sgfree_qf = [None, None]
dnfree_qf = [None, None]
vq_qf = [None]


def build_l1_ffn():
    global evfree_qf, sgfree_qf, dnfree_qf
    evfree_qf = [None, None]
    sgfree_qf = [None, None]
    dnfree_qf = [None, None]
    p = Prog("l1ffn")
    with p.es:
        csb, cemit = load_consts(p, ["identf", "IH16", "IH64", "H128"])
        vq_qf[0] = p.sbuf("qf_vq", [128, 512], BF16)
        with p.nc.Block() as blk:
            p.block = blk

            @blk.gpsimd
            def _(g):
                cemit(g)
            emit_l1_ffn(p, csb)
    return p


_CN = {
    "ln0": ["identf"],
    "l0attn": ["identf", "identb", "ones_row", "onesf_row", "onescol_b",
               "m_upincl_neg", "m_upstrict_neg", "m_lowstrict_pos", "twoI"],
    "l0moe": ["identf", "ones_row"],
    "lnfm": ["onesf_row", "onescol_b"],
    "l1attn": ["identf", "identb", "onesf_row", "m_upincl_neg"],
    "l1ffn": ["identf", "IH16", "IH64", "H128"],
    "final": ["identf"],
}

ALL_CONSTS = ["identf", "identb", "ones_row", "onesf_row", "onescol_b",
              "m_upincl_neg", "m_upstrict_neg", "m_lowstrict_pos", "twoI",
              "IH16", "IH64", "H128"]


def build_fused():
    global stg_free_da, stg_free_ma, evfree_qf, sgfree_qf, dnfree_qf
    p = Prog("fused")
    with p.es:
        csb, cemit = load_consts(p, ALL_CONSTS)
        vq_qf[0] = p.es.enter_context(p.nc.sbuf_tensor("qf_vq", [128, 512], BF16))
        x_in = p.dram("g_x_in", [T, D], F32, kind="ExternalInput")
        y_out = p.dram("g_y_out", [T, D], F32, kind="ExternalOutput")
        xaccA = p.dram("g_xaccA", [D, T], F32)
        xaccB = p.dram("g_xaccB", [D, T], F32)
        xn0t = p.dram("g_xn0", [T, D], BF16)
        xn1t = p.dram("g_xn1", [D, T], BF16)
        W = {}
        wspecs = [
            ("L0_attn_W_qkv", [8192, D]), ("L0_attn_W_z", [4096, D]),
            ("L0_attn_W_b", [32, D]), ("L0_attn_W_a", [32, D]),
            ("L0_attn_conv_w", [8192, 4]), ("L0_attn_dt_bias", [32]),
            ("L0_attn_A_log", [32]), ("L0_attn_norm_w", [128]),
            ("L0_attn_W_out", [D, 4096]),
            ("L0_ffn_gate_w", [8, D]), ("L0_ffn_w_gate", [8, 1024, D]),
            ("L0_ffn_w_up", [8, 1024, D]), ("L0_ffn_w_down", [8, D, 1024]),
            ("L0_ffn_sh_gate", [D, D]), ("L0_ffn_sh_up", [D, D]),
            ("L0_ffn_sh_down", [D, D]), ("L0_ffn_se_gate_w", [1, D]),
            ("L1_attn_in_proj_w", [8512, D]), ("L1_attn_conv_w", [4352, 4]),
            ("L1_attn_conv_b", [4352]), ("L1_attn_dt_bias", [64]),
            ("L1_attn_A_log", [64]), ("L1_attn_D", [64]),
            ("L1_attn_norm_weight", [4096]), ("L1_attn_out_proj_w", [D, 4096]),
            ("L1_ffn_SU_ug", [D]), ("L1_ffn_SU_dp", [8192]),
            ("L1_ffn_Wscale_ug", [16384]), ("L1_ffn_Wscale_dp", [D]),
            ("L1_ffn_W_up", [8192, D]), ("L1_ffn_W_gate", [8192, D]),
            ("L1_ffn_W_down", [D, 8192]),
        ]
        for n, shp in wspecs:
            W[n] = p.dram(n, shp, F32, kind="ExternalInput")

        with p.nc.Block() as blk:
            p.block = blk

            @blk.gpsimd
            def _(g):
                cemit(g)

            def phase(pfx, fn, io):
                p.push_frame(pfx)
                p.io = io
                fn(p, csb)
                p.io = {}
                p.barrier()
                p.pop_frame()

            phase("P0_", emit_ln0, {"x_in": x_in, "xacc": xaccA, "xn0": xn0t})
            stg_free_da = [None, None]
            phase("P1_", emit_l0_attn, {
                "xn0_in": xn0t, "xacc_in": xaccA, "xacc": xaccB,
                "W_qkv": W["L0_attn_W_qkv"], "W_z": W["L0_attn_W_z"],
                "W_b": W["L0_attn_W_b"], "W_a": W["L0_attn_W_a"],
                "conv_w": W["L0_attn_conv_w"], "dt_bias": W["L0_attn_dt_bias"],
                "A_log": W["L0_attn_A_log"], "norm_w": W["L0_attn_norm_w"],
                "W_out": W["L0_attn_W_out"]})
            phase("P2_", emit_l0_moe, {
                "xn0_in": xn0t, "xacc_in": xaccB, "xacc": xaccA,
                "gate_w": W["L0_ffn_gate_w"], "w_gate": W["L0_ffn_w_gate"],
                "w_up": W["L0_ffn_w_up"], "w_down": W["L0_ffn_w_down"],
                "sh_gate": W["L0_ffn_sh_gate"], "sh_up": W["L0_ffn_sh_up"],
                "sh_down": W["L0_ffn_sh_down"], "se_gate_w": W["L0_ffn_se_gate_w"]})
            phase("P3_", emit_ln_fm, {"xacc_in": xaccA, "xn1": xn1t})
            stg_free_ma = [None, None]
            phase("P4_", emit_l1_attn, {
                "xn1_in": xn1t, "xacc_in": xaccA, "xacc": xaccB,
                "in_proj_w": W["L1_attn_in_proj_w"], "conv_w": W["L1_attn_conv_w"],
                "conv_b": W["L1_attn_conv_b"], "dt_bias": W["L1_attn_dt_bias"],
                "A_log": W["L1_attn_A_log"], "Dvec": W["L1_attn_D"],
                "norm_weight": W["L1_attn_norm_weight"], "out_proj_w": W["L1_attn_out_proj_w"]})
            evfree_qf = [None, None]
            sgfree_qf = [None, None]
            dnfree_qf = [None, None]
            phase("P5_", emit_l1_ffn, {
                "xn1_in": xn1t, "xacc_in": xaccB, "xacc": xaccA,
                "SU_ug": W["L1_ffn_SU_ug"], "SU_dp": W["L1_ffn_SU_dp"],
                "Wscale_ug": W["L1_ffn_Wscale_ug"], "Wscale_dp": W["L1_ffn_Wscale_dp"],
                "W_up": W["L1_ffn_W_up"], "W_gate": W["L1_ffn_W_gate"],
                "W_down": W["L1_ffn_W_down"]})
            phase("P6_", emit_final, {"xacc_in": xaccA, "y_out": y_out})
    return p


FUSED = True

_PROGS = {}


def _prog(name):
    if name not in _PROGS:
        _PROGS[name] = {
            "ln0": build_ln0, "l0attn": build_l0_attn, "l0moe": build_l0_moe,
            "lnfm": build_ln_fm, "l1attn": build_l1_attn, "l1ffn": build_l1_ffn,
            "final": build_final, "fused": build_fused,
        }[name]()
    return _PROGS[name]


def kernel(**inputs):
    cores = list(range(NCORE))
    x = np.ascontiguousarray(np.asarray(inputs["L0_attn_x"], dtype=np.float32))

    if FUSED:
        p = _prog("fused")
        ci = const_inputs(p, ALL_CONSTS)
        in_maps = []
        for b in range(NCORE):
            m = dict(ci)
            m["g_x_in"] = x[b]
            for k in inputs:
                if k != "L0_attn_x":
                    m[k] = inputs[k]
            in_maps.append(m)
        res = run_bass_kernel_spmd(p.nc, in_maps, core_ids=cores)
        return np.stack([np.asarray(res.results[b]["g_y_out"], dtype=np.float32)
                         for b in range(NCORE)], axis=0)

    def run(name, per_core, shared):
        p = _prog(name)
        ci = const_inputs(p, _CN[name])
        in_maps = []
        for b in range(NCORE):
            m = dict(shared)
            m.update(ci)
            m.update({k: v[b] for k, v in per_core.items()})
            in_maps.append(m)
        res = run_bass_kernel_spmd(p.nc, in_maps, core_ids=cores)
        return res.results

    r = run("ln0", {"x_in": [x[b] for b in range(NCORE)]}, {})
    xacc = [r[b]["xacc"] for b in range(NCORE)]
    xn0 = [r[b]["xn0"] for b in range(NCORE)]

    r = run("l0attn", {"xn0_in": xn0, "xacc_in": xacc}, {
        "W_qkv": inputs["L0_attn_W_qkv"], "W_z": inputs["L0_attn_W_z"],
        "W_b": inputs["L0_attn_W_b"], "W_a": inputs["L0_attn_W_a"],
        "conv_w": inputs["L0_attn_conv_w"], "dt_bias": inputs["L0_attn_dt_bias"],
        "A_log": inputs["L0_attn_A_log"], "norm_w": inputs["L0_attn_norm_w"],
        "W_out": inputs["L0_attn_W_out"]})
    xacc = [r[b]["xacc"] for b in range(NCORE)]

    r = run("l0moe", {"xn0_in": xn0, "xacc_in": xacc}, {
        "gate_w": inputs["L0_ffn_gate_w"], "w_gate": inputs["L0_ffn_w_gate"],
        "w_up": inputs["L0_ffn_w_up"], "w_down": inputs["L0_ffn_w_down"],
        "sh_gate": inputs["L0_ffn_sh_gate"], "sh_up": inputs["L0_ffn_sh_up"],
        "sh_down": inputs["L0_ffn_sh_down"], "se_gate_w": inputs["L0_ffn_se_gate_w"]})
    xacc = [r[b]["xacc"] for b in range(NCORE)]

    r = run("lnfm", {"xacc_in": xacc}, {})
    xn1 = [r[b]["xn1"] for b in range(NCORE)]

    r = run("l1attn", {"xn1_in": xn1, "xacc_in": xacc}, {
        "in_proj_w": inputs["L1_attn_in_proj_w"], "conv_w": inputs["L1_attn_conv_w"],
        "conv_b": inputs["L1_attn_conv_b"], "dt_bias": inputs["L1_attn_dt_bias"],
        "A_log": inputs["L1_attn_A_log"], "Dvec": inputs["L1_attn_D"],
        "norm_weight": inputs["L1_attn_norm_weight"], "out_proj_w": inputs["L1_attn_out_proj_w"]})
    xacc = [r[b]["xacc"] for b in range(NCORE)]

    r = run("l1ffn", {"xn1_in": xn1, "xacc_in": xacc}, {
        "SU_ug": inputs["L1_ffn_SU_ug"], "SU_dp": inputs["L1_ffn_SU_dp"],
        "Wscale_ug": inputs["L1_ffn_Wscale_ug"], "Wscale_dp": inputs["L1_ffn_Wscale_dp"],
        "W_up": inputs["L1_ffn_W_up"], "W_gate": inputs["L1_ffn_W_gate"],
        "W_down": inputs["L1_ffn_W_down"]})
    xacc = [r[b]["xacc"] for b in range(NCORE)]

    r = run("final", {"xacc_in": xacc}, {})
    out = np.stack([np.asarray(r[b]["y_out"], dtype=np.float32) for b in range(NCORE)], axis=0)
    return out


# version 06
# speedup vs baseline: 1.3187x; 1.3095x over previous
import numpy as np
import ml_dtypes
from contextlib import ExitStack
from collections import defaultdict
import concourse.bass as bass
import concourse.mybir as mybir
from concourse.bass_utils import run_bass_kernel_spmd

F32 = mybir.dt.float32
BF16 = mybir.dt.bfloat16
AF = mybir.ActivationFunctionType
ALU = mybir.AluOpType
AX = mybir.AxisListType
BF = ml_dtypes.bfloat16

T = 2048
D = 2048
NCORE = 8
NEG = -1e30
ENGS = ("tensor", "vector", "scalar", "gpsimd", "sync")
CPOOL_N = 32


def _sylvester(n):
    H = np.array([[1.0]], dtype=np.float64)
    while H.shape[0] < n:
        H = np.block([[H, H], [H, -H]])
    return H


_HOST_CONSTS = None


def host_constants():
    global _HOST_CONSTS
    if _HOST_CONSTS is not None:
        return _HOST_CONSTS
    c = {}
    c["identb"] = np.eye(128, dtype=BF)
    c["identf"] = np.eye(128, dtype=np.float32)
    r = np.arange(128)
    c["m_upincl_neg"] = np.where(r[:, None] <= r[None, :], 0.0, NEG).astype(np.float32)
    c["m_upstrict_neg"] = np.where(r[:, None] < r[None, :], 0.0, NEG).astype(np.float32)
    c["m_lowstrict_pos"] = np.where(r[None, :] < r[:, None], 0.0, -NEG).astype(np.float32)
    c["twoI"] = (2.0 * np.eye(128)).astype(np.float32)
    c["IH16"] = np.kron(np.eye(8), _sylvester(16)).astype(BF)
    c["IH64"] = np.kron(np.eye(2), _sylvester(64)).astype(BF)
    c["H128"] = _sylvester(128).astype(BF)
    c["ones_row"] = np.ones((1, 128), dtype=BF)
    c["onesf_row"] = np.ones((1, 128), dtype=np.float32)
    c["negonesf_row"] = np.full((1, 128), -1.0, dtype=np.float32)
    c["m_lowstrict_neg"] = np.where(r[None, :] < r[:, None], 0.0, NEG).astype(np.float32)
    c["onescol_b"] = np.ones((128, 1), dtype=BF)
    _HOST_CONSTS = c
    return c


class Prog:
    def __init__(self, name="prog"):
        self.name = name
        self.nc = bass.Bass("TRN2", target_bir_lowering=False)
        self.es = ExitStack()
        self._frames = []
        self._sems = {}
        self.tot = defaultdict(int)
        self.seen = {e: defaultdict(int) for e in ENGS}
        self.block = None
        self.cvals = []
        self._cpool = None
        self.snaps = {}
        self.prefix = ""
        self.io = {}

    def push_frame(self, prefix=""):
        self._frames.append(ExitStack())
        self.prefix = prefix

    def pop_frame(self):
        self._frames.pop().close()
        self.prefix = ""

    def barrier(self):
        for engname in ENGS:
            sec = getattr(self.block, engname)

            def b(eng, engname=engname):
                self.final_wait(eng, engname)
            sec(b)

    def sem(self, name):
        if name not in self._sems:
            self._sems[name] = self.es.enter_context(self.nc.semaphore(name))
        return self._sems[name]

    def _stack(self):
        return self._frames[-1] if self._frames else self.es

    def sbuf(self, name, shape, dt):
        return self._stack().enter_context(self.nc.sbuf_tensor(self.prefix + name, shape, dt))

    def psum(self, name, shape, dt=F32):
        return self._stack().enter_context(self.nc.psum_tensor(self.prefix + name, shape, dt))

    def dram(self, name, shape, dt, kind="Internal"):
        if name in self.io:
            return self.io[name]
        pfx = self.prefix if kind == "Internal" else ""
        return self.nc.dram_tensor(pfx + name, shape, dt, kind=kind).ap()

    def cscalar(self, v):
        v = float(v)
        if self._cpool is None:
            self._cpool = self.es.enter_context(self.nc.sbuf_tensor("cpool", [128, CPOOL_N], F32))
        if v not in self.cvals:
            assert len(self.cvals) < CPOOL_N
            self.cvals.append(v)
            i = self.cvals.index(v)
            self.nc.const_aps.register(F32, v, self._cpool[:, i:i + 1])
        return v

    def inc(self, inst, name, n=1):
        inst.then_inc(self.sem(name), n)
        self.tot[name] += n
        return (name, self.tot[name])

    def tok(self, name):
        return (name, self.tot[name])

    def wait(self, eng, engname, *toks):
        for tk in toks:
            if tk is None:
                continue
            name, val = tk
            if val > 0 and self.seen[engname][name] < val:
                eng.wait_ge(self.sem(name), val)
                self.seen[engname][name] = val

    def snapshot(self, key):
        self.snaps[key] = dict(self.tot)

    def snap_tok(self, key, name):
        if key not in self.snaps:
            return None
        return (name, self.snaps[key].get(name, 0))

    def wait_snap(self, eng, engname, key, names):
        for n in names:
            self.wait(eng, engname, self.snap_tok(key, n))

    def final_wait(self, eng, engname):
        for name in list(self.tot.keys()):
            self.wait(eng, engname, (name, self.tot[name]))


def load_consts(p, names):
    cvals = host_constants()
    out = {}
    aps = {}
    dt_map = {np.dtype(np.float32): F32, np.dtype(BF): BF16}
    for n in names:
        a = cvals[n]
        aps[n] = p.dram("c_" + n, list(a.shape), dt_map[a.dtype], kind="ExternalInput")
        out[n] = p.sbuf("cs_" + n, list(a.shape), dt_map[a.dtype])
    cpool_in = p.dram("cpool_in", [128, CPOOL_N], F32, kind="ExternalInput")

    def emit(g):
        if p._cpool is None:
            p._cpool = p.es.enter_context(p.nc.sbuf_tensor("cpool", [128, CPOOL_N], F32))
        last = g.dma_start(out=p._cpool[:], in_=cpool_in[:, :])
        p.inc(last, "cld", 16)
        for n in names:
            last = g.dma_start(out=out[n][:], in_=aps[n][:, :] if aps[n].ndim == 2 else aps[n][:])
            p.inc(last, "cld", 16)
    return out, emit


def const_inputs(p, names):
    cvals = host_constants()
    d = {"c_" + n: cvals[n] for n in names}
    pool = np.zeros((128, CPOOL_N), dtype=np.float32)
    for i, v in enumerate(p.cvals):
        pool[:, i] = v
    d["cpool_in"] = pool
    return d


def emit_ln0(p, csb):
    x_in = p.dram("x_in", [T, D], F32, kind="ExternalInput")
    xacc = p.dram("xacc", [D, T], F32, kind="ExternalOutput")
    xn0 = p.dram("xn0", [T, D], BF16, kind="ExternalOutput")

    p.cscalar(1e-5)
    p.cscalar(-0.5)

    xt = [p.sbuf(f"ln_x{i}", [128, D], F32) for i in range(2)]
    xnb = [p.sbuf(f"ln_xn{i}", [128, D], BF16) for i in range(2)]
    sq = p.sbuf("ln_sq", [128, D], F32)
    cols = [p.sbuf(f"ln_cols{i}", [128, 8], F32) for i in range(2)]
    stg = [p.sbuf(f"ln_stg{i}", [128, 128], F32) for i in range(4)]
    pst = [p.psum(f"ln_pst{i}", [128, 128], F32) for i in range(2)]
    identf = csb["identf"]

    NT = T // 128
    for tc in range(NT):
        b = tc % 2

        @p.block.gpsimd
        def _(g):
            p.wait_snap(g, "gpsimd", tc - 2, [f"lnxn{b}", "lnev0", "lnev1", "s", "v", "xnst"])
            e = g.dma_start(out=xt[b][:], in_=x_in[tc * 128:(tc + 1) * 128, :])
            p.inc(e, f"lnld{b}", 16)

        @p.block.scalar
        def _(s):
            p.wait(s, "scalar", p.tok(f"lnld{b}"), p.tok("cld"))
            e = s.activation(sq[:], xt[b][:], AF.Square, accum_out=cols[b][:, 1:2])
            p.inc(e, "s", 1)

        @p.block.vector
        def _(v):
            p.wait(v, "vector", p.tok(f"lnld{b}"))
            e = v.reduce_sum(cols[b][:, 0:1], xt[b][:], axis=AX.X)
            tk = p.inc(e, "v", 1)
            p.wait(v, "vector", tk)
            e = v.tensor_scalar_mul(cols[b][:, 2:3], cols[b][:, 0:1], -1.0 / D)
            tk = p.inc(e, "v", 1)
            p.wait(v, "vector", tk, p.tok("s"))
            e = v.scalar_tensor_tensor(cols[b][:, 3:4], cols[b][:, 0:1], cols[b][:, 2:3], cols[b][:, 1:2], ALU.mult, ALU.add)
            tk = p.inc(e, "v", 1)
            p.wait(v, "vector", tk)
            e = v.tensor_scalar_mul(cols[b][:, 4:5], cols[b][:, 3:4], 1.0 / D)
            p.inc(e, "v", 1)

        @p.block.scalar
        def _(s):
            p.wait(s, "scalar", p.tok("v"))
            e = s.activation(cols[b][:, 6:7], cols[b][:, 4:5], AF.Ln, bias=1e-5)
            tk = p.inc(e, "s", 1)
            p.wait(s, "scalar", tk)
            e = s.activation(cols[b][:, 5:6], cols[b][:, 6:7], AF.Exp, scale=-0.5)
            p.inc(e, "s", 1)

        @p.block.vector
        def _(v):
            p.wait(v, "vector", p.tok("s"))
            e = v.tensor_scalar(xnb[b][:], xt[b][:], cols[b][:, 2:3], cols[b][:, 5:6], ALU.add, ALU.mult)
            p.inc(e, f"lnxn{b}", 1)

        @p.block.gpsimd
        def _(g):
            p.wait(g, "gpsimd", p.tok(f"lnxn{b}"))
            e = g.dma_start(out=xn0[tc * 128:(tc + 1) * 128, :], in_=xnb[b][:])
            p.inc(e, "xnst", 16)

        for kc in range(D // 128):
            pb = kc % 2
            sb = kc % 4
            idx = tc * 16 + kc

            @p.block.tensor
            def _(t):
                p.wait(t, "tensor", p.tok(f"lnld{b}"), p.tok("cld"))
                if idx >= 2:
                    p.wait(t, "tensor", (f"lnev{pb}", (idx - 2) // 2 + 1))
                e = t.transpose(pst[pb][:, :], xt[b][:, kc * 128:(kc + 1) * 128], identf[:, :])
                p.inc(e, f"lntr{pb}", 1)

            @p.block.vector
            def _(v):
                p.wait(v, "vector", p.tok(f"lntr{pb}"))
                if idx >= 4:
                    p.wait(v, "vector", (f"lnst{sb}", ((idx - 4) // 4 + 1) * 16))
                e = v.tensor_copy(stg[sb][:], pst[pb][:, :])
                p.inc(e, f"lnev{pb}", 1)

            @p.block.sync
            def _(sy):
                p.wait(sy, "sync", p.tok(f"lnev{pb}"))
                e = sy.dma_start(out=xacc[kc * 128:(kc + 1) * 128, tc * 128:(tc + 1) * 128], in_=stg[sb][:])
                p.inc(e, f"lnst{sb}", 16)

        p.snapshot(tc)

    @p.block.sync
    def _(sy):
        p.final_wait(sy, "sync")


def emit_final(p, csb, src_name="xacc_in"):
    xacc = p.dram(src_name, [D, T], F32, kind="ExternalInput")
    out = p.dram("y_out", [T, D], F32, kind="ExternalOutput")

    xt = [p.sbuf(f"fn_x{i}", [128, T], F32) for i in range(2)]
    stg = [p.sbuf(f"fn_stg{i}", [128, 128], F32) for i in range(4)]
    pst = [p.psum(f"fn_pst{i}", [128, 128], F32) for i in range(2)]
    identf = csb["identf"]

    ND = D // 128
    for dc in range(ND):
        b = dc % 2

        @p.block.gpsimd
        def _(g):
            p.wait_snap(g, "gpsimd", dc - 2, ["fnev0", "fnev1"])
            e = g.dma_start(out=xt[b][:], in_=xacc[dc * 128:(dc + 1) * 128, :])
            p.inc(e, f"fnld{b}", 16)

        for tc in range(T // 128):
            pb = tc % 2
            sb = tc % 4
            idx = dc * 16 + tc

            @p.block.tensor
            def _(t):
                p.wait(t, "tensor", p.tok(f"fnld{b}"), p.tok("cld"))
                if idx >= 2:
                    p.wait(t, "tensor", (f"fnev{pb}", (idx - 2) // 2 + 1))
                e = t.transpose(pst[pb][:, :], xt[b][:, tc * 128:(tc + 1) * 128], identf[:, :])
                p.inc(e, f"fntr{pb}", 1)

            @p.block.vector
            def _(v):
                p.wait(v, "vector", p.tok(f"fntr{pb}"))
                if idx >= 4:
                    p.wait(v, "vector", (f"fnst{sb}", ((idx - 4) // 4 + 1) * 16))
                e = v.tensor_copy(stg[sb][:], pst[pb][:, :])
                p.inc(e, f"fnev{pb}", 1)

            @p.block.sync
            def _(sy):
                p.wait(sy, "sync", p.tok(f"fnev{pb}"))
                e = sy.dma_start(out=out[tc * 128:(tc + 1) * 128, dc * 128:(dc + 1) * 128], in_=stg[sb][:])
                p.inc(e, f"fnst{sb}", 16)

        p.snapshot(dc)

    @p.block.sync
    def _(sy):
        p.final_wait(sy, "sync")


def build_ln0():
    p = Prog("ln0")
    with p.es:
        csb, cemit = load_consts(p, ["identf"])
        with p.nc.Block() as blk:
            p.block = blk

            @blk.gpsimd
            def _(g):
                cemit(g)
            emit_ln0(p, csb)
    return p


def build_final():
    p = Prog("final")
    with p.es:
        csb, cemit = load_consts(p, ["identf"])
        with p.nc.Block() as blk:
            p.block = blk

            @blk.gpsimd
            def _(g):
                cemit(g)
            emit_final(p, csb)
    return p


def emit_prep_cast(p, g, src_ap, dst_ap, tokname, chunk_rows=4096):
    n = src_ap.shape[0]
    r = 0
    while r < n:
        m = min(chunk_rows, n - r)
        e = g.dma_start(out=dst_ap[r:r + m, :], in_=src_ap[r:r + m, :])
        p.inc(e, tokname, 16)
        r += m


def emit_load_xn(p, g_or_sy, xn_ap, xn_sb, tokname, tm=True):
    for dc in range(D // 128):
        if tm:
            e = g_or_sy.dma_start(out=xn_sb[:, dc * T:(dc + 1) * T],
                                  in_=xn_ap[0:T, dc * 128:(dc + 1) * 128], transpose=True)
        else:
            e = g_or_sy.dma_start(out=xn_sb[:, dc * T:(dc + 1) * T],
                                  in_=xn_ap[dc * 128:(dc + 1) * 128, :])
        p.inc(e, tokname, 16)


def emit_prep_cast(p, g, src_ap, dst_ap, tokname, chunk_rows=4096):
    n = src_ap.shape[0]
    r = 0
    while r < n:
        m = min(chunk_rows, n - r)
        e = g.dma_start(out=dst_ap[r:r + m, :], in_=src_ap[r:r + m, :])
        p.inc(e, tokname, 16)
        r += m


def emit_load_xn(p, eng, xn_ap, xn_sb, tokname, tm=True):
    for dc in range(D // 128):
        if tm:
            e = eng.dma_start(out=xn_sb[:, dc * T:(dc + 1) * T],
                              in_=xn_ap[0:T, dc * 128:(dc + 1) * 128], transpose=True)
        else:
            e = eng.dma_start(out=xn_sb[:, dc * T:(dc + 1) * T],
                              in_=xn_ap[dc * 128:(dc + 1) * 128, :])
        p.inc(e, tokname, 16)


class PsumPool:
    def __init__(self, p, name, n, shape=(128, 512)):
        self.p = p
        self.tiles = [p.psum(f"{name}{i}", list(shape)) for i in range(n)]
        self.free_tok = [None] * n
        self.i = 0

    def acquire(self):
        i = self.i
        self.i = (self.i + 1) % len(self.tiles)
        return i, self.tiles[i], self.free_tok[i]

    def release(self, i, tok):
        self.free_tok[i] = tok


def emit_l0_moe(p, csb):
    NH = 1024
    HS = 2048
    xn0 = p.dram("xn0_in", [T, D], BF16, kind="ExternalInput")
    xacc_in = p.dram("xacc_in", [D, T], F32, kind="ExternalInput")
    xacc = p.dram("xacc", [D, T], F32, kind="ExternalOutput")
    gate_w = p.dram("gate_w", [8, D], F32, kind="ExternalInput")
    w_gate = p.dram("w_gate", [8, NH, D], F32, kind="ExternalInput")
    w_up = p.dram("w_up", [8, NH, D], F32, kind="ExternalInput")
    w_down = p.dram("w_down", [8, D, NH], F32, kind="ExternalInput")
    sh_gate = p.dram("sh_gate", [HS, D], F32, kind="ExternalInput")
    sh_up = p.dram("sh_up", [HS, D], F32, kind="ExternalInput")
    sh_down = p.dram("sh_down", [D, HS], F32, kind="ExternalInput")
    se_gate_w = p.dram("se_gate_w", [1, D], F32, kind="ExternalInput")

    wg_bf = p.dram("wg_bf", [8 * NH, D], BF16)
    wu_bf = p.dram("wu_bf", [8 * NH, D], BF16)
    wd_bf = p.dram("wd_bf", [8 * D, NH], BF16)
    shg_bf = p.dram("shg_bf", [HS, D], BF16)
    shu_bf = p.dram("shu_bf", [HS, D], BF16)
    shd_bf = p.dram("shd_bf", [D, HS], BF16)

    identf = csb["identf"]
    ones_row = csb["ones_row"]

    xn_sb = p.sbuf("moe_xn", [128, 16 * T], BF16)
    gsc_sb = p.sbuf("moe_gsc", [9, D], F32)
    gwT = p.sbuf("moe_gwT", [128, 16 * 9], BF16)
    sc = p.sbuf("moe_sc", [128, 18], F32)
    eqb = p.sbuf("moe_eqb", [128, 32], F32)
    combtm = p.sbuf("moe_combtm", [128, 16], F32)
    combT = p.sbuf("moe_combT", [16, T], F32)
    crow1 = p.sbuf("moe_crow1", [1, T], BF16)
    wub = p.sbuf("moe_wub", [128, 16 * 512], BF16)
    wgb = p.sbuf("moe_wgb", [128, 16 * 512], BF16)
    wdb = p.sbuf("moe_wdb", [128, 16 * 512], BF16)
    h_sb = p.sbuf("moe_h", [128, 16 * 1024], BF16)
    sgs = [p.sbuf(f"moe_sg{i}", [128, 512], F32) for i in range(2)]
    tps = [p.sbuf(f"moe_tp{i}", [128, 512], F32) for i in range(2)]
    stg = [p.sbuf(f"moe_stg{i}", [128, 512], F32) for i in range(4)]

    pool = PsumPool(p, "moe_pp", 5)
    cps = [p.psum(f"moe_cps{i}", [128, 512], F32) for i in range(2)]
    arena = p.psum("moe_ar", [128, 512], F32)

    @p.block.gpsimd
    def _(g):
        e = g.dma_start(out=gsc_sb[0:8, :], in_=gate_w[:, :])
        p.inc(e, "gsc", 16)
        e = g.dma_start(out=gsc_sb[8:9, :], in_=se_gate_w[:, :])
        p.inc(e, "gsc", 16)
        emit_prep_cast(p, g, w_gate.rearrange("e h d -> (e h) d"), wg_bf, "pw_ug")
        emit_prep_cast(p, g, w_up.rearrange("e h d -> (e h) d"), wu_bf, "pw_ug")
        emit_prep_cast(p, g, sh_gate, shg_bf, "pw_ug")
        emit_prep_cast(p, g, sh_up, shu_bf, "pw_ug")
        emit_prep_cast(p, g, w_down.rearrange("e d h -> (e d) h"), wd_bf, "pw_d")
        emit_prep_cast(p, g, sh_down, shd_bf, "pw_d")
        for i in range(4):
            e = g.dma_start(out=xacc[i * 512:(i + 1) * 512, :], in_=xacc_in[i * 512:(i + 1) * 512, :])
            p.inc(e, "cpacc", 16)

    @p.block.sync
    def _(sy):
        emit_load_xn(p, sy, xn0, xn_sb, "xnld", tm=True)

    for kc in range(16):
        @p.block.tensor
        def _(t):
            p.wait(t, "tensor", p.tok("gsc"), p.tok("cld"))
            if kc >= 1:
                p.wait(t, "tensor", p.tok("gscev"))
            e = t.transpose(arena[:, 0:9], gsc_sb[0:9, kc * 128:(kc + 1) * 128], identf[0:9, 0:9])
            p.inc(e, "gsctr", 1)

        @p.block.vector
        def _(v):
            p.wait(v, "vector", p.tok("gsctr"))
            e = v.tensor_copy(gwT[:, kc * 9:(kc + 1) * 9], arena[0:128, 0:9] if False else arena[:, 0:9])
            p.inc(e, "gscev", 1)

    for tc in range(16):
        @p.block.tensor
        def _(t):
            p.wait(t, "tensor", p.tok("xnld"), p.tok("gscev"))
            if tc > 0:
                p.wait(t, "tensor", p.tok("ctrev"))
            e = None
            for kc in range(16):
                e = t.matmul(arena[:, 0:9], xn_sb[:, kc * T + tc * 128: kc * T + tc * 128 + 128],
                             gwT[:, kc * 9:(kc + 1) * 9], start=(kc == 0), stop=(kc == 15))
            p.inc(e, "scmm", 1)

        @p.block.vector
        def _(v):
            p.wait(v, "vector", p.tok("scmm"))
            e = v.tensor_copy(sc[:, 0:9], arena[:, 0:9])
            tk = p.inc(e, "scev", 1)
            p.wait(v, "vector", tk)
            e = v.reduce_max(sc[:, 12:13], sc[:, 0:8], axis=AX.X)
            tk = p.inc(e, "v", 1)
            p.wait(v, "vector", tk)
            e = v.tensor_scalar(eqb[:, 0:8], sc[:, 0:8], sc[:, 12:13], None, ALU.is_equal)
            tk = p.inc(e, "v", 1)
            p.wait(v, "vector", tk)
            e = v.scalar_tensor_tensor(eqb[:, 8:16], eqb[:, 0:8], float(NEG), sc[:, 0:8], ALU.mult, ALU.add)
            tk = p.inc(e, "v", 1)
            p.wait(v, "vector", tk)
            e = v.reduce_max(sc[:, 13:14], eqb[:, 8:16], axis=AX.X)
            tk = p.inc(e, "v", 1)
            p.wait(v, "vector", tk)
            e = v.tensor_scalar(eqb[:, 16:24], eqb[:, 8:16], sc[:, 13:14], None, ALU.is_equal)
            tk = p.inc(e, "v", 1)
            p.wait(v, "vector", tk)
            e = v.tensor_sub(sc[:, 14:15], sc[:, 12:13], sc[:, 13:14])
            p.inc(e, "sigin", 1)

        @p.block.scalar
        def _(s):
            p.wait(s, "scalar", p.tok("sigin"))
            e = s.activation(sc[:, 15:16], sc[:, 14:15], AF.Sigmoid)
            p.inc(e, "s", 1)
            e = s.activation(combtm[:, 8:9], sc[:, 8:9], AF.Sigmoid)
            p.inc(e, "s", 1)

        @p.block.vector
        def _(v):
            p.wait(v, "vector", p.tok("s"))
            e = v.tensor_scalar(sc[:, 16:17], sc[:, 15:16], -1.0, 1.0, ALU.mult, ALU.add)
            tk = p.inc(e, "v", 1)
            p.wait(v, "vector", tk)
            e = v.tensor_scalar(eqb[:, 24:32], eqb[:, 0:8], sc[:, 15:16], None, ALU.mult)
            tk = p.inc(e, "v", 1)
            p.wait(v, "vector", tk)
            e = v.scalar_tensor_tensor(combtm[:, 0:8], eqb[:, 16:24], sc[:, 16:17], eqb[:, 24:32], ALU.mult, ALU.add)
            p.inc(e, "combv", 1)
            if tc == 0:
                e = v.memset(combtm[:, 9:16], 0.0)
                p.inc(e, "combv", 1)

        @p.block.tensor
        def _(t):
            p.wait(t, "tensor", p.tok("combv"))
            e = t.transpose(arena[0:16, 128:256], combtm[:, 0:16], identf[:, :])
            p.inc(e, "ctr", 1)

        @p.block.vector
        def _(v):
            p.wait(v, "vector", p.tok("ctr"))
            e = v.tensor_copy(combT[:, tc * 128:(tc + 1) * 128], arena[0:16, 128:256])
            p.inc(e, "ctrev", 1)


    GROUPS = globals().get("MOE_GROUPS") or [([0, 1], False), ([2, 3], False), ([4, 5], False), ([6, 7], False), ([], True)]
    last_upgate_mm = None
    last_down_mm = None
    cps_free = [None, None]
    sg_free = [None, None]
    tp_free = [None, None]
    stg_free = [None] * 4

    for th in range(2):
        for (experts, has_sh) in GROUPS:
            units = []
            for ei, e_ in enumerate(experts):
                units.append((wg_bf, wu_bf, e_ * NH, e_, ei * 8))
            if has_sh:
                units.append((shg_bf, shu_bf, 0, 8, len(experts) * 8))
                units.append((shg_bf, shu_bf, NH, 8, len(experts) * 8 + 8))
            nj = len(units) * 8

            for (gsrc, usrc, rlo, crow_q, jbase) in units:
                @p.block.gpsimd
                def _(g):
                    p.wait(g, "gpsimd", p.tok("ctrev"))
                    if cps_free[0] is not None:
                        p.wait(g, "gpsimd", cps_free[0])
                    if cps_free[1] is not None:
                        p.wait(g, "gpsimd", cps_free[1])
                    e = g.dma_start(out=crow1[0:1, :], in_=combT[crow_q:crow_q + 1, :])
                    p.inc(e, "crowld", 16)

                @p.block.tensor
                def _(t):
                    p.wait(t, "tensor", p.tok("crowld"))
                    for tt in range(2):
                        p.wait(t, "tensor", cps_free[tt])
                        e = t.matmul(cps[tt][:, :], ones_row[:, :],
                                     crow1[0:1, th * 1024 + tt * 512: th * 1024 + tt * 512 + 512],
                                     start=True, stop=True)
                        p.inc(e, "t", 1)

                for oh in range(2):
                    @p.block.sync
                    def _(sy):
                        p.wait(sy, "sync", p.tok("pw_ug"))
                        if last_upgate_mm is not None:
                            p.wait(sy, "sync", last_upgate_mm)
                        if last_down_mm is not None:
                            p.wait(sy, "sync", last_down_mm)
                        for kc in range(16):
                            e = sy.dma_start(out=wgb[:, kc * 512:(kc + 1) * 512],
                                             in_=gsrc[rlo + oh * 512:rlo + oh * 512 + 512,
                                                      kc * 128:(kc + 1) * 128], transpose=True)
                            p.inc(e, "wug", 16)
                            e = sy.dma_start(out=wub[:, kc * 512:(kc + 1) * 512],
                                             in_=usrc[rlo + oh * 512:rlo + oh * 512 + 512,
                                                      kc * 128:(kc + 1) * 128], transpose=True)
                            p.inc(e, "wug", 16)

                    for oc4 in range(4):
                        oc = oh * 4 + oc4
                        for tt in range(2):
                            bg, gps, gfree = pool.acquire()
                            bu, ups, ufree = pool.acquire()
                            col0 = th * 1024 + tt * 512

                            @p.block.tensor
                            def _(t):
                                p.wait(t, "tensor", p.tok("wug"), gfree, ufree)
                                e = None
                                for kc in range(16):
                                    e = t.matmul(gps[:, :], wgb[:, kc * 512 + oc4 * 128: kc * 512 + oc4 * 128 + 128],
                                                 xn_sb[:, kc * T + col0: kc * T + col0 + 512],
                                                 start=(kc == 0), stop=(kc == 15))
                                tkg = p.inc(e, "t", 1)
                                for kc in range(16):
                                    e = t.matmul(ups[:, :], wub[:, kc * 512 + oc4 * 128: kc * 512 + oc4 * 128 + 128],
                                                 xn_sb[:, kc * T + col0: kc * T + col0 + 512],
                                                 start=(kc == 0), stop=(kc == 15))
                                tku = p.inc(e, "t", 1)
                                globals()["_moe_tkg"], globals()["_moe_tku"] = tkg, tku

                            tkg, tku = globals()["_moe_tkg"], globals()["_moe_tku"]

                            @p.block.scalar
                            def _(s):
                                p.wait(s, "scalar", tkg, sg_free[tt])
                                e = s.activation(sgs[tt][:], gps[:, :], AF.Silu)
                                globals()["_moe_tks"] = p.inc(e, "s", 1)

                            tks = globals()["_moe_tks"]

                            @p.block.vector
                            def _(v):
                                p.wait(v, "vector", tks, tku, tp_free[tt])
                                if last_down_mm is not None:
                                    p.wait(v, "vector", last_down_mm)
                                e = v.tensor_tensor(tps[tt][:], sgs[tt][:], ups[:, :], ALU.mult)
                                tkv = p.inc(e, "v", 1)
                                p.wait(v, "vector", tkv)
                                e = v.tensor_tensor(h_sb[:, (jbase + oc) * 1024 + tt * 512: (jbase + oc) * 1024 + tt * 512 + 512],
                                                    tps[tt][:], cps[tt][:, :], ALU.mult)
                                tkh = p.inc(e, "hwr", 1)
                                globals()["_moe_tkv"], globals()["_moe_tkh"] = tkv, tkh

                            tkv, tkh = globals()["_moe_tkv"], globals()["_moe_tkh"]
                            pool.release(bg, tks)
                            pool.release(bu, tkv)
                            sg_free[tt] = tkv
                            tp_free[tt] = tkh
                            cps_free[tt] = tkh
                    last_upgate_mm = p.tok("t")
                last_upgate_mm = p.tok("t")

            h_ready = p.tok("hwr")
            if globals().get("DBG_MOE") and th == 0 and experts[:1] == [0]:
                dbg_h0 = p.dram("dbg_h0", [128, 16 * 1024], F32, kind="ExternalOutput")

                @p.block.gpsimd
                def _(g):
                    p.wait(g, "gpsimd", h_ready)
                    e = g.dma_start(out=dbg_h0[:, :], in_=h_sb[:, :])
                    p.inc(e, "dbg", 16)

            acc_base = p.tok("acc")
            prev_og_mm = last_down_mm
            for og in range(4):
                @p.block.sync
                def _(sy):
                    p.wait(sy, "sync", p.tok("pw_d"))
                    if prev_og_mm is not None:
                        p.wait(sy, "sync", prev_og_mm)
                    jj = 0
                    for (gsrc, usrc, rlo, crow_q, jbase) in units:
                        if gsrc is wg_bf:
                            e_ = rlo // NH
                            for kc in range(8):
                                e = sy.dma_start(out=wdb[:, jj * 512:(jj + 1) * 512],
                                                 in_=wd_bf[e_ * D + og * 512: e_ * D + og * 512 + 512,
                                                           kc * 128:(kc + 1) * 128], transpose=True)
                                p.inc(e, "wd", 16)
                                jj += 1
                        else:
                            for kc in range(rlo // 128, rlo // 128 + 8):
                                e = sy.dma_start(out=wdb[:, jj * 512:(jj + 1) * 512],
                                                 in_=shd_bf[og * 512:(og + 1) * 512, kc * 128:(kc + 1) * 128],
                                                 transpose=True)
                                p.inc(e, "wd", 16)
                                jj += 1

                for ol in range(4):
                    for tt in range(2):
                        bk, ps_, free = pool.acquire()

                        @p.block.tensor
                        def _(t):
                            p.wait(t, "tensor", p.tok("wd"), free, h_ready)
                            e = None
                            for j in range(nj):
                                e = t.matmul(ps_[:, :], wdb[:, j * 512 + ol * 128: j * 512 + ol * 128 + 128],
                                             h_sb[:, j * 1024 + tt * 512: j * 1024 + tt * 512 + 512],
                                             start=(j == 0), stop=(j == nj - 1))
                            globals()["_moe_tkm"] = p.inc(e, "t", 1)

                        tkm = globals()["_moe_tkm"]
                        sb_i = (og * 8 + ol * 2 + tt) % 4

                        @p.block.scalar
                        def _(s):
                            p.wait(s, "scalar", tkm, stg_free[sb_i])
                            e = s.copy(stg[sb_i][:], ps_[:, :])
                            globals()["_moe_tke"] = p.inc(e, "dnev", 1)

                        tke = globals()["_moe_tke"]
                        pool.release(bk, tke)

                        @p.block.gpsimd
                        def _(g):
                            p.wait(g, "gpsimd", tke, p.tok("cpacc"), acc_base)
                            e = g.dma_start(out=xacc[og * 512 + ol * 128: og * 512 + ol * 128 + 128,
                                                     th * 1024 + tt * 512: th * 1024 + tt * 512 + 512],
                                            in_=stg[sb_i][:], accum_op=ALU.add)
                            stg_free[sb_i] = p.inc(e, "acc", 16)
                prev_og_mm = p.tok("t")
            last_down_mm = p.tok("t")

    if globals().get("DBG_MOE"):
        dbg_comb = p.dram("dbg_comb", [16, T], F32, kind="ExternalOutput")
        dbg_h = p.dram("dbg_h", [128, 16 * 1024], F32, kind="ExternalOutput")
        dbg_xn = p.dram("dbg_xn", [128, T], F32, kind="ExternalOutput")

        dbg_sc = p.dram("dbg_sc", [128, 18], F32, kind="ExternalOutput")
        dbg_eqb = p.dram("dbg_eqb", [128, 32], F32, kind="ExternalOutput")

        @p.block.gpsimd
        def _(g):
            p.final_wait(g, "gpsimd")
            e = g.dma_start(out=dbg_comb[:, :], in_=combT[:, :])
            p.inc(e, "dbg", 16)
            e = g.dma_start(out=dbg_xn[:, :], in_=xn_sb[:, 0:T])
            p.inc(e, "dbg", 16)
            e = g.dma_start(out=dbg_sc[:, :], in_=sc[:, :])
            p.inc(e, "dbg", 16)
            e = g.dma_start(out=dbg_eqb[:, :], in_=eqb[:, :])
            p.inc(e, "dbg", 16)

    @p.block.sync
    def _(sy):
        p.final_wait(sy, "sync")


DBG_MOE = False


def build_l0_moe():
    p = Prog("l0moe")
    with p.es:
        csb, cemit = load_consts(p, ["identf", "ones_row"])
        with p.nc.Block() as blk:
            p.block = blk

            @blk.gpsimd
            def _(g):
                cemit(g)
            emit_l0_moe(p, csb)
    return p


def _chain(p, steps, nobar=False):
    prev = None
    first = True
    for engname, fn, extra in steps:
        sec = getattr(p.block, engname)
        box = {}
        bar = [p.tok("cht"), p.tok("chv"), p.tok("chs")] if (first and not nobar) else []
        first = False

        def body(eng, fn=fn, extra=extra, prevtok=prev, engname=engname, box=box, bar=bar):
            p.wait(eng, engname, prevtok, *(list(extra) + bar))
            inst = fn(eng)
            if inst is not None:
                n = 16 if engname in ("gpsimd", "sync") else 1
                box["tok"] = p.inc(inst, "ch" + engname[0], n)
            else:
                box["tok"] = prevtok
        sec(body)
        prev = box["tok"]
    return prev


def preload_set6(s):
    return s.add_instruction(mybir.InstLoadActFuncSet(
        name=s.bass.get_next_instruction_name(), act_func_set_id=6, ins=[], outs=[]))


def _chains_zip(p, chains, bar_extra=()):
    prevs = [None] * len(chains)
    first = [True] * len(chains)
    bar = [p.tok("cht"), p.tok("chv"), p.tok("chs")] + list(bar_extra)
    nmax = max(len(c) for c in chains)
    for i in range(nmax):
        for ci, c in enumerate(chains):
            if i >= len(c):
                continue
            engname, fn, extra = c[i]
            sec = getattr(p.block, engname)
            box = {}
            b = bar if first[ci] else []
            first[ci] = False

            def body(eng, fn=fn, extra=extra, prevtok=prevs[ci], engname=engname, box=box, b=b):
                p.wait(eng, engname, prevtok, *(list(extra) + b))
                inst = fn(eng)
                if inst is not None:
                    n = 16 if engname in ("gpsimd", "sync") else 1
                    box["tok"] = p.inc(inst, "ch" + engname[0], n)
                else:
                    box["tok"] = prevtok
            sec(body)
            prevs[ci] = box["tok"]
    return prevs


def emit_l0_attn(p, csb):
    NK, DK, DV, PP, KC = 16, 128, 128, 2, 4
    QD = NK * DK
    VD = 2 * NK * DV
    xn0 = p.dram("xn0_in", [T, D], BF16, kind="ExternalInput")
    xacc_in = p.dram("xacc_in", [D, T], F32, kind="ExternalInput")
    xacc = p.dram("xacc", [D, T], F32, kind="ExternalOutput")
    W_qkv = p.dram("W_qkv", [2 * QD + VD, D], F32, kind="ExternalInput")
    W_z = p.dram("W_z", [VD, D], F32, kind="ExternalInput")
    W_b = p.dram("W_b", [32, D], F32, kind="ExternalInput")
    W_a = p.dram("W_a", [32, D], F32, kind="ExternalInput")
    conv_w = p.dram("conv_w", [2 * QD + VD, KC], F32, kind="ExternalInput")
    dt_bias = p.dram("dt_bias", [32], F32, kind="ExternalInput")
    A_log = p.dram("A_log", [32], F32, kind="ExternalInput")
    norm_w = p.dram("norm_w", [128], F32, kind="ExternalInput")
    W_out = p.dram("W_out", [D, VD], F32, kind="ExternalInput")

    wqkv_bf = p.dram("wqkv_bf", [2 * QD + VD, D], BF16)
    wz_bf = p.dram("wz_bf", [VD, D], BF16)
    wout_bf = p.dram("wout_bf", [D, VD], BF16)

    identf = csb["identf"]
    identb = csb["identb"]
    ones_row = csb["ones_row"]
    onesf_row = csb["onesf_row"]
    onescol_b = csb["onescol_b"]
    m_ui = csb["m_upincl_neg"]
    m_us = csb["m_upstrict_neg"]
    m_ls = csb["m_lowstrict_neg"]
    negonesf_row = csb["negonesf_row"]
    twoI = csb["twoI"]

    for v_ in (1e-6, 1.0 / 128, -0.5, -1.0, 1.0):
        p.cscalar(v_)

    xn_sb = p.sbuf("da_xn", [128, 16 * T], BF16)
    baT = p.sbuf("da_baT", [128, 16 * 64], BF16)
    qrow1 = p.sbuf("da_qrow1", [128, T], F32)
    qrow2 = p.sbuf("da_qrow2", [64, T], F32)
    cols_all = p.sbuf("da_cols", [128, 16 * 192], F32)
    rows_buf = p.sbuf("da_rows", [1, 1024], F32)
    cw_sb = p.sbuf("da_cw", [128, 64 * 4], F32)
    misc = p.sbuf("da_misc", [128, 16], F32)
    nw_rep = p.sbuf("da_nw", [128, 128], F32)
    nw_row = p.sbuf("da_nwrow", [1, 128], F32)
    wq_l = p.sbuf("da_wq", [128, 16 * 128], BF16)
    wk_l = p.sbuf("da_wk", [128, 16 * 128], BF16)
    wv_l = p.sbuf("da_wv", [128, 16 * 256], BF16)
    wz_l = p.sbuf("da_wz", [128, 16 * 256], BF16)
    qraw = p.sbuf("da_qraw", [128, T + 4], BF16)
    qc = p.sbuf("da_qc", [128, T], BF16)
    kc_ = p.sbuf("da_kc", [128, T], BF16)
    vc = [p.sbuf(f"da_vc{i}", [128, T], BF16) for i in range(2)]
    ssrow = p.sbuf("da_ssrow", [1, T], F32)
    zsil = p.sbuf("da_zsil", [128, 2 * T], BF16)
    basb = zsil[0:64, 0:4096].bitcast(F32)
    S_bf = p.sbuf("da_S", [128, 32 * 128], BF16)
    g_sb = p.sbuf("da_g", [128, 4 * T], BF16)
    gv = g_sb[0:32, :].bitcast(F32)
    wodb = p.sbuf("da_wod", [128, 4 * 512], BF16)
    stg = [p.sbuf(f"da_stg{i}", [128, 512], F32) for i in range(2)]
    KK_sb = p.sbuf("da_KK", [128, 128], F32)
    KQ_sb = p.sbuf("da_KQ", [128, 128], F32)
    KT_sb = p.sbuf("da_KT", [128, 128], BF16)
    VT_sb = [p.sbuf(f"da_VT{i}", [128, 128], BF16) for i in range(2)]
    Gt = [p.sbuf(f"da_Gt{i}", [128, 128], F32) for i in range(2)]
    Et = [p.sbuf(f"da_Et{i}", [128, 128], F32) for i in range(2)]
    MT_sb = [p.sbuf(f"da_MT{i}", [128, 128], BF16) for i in range(2)]
    X_sb = [p.sbuf(f"da_X{i}", [128, 128], BF16) for i in range(2)]
    XT_sb = [p.sbuf(f"da_XT{i}", [128, 128], BF16) for i in range(2)]
    W2_sb = [p.sbuf(f"da_W2{i}", [128, 128], BF16) for i in range(2)]
    R_sb = [p.sbuf(f"da_R{i}", [128, 128], BF16) for i in range(2)]
    U_sb = [p.sbuf(f"da_U{i}", [128, 128], BF16) for i in range(2)]
    U2_sb = [p.sbuf(f"da_U2{i}", [128, 128], BF16) for i in range(2)]
    O_sb = [p.sbuf(f"da_O{i}", [128, 128], F32) for i in range(2)]
    On_sb = [p.sbuf(f"da_On{i}", [128, 128], BF16) for i in range(2)]

    pool = PsumPool(p, "da_pp", 3)
    bk0 = p.psum("da_bk0", [128, 512], F32)
    bkA = [p.psum(f"da_bkA{i}", [128, 512], F32) for i in range(2)]
    bkB = [p.psum(f"da_bkB{i}", [128, 512], F32) for i in range(2)]

    @p.block.gpsimd
    def _(g):
        emit_prep_cast(p, g, W_qkv, wqkv_bf, "pw_qkv")
        emit_prep_cast(p, g, W_z, wz_bf, "pw_z")
        emit_prep_cast(p, g, W_out, wout_bf, "pw_o")
        e = g.dma_start(out=basb[0:32, :], in_=W_b[:, :])
        p.inc(e, "mld", 16)
        e = g.dma_start(out=basb[32:64, :], in_=W_a[:, :])
        p.inc(e, "mld", 16)
        e = g.dma_start(out=cw_sb[:, :].rearrange("p (c j) -> p c j", j=4), in_=conv_w.rearrange("(c q) j -> q c j", q=128))
        p.inc(e, "mld", 16)
        e = g.dma_start(out=misc[0:32, 0:1], in_=dt_bias[:, None])
        p.inc(e, "mld", 16)
        e = g.dma_start(out=misc[0:32, 1:2], in_=A_log[:, None])
        p.inc(e, "mld", 16)
        e = g.dma_start(out=nw_row[0:1, :], in_=norm_w[None, :])
        p.inc(e, "mld", 16)
        for i in range(4):
            e = g.dma_start(out=xacc[i * 512:(i + 1) * 512, :], in_=xacc_in[i * 512:(i + 1) * 512, :])
            p.inc(e, "cpacc", 16)
        e = g.memset(qraw[:, 0:4], 0.0)
        p.inc(e, "mld", 1)

    @p.block.sync
    def _(sy):
        emit_load_xn(p, sy, xn0, xn_sb, "xnld", tm=True)

    _chain(p, [
        ("tensor", lambda t: t.matmul(bk0[:, 0:128], onesf_row[:, :], nw_row[0:1, :], start=True, stop=True), [p.tok("mld"), p.tok("cld")]),
        ("vector", lambda v: v.tensor_copy(nw_rep[:, :], bk0[:, 0:128]), []),
    ])

    for kcc in range(16):
        _chain(p, [
            ("tensor", lambda t, kcc=kcc: t.transpose(bk0.bitcast(BF16)[:, 512:512 + 64] if False else bk0[:, 128:192], basb[0:64, kcc * 128:(kcc + 1) * 128], identf[0:64, 0:64]), [p.tok("mld"), p.tok("cld")]),
            ("vector", lambda v, kcc=kcc: v.tensor_copy(baT[:, kcc * 64:(kcc + 1) * 64], bk0[:, 128:192]), []),
        ])

    _regions = [gv[:, 0:T], gv[:, T:2 * T],
                zsil[0:32, :].bitcast(F32), S_bf[0:32, :].bitcast(F32)]

    def Ar(s0, s1=None):
        r = _regions[s0]
        return r if s1 is None else r[:, s1[0]:s1[1]]
    for tt in range(4):
        bnk, ps_, free = pool.acquire()

        def mmfn(t, tt=tt, ps_=ps_, free=free):
            p.wait(t, "tensor", free, p.tok("xnld"))
            e = None
            for kcc in range(16):
                e = t.matmul(ps_[0:64, 0:512], baT[:, kcc * 64:(kcc + 1) * 64],
                             xn_sb[:, kcc * T + tt * 512: kcc * T + tt * 512 + 512],
                             start=(kcc == 0), stop=(kcc == 15))
            return e
        tk = _chain(p, [
            ("tensor", mmfn, []),
            ("vector", lambda v, tt=tt, ps_=ps_: v.tensor_copy(Ar(0, (tt * 512, (tt + 1) * 512)), ps_[0:32, 0:512]), []),
            ("vector", lambda v, tt=tt, ps_=ps_: v.tensor_copy(Ar(1, (tt * 512, (tt + 1) * 512)), ps_[32:64, 0:512]), []),
        ])
        pool.release(bnk, tk)

    _chain(p, [
        ("scalar", lambda s: s.activation(Ar(3), Ar(0), AF.Sigmoid), []),
        ("vector", lambda v: v.tensor_scalar_mul(qrow1[64:96, :], Ar(3), -1.0), []),
        ("scalar", lambda s: s.activation(Ar(2), Ar(1), AF.Abs, bias=misc[0:32, 0:1]), []),
        ("scalar", lambda s: s.activation(Ar(2), Ar(2), AF.Exp, scale=-1.0), []),
        ("scalar", lambda s: s.activation(Ar(2), Ar(2), AF.Ln, bias=1.0), []),
        ("vector", lambda v: v.tensor_scalar(Ar(3), Ar(1), misc[0:32, 0:1], 0.0, ALU.add, ALU.max), []),
        ("vector", lambda v: v.tensor_add(Ar(1), Ar(2), Ar(3)), []),
        ("scalar", lambda s: s.activation(misc[0:32, 2:3], misc[0:32, 1:2], AF.Exp), []),
        ("vector", lambda v: v.tensor_scalar_mul(misc[0:32, 3:4], misc[0:32, 2:3], -1.0), []),
        ("vector", lambda v: v.tensor_scalar(Ar(2), Ar(1), misc[0:32, 3:4], None, ALU.mult), []),
    ])
    for c in range(16):
        sl = slice(c * 128, (c + 1) * 128)
        _chain(p, [
            ("vector", lambda v, sl=sl: v.tensor_tensor_scan(qrow1[0:32, sl], Ar(2)[:, sl], Ar(2)[:, sl], 0.0, ALU.add, ALU.bypass), []),
            ("vector", lambda v, sl=sl, c=c: v.tensor_scalar(Ar(3)[:, sl], qrow1[0:32, sl], -1.0, qrow1[0:32, c * 128 + 127:c * 128 + 128], ALU.mult, ALU.add), []),
            ("scalar", lambda s, sl=sl: s.activation(qrow2[32:64, sl], Ar(3)[:, sl], AF.Exp), []),
        ])
    _chain(p, [
        ("scalar", lambda s: s.activation(qrow2[0:32, :], qrow1[0:32, :], AF.Exp), []),
    ])
    for c in range(16):
        _chain(p, [
            ("vector", lambda v, c=c: v.memset(qrow1[32:64, c * 128:c * 128 + 1], 0.0), []),
            ("vector", lambda v, c=c: v.tensor_copy(qrow1[32:64, c * 128 + 1:(c + 1) * 128], qrow1[0:32, c * 128:(c + 1) * 128 - 1]), []),
            ("vector", lambda v, c=c: v.memset(qrow1[96:128, c * 128:c * 128 + 1], 1.0), []),
            ("vector", lambda v, c=c: v.tensor_copy(qrow1[96:128, c * 128 + 1:(c + 1) * 128], qrow2[0:32, c * 128:(c + 1) * 128 - 1]), []),
        ])
    for c in range(16):
        _chain(p, [
            ("tensor", lambda t, c=c: t.transpose(bk0[:, 128:256], qrow1[:, c * 128:(c + 1) * 128], identf[:, :]), []),
            ("vector", lambda v, c=c: v.tensor_copy(cols_all[:, c * 192: c * 192 + 128], bk0[:, 128:256]), []),
            ("tensor", lambda t, c=c: t.transpose(bk0[:, 256:320], qrow2[0:64, c * 128:(c + 1) * 128], identf[0:64, 0:64]), []),
            ("vector", lambda v, c=c: v.tensor_copy(cols_all[:, c * 192 + 128: c * 192 + 192], bk0[:, 256:320]), []),
        ])

    _chain(p, [("vector", lambda v: v.memset(S_bf[:, :], 0.0), [])])

    def col(q, vh, c):
        off = {"lA": 0, "lB": 32, "negb": 64, "elB": 96, "elA": 128, "w": 160}[q]
        return cols_all[:, c * 192 + off + vh: c * 192 + off + vh + 1]

    last_inproj_mm = None
    acc_base = p.tok("acc")
    for h in range(NK):
        hg = h % 2

        @p.block.sync
        def _(sy):
            p.wait(sy, "sync", p.tok("pw_qkv"), p.tok("pw_z"))
            if last_inproj_mm is not None:
                p.wait(sy, "sync", last_inproj_mm)
            for kcc in range(16):
                e = sy.dma_start(out=wq_l[:, kcc * 128:(kcc + 1) * 128],
                                 in_=wqkv_bf[h * 128:(h + 1) * 128, kcc * 128:(kcc + 1) * 128], transpose=True)
                p.inc(e, "wld", 16)
                e = sy.dma_start(out=wk_l[:, kcc * 128:(kcc + 1) * 128],
                                 in_=wqkv_bf[QD + h * 128: QD + (h + 1) * 128, kcc * 128:(kcc + 1) * 128], transpose=True)
                p.inc(e, "wld", 16)
                e = sy.dma_start(out=wv_l[:, kcc * 256:(kcc + 1) * 256],
                                 in_=wqkv_bf[2 * QD + h * 256: 2 * QD + (h + 1) * 256, kcc * 128:(kcc + 1) * 128], transpose=True)
                p.inc(e, "wld", 16)
                e = sy.dma_start(out=wz_l[:, kcc * 256:(kcc + 1) * 256],
                                 in_=wz_bf[h * 256:(h + 1) * 256, kcc * 128:(kcc + 1) * 128], transpose=True)
                p.inc(e, "wld", 16)

        parts = [(qc, wq_l, 0, 128, h), (kc_, wk_l, 0, 128, 16 + h),
                 (vc[0], wv_l, 0, 256, 32 + 2 * h), (vc[1], wv_l, 128, 256, 33 + 2 * h)]
        for (dst, wbuf, coff, wstride, cc) in parts:
            for tt in range(4):
                bnk, ps_, free = pool.acquire()

                def mmfn(t, tt=tt, ps_=ps_, free=free, wbuf=wbuf, coff=coff, wstride=wstride):
                    p.wait(t, "tensor", free, p.tok("wld"), p.tok("xnld"), p.tok("mld"))
                    e = None
                    for kcc in range(16):
                        e = t.matmul(ps_[:, :], wbuf[:, kcc * wstride + coff: kcc * wstride + coff + 128],
                                     xn_sb[:, kcc * T + tt * 512: kcc * T + tt * 512 + 512],
                                     start=(kcc == 0), stop=(kcc == 15))
                    return e
                tk = _chain(p, [
                    ("tensor", mmfn, []),
                    ("vector", lambda v, tt=tt, ps_=ps_: v.tensor_copy(qraw[:, 4 + tt * 512: 4 + (tt + 1) * 512], ps_[:, :]), []),
                ])
                pool.release(bnk, tk)
            steps = [("vector", lambda v, cc=cc: v.tensor_scalar(dst[:, :], qraw[:, 1:1 + T], cw_sb[:, cc * 4 + 0: cc * 4 + 1], None, ALU.mult), [])]
            for j in range(1, 4):
                steps.append(("vector", lambda v, j=j, cc=cc, dst=dst: v.scalar_tensor_tensor(
                    dst[:, :], qraw[:, 1 + j:1 + j + T], cw_sb[:, cc * 4 + j: cc * 4 + j + 1], dst[:, :], ALU.mult, ALU.add), []))
            _chain(p, steps)

        for pi, buf in enumerate((qc, kc_)):
            _chain(p, [("scalar", lambda s, buf=buf: s.activation(qraw[:, 4:4 + T], buf[:, :], AF.Square), [])])
            for tt in range(4):
                _chain(p, [
                    ("tensor", lambda t, tt=tt: t.matmul(bk0[0:1, 0:512], onescol_b[:, :], qraw[:, 4 + tt * 512:4 + (tt + 1) * 512], start=True, stop=True), []),
                    ("vector", lambda v, tt=tt, pi=pi: v.tensor_copy(ssrow[0:1, tt * 512:(tt + 1) * 512], bk0[0:1, 0:512]), []),
                ])
            _chain(p, [
                ("vector", lambda v, pi=pi: v.tensor_scalar_max(ssrow[0:1, :], ssrow[0:1, :], 1e-24), []),
                ("scalar", lambda s, pi=pi: s.activation(ssrow[0:1, :], ssrow[0:1, :], AF.Ln), []),
                ("vector", lambda v, pi=pi: v.tensor_scalar_mul(ssrow[0:1, :], ssrow[0:1, :], -0.5), []),
                ("scalar", lambda s, pi=pi: s.activation(ssrow[0:1, :], ssrow[0:1, :], AF.Exp), []),
            ])
            for tt in range(4):
                _chain(p, [
                    ("tensor", lambda t, tt=tt, pi=pi: t.matmul(bkA[0][:, 0:512], onesf_row[:, :], ssrow[0:1, tt * 512:(tt + 1) * 512], start=True, stop=True), []),
                    ("vector", lambda v, tt=tt, buf=buf: v.tensor_tensor(buf[:, tt * 512:(tt + 1) * 512], buf[:, tt * 512:(tt + 1) * 512], bkA[0][:, 0:512], ALU.mult), []),
                ])

        for zo in range(2):
            for tt in range(4):
                bnk, ps_, free = pool.acquire()

                def mmfn(t, tt=tt, ps_=ps_, free=free, zo=zo):
                    p.wait(t, "tensor", free)
                    e = None
                    for kcc in range(16):
                        e = t.matmul(ps_[:, :], wz_l[:, kcc * 256 + zo * 128: kcc * 256 + zo * 128 + 128],
                                     xn_sb[:, kcc * T + tt * 512: kcc * T + tt * 512 + 512],
                                     start=(kcc == 0), stop=(kcc == 15))
                    return e
                tk = _chain(p, [
                    ("tensor", mmfn, []),
                    ("scalar", lambda s, tt=tt, ps_=ps_, zo=zo: s.activation(
                        zsil[:, zo * T + tt * 512: zo * T + (tt + 1) * 512], ps_[:, :], AF.Silu), []),
                ])
                pool.release(bnk, tk)
        last_inproj_mm = p.tok("cht")

        for c in range(16):
            csl = slice(c * 128, (c + 1) * 128)
            _chain(p, ([("scalar", lambda s: preload_set6(s), [])] if (c == 0 and USE_PRELOAD) else []) + [
                ("tensor", lambda t: t.matmul(bk0[:, 0:128], kc_[:, csl], kc_[:, csl], start=True, stop=True), []),
                ("vector", lambda v: v.tensor_copy(KK_sb[:, :], bk0[:, 0:128]), []),
                ("tensor", lambda t: t.matmul(bk0[:, 128:256], kc_[:, csl], qc[:, csl], start=True, stop=True), []),
                ("vector", lambda v: v.tensor_copy(KQ_sb[:, :], bk0[:, 128:256]), []),
                ("tensor", lambda t: t.transpose(bk0.bitcast(BF16)[:, 512:640], kc_[:, csl], identb[:, :]), []),
                ("vector", lambda v: v.tensor_copy(KT_sb[:, :], bk0.bitcast(BF16)[:, 512:640]), []),
                ("tensor", lambda t: t.transpose(bk0.bitcast(BF16)[:, 640:768], vc[0][:, csl], identb[:, :]), []),
                ("vector", lambda v: v.tensor_copy(VT_sb[0][:, :], bk0.bitcast(BF16)[:, 640:768]), []),
                ("tensor", lambda t: t.transpose(bk0.bitcast(BF16)[:, 768:896], vc[1][:, csl], identb[:, :]), []),
                ("vector", lambda v: v.tensor_copy(VT_sb[1][:, :], bk0.bitcast(BF16)[:, 768:896]), []),
            ])
            for pp_ in range(2):
                vh = 2 * h + pp_

                @p.block.gpsimd
                def _(g, vh=vh, pp_=pp_, c=c):
                    p.wait(g, "gpsimd", p.tok("chv"), p.tok("chs"), p.tok("cht"))
                    e = g.dma_start(out=rows_buf[0:1, pp_ * 512:(pp_ + 1) * 512],
                                    in_=qrow1[vh:vh + 97:32, c * 128:(c + 1) * 128])
                    p.inc(e, "rowld", 16)

            chains = []
            for pp_ in range(2):
                vh = 2 * h + pp_
                ro = pp_ * 512
                i = pp_
                lA_c, lB_c, nb_c, elB_c = col("lA", vh, c), col("lB", vh, c), col("negb", vh, c), col("elB", vh, c)
                elA_c, w_c = col("elA", vh, c), col("w", vh, c)
                Ssl = S_bf[:, vh * 128:(vh + 1) * 128]
                mco = 4 * (pp_ + 1)
                ka, kb = bkA[i], bkB[i]

                def step(engname, fn, extra=()):
                    return (engname, fn, list(extra))

                def mk(i=i, ka=ka, kb=kb, lA_c=lA_c, lB_c=lB_c, nb_c=nb_c, elB_c=elB_c,
                       elA_c=elA_c, w_c=w_c, Ssl=Ssl, mco=mco, ro=ro, pp_=pp_, c=c, csl=csl, hg=hg):
                    st = [
                        step("tensor", lambda t: t.matmul(ka[:, 0:128], onesf_row[:, :], rows_buf[0:1, ro + 0:ro + 128], start=True, stop=True), [p.tok("rowld")]),
                        step("tensor", lambda t: t.matmul(ka[:, 128:256], onesf_row[:, :], rows_buf[0:1, ro + 128:ro + 256], start=True, stop=True)),
                        step("tensor", lambda t: t.matmul(ka[:, 256:384], onesf_row[:, :], rows_buf[0:1, ro + 256:ro + 384], start=True, stop=True)),
                        step("tensor", lambda t: t.matmul(ka[:, 384:512], negonesf_row[:, :], rows_buf[0:1, ro + 0:ro + 128], start=True, stop=True)),
                        step("vector", lambda v: v.scalar_tensor_tensor(Gt[i][:, :], ka[:, 0:128], lA_c, m_ui[:, :], ALU.subtract, ALU.add)),
                        step("scalar", lambda s: s.activation(Et[i][:, :], Gt[i][:, :], AF.Exp)),
                        step("vector", lambda v: v.tensor_tensor(MT_sb[i][:, :], Et[i][:, :], KQ_sb[:, :], ALU.mult)),
                        step("vector", lambda v: v.scalar_tensor_tensor(Gt[i][:, :], ka[:, 128:256], lA_c, m_us[:, :], ALU.subtract, ALU.add)),
                        step("scalar", lambda s: s.activation(Et[i][:, :], Gt[i][:, :], AF.Exp)),
                        step("vector", lambda v: v.tensor_tensor(Et[i][:, :], Et[i][:, :], KK_sb[:, :], ALU.mult)),
                        step("vector", lambda v: v.tensor_tensor(Et[i][:, :], Et[i][:, :], ka[:, 256:384], ALU.mult)),
                        step("vector", lambda v: v.tensor_add(XT_sb[i][:, :], identf[:, :], Et[i][:, :])),
                        step("vector", lambda v: v.tensor_sub(R_sb[i][:, :], identf[:, :], Et[i][:, :])),
                        step("vector", lambda v: v.scalar_tensor_tensor(Gt[i][:, :], ka[:, 384:512], lB_c, m_ls[:, :], ALU.add, ALU.add)),
                        step("scalar", lambda s: s.activation(Et[i][:, :], Gt[i][:, :], AF.Exp)),
                        step("vector", lambda v: v.tensor_mul(Et[i][:, :], Et[i][:, :], KK_sb[:, :])),
                        step("vector", lambda v: v.scalar_tensor_tensor(X_sb[i][:, :], Et[i][:, :], nb_c, identf[:, :], ALU.mult, ALU.add)),
                    ]
                    for it in range(6):
                        st += [
                            step("tensor", lambda t: t.matmul(kb[:, 0:128], R_sb[i][:, :], X_sb[i][:, :], start=True, stop=True)),
                            step("vector", lambda v: v.scalar_tensor_tensor(W2_sb[i][:, :], kb[:, 0:128], -1.0, twoI[:, :], ALU.mult, ALU.add)),
                            step("tensor", lambda t: t.matmul(kb[:, 128:256], XT_sb[i][:, :], W2_sb[i][:, :], start=True, stop=True)),
                            step("tensor", lambda t: t.matmul(kb[:, 256:384], W2_sb[i][:, :], XT_sb[i][:, :], start=True, stop=True)),
                            step("vector", lambda v: v.tensor_copy(X_sb[i][:, :], kb[:, 128:256])),
                            step("scalar", lambda s: s.copy(XT_sb[i][:, :], kb[:, 256:384])),
                        ]
                    st += [
                        step("tensor", lambda t: t.matmul(ka[:, 0:128], kc_[:, csl], Ssl, start=True, stop=True)),
                        step("vector", lambda v: v.scalar_tensor_tensor(Gt[i][:, :], ka[:, 0:128], elB_c, VT_sb[pp_][:, :], ALU.mult, ALU.subtract)),
                        step("vector", lambda v: v.tensor_scalar(R_sb[i][:, :], Gt[i][:, :], nb_c, None, ALU.mult)),
                        step("tensor", lambda t: t.matmul(kb[:, 384:512], XT_sb[i][:, :], R_sb[i][:, :], start=True, stop=True)),
                        step("vector", lambda v: v.tensor_copy(U_sb[i][:, :], kb[:, 384:512])),
                        step("vector", lambda v: v.tensor_scalar(U2_sb[i][:, :], kb[:, 384:512], w_c, None, ALU.mult)),
                        step("tensor", lambda t: t.matmul(ka[:, 128:256], qc[:, csl], Ssl, start=True, stop=True)),
                        step("vector", lambda v: v.tensor_scalar(Gt[i][:, :], ka[:, 128:256], elA_c, None, ALU.mult)),
                        step("tensor", lambda t: t.matmul(ka[:, 256:384], KT_sb[:, :], U2_sb[i][:, :], start=True, stop=True)),
                        step("vector", lambda v: v.tensor_scalar(misc[:, mco:mco + 1], w_c, elA_c, None, ALU.mult)),
                        step("vector", lambda v: v.scalar_tensor_tensor(Ssl, Ssl, misc[:, mco:mco + 1], ka[:, 256:384], ALU.mult, ALU.add)),
                        step("tensor", lambda t: t.matmul(ka[:, 384:512], MT_sb[i][:, :], U_sb[i][:, :], start=True, stop=True)),
                        step("vector", lambda v: v.tensor_add(O_sb[i][:, :], ka[:, 384:512], Gt[i][:, :])),
                        step("scalar", lambda s: s.activation(Gt[i][:, :], O_sb[i][:, :], AF.Square, accum_out=misc[:, mco + 1:mco + 2])),
                        step("vector", lambda v: v.tensor_scalar(misc[:, mco + 2:mco + 3], misc[:, mco + 1:mco + 2], 1.0 / 128, 1e-6, ALU.mult, ALU.add)),
                        step("scalar", lambda s: s.activation(misc[:, mco + 2:mco + 3], misc[:, mco + 2:mco + 3], AF.Ln)),
                        step("vector", lambda v: v.tensor_scalar_mul(misc[:, mco + 3:mco + 4], misc[:, mco + 2:mco + 3], -0.5)),
                        step("scalar", lambda s: s.activation(misc[:, mco + 3:mco + 4], misc[:, mco + 3:mco + 4], AF.Exp)),
                        step("vector", lambda v: v.scalar_tensor_tensor(On_sb[i][:, :], O_sb[i][:, :], misc[:, mco + 3:mco + 4], nw_rep[:, :], ALU.mult, ALU.mult)),
                        step("tensor", lambda t: t.transpose(kb.bitcast(BF16)[:, 0:128], On_sb[i][:, :], identb[:, :])),
                        step("vector", lambda v: v.tensor_tensor(g_sb[:, (hg * 2 + pp_) * T + c * 128: (hg * 2 + pp_) * T + (c + 1) * 128],
                                                                 kb.bitcast(BF16)[:, 0:128], zsil[:, pp_ * T + c * 128: pp_ * T + (c + 1) * 128], ALU.mult)),
                    ]
                    return st
                chains.append(mk())
            _chains_zip(p, chains)

        if hg == 1:
            h0 = h - 1
            g_ready = p.tok("chv")
            prev_og_mm = None
            for og in range(4):
                @p.block.sync
                def _(sy, og=og, h0=h0):
                    p.wait(sy, "sync", p.tok("pw_o"))
                    if prev_og_mm is not None:
                        p.wait(sy, "sync", prev_og_mm)
                    for j in range(4):
                        vhg = 2 * h0 + j
                        e = sy.dma_start(out=wodb[:, j * 512:(j + 1) * 512],
                                         in_=wout_bf[og * 512:(og + 1) * 512, vhg * 128:(vhg + 1) * 128], transpose=True)
                        p.inc(e, "wod", 16)

                for ol in range(4):
                    for tt in range(4):
                        bnk, ps_, free = pool.acquire()

                        def mmfn(t, ps_=ps_, free=free, ol=ol, tt=tt, g_ready=g_ready):
                            p.wait(t, "tensor", p.tok("wod"), free, g_ready)
                            e = None
                            for j in range(4):
                                e = t.matmul(ps_[:, :], wodb[:, j * 512 + ol * 128: j * 512 + ol * 128 + 128],
                                             g_sb[:, j * T + tt * 512: j * T + (tt + 1) * 512],
                                             start=(j == 0), stop=(j == 3))
                            return e
                        sb_i = (ol * 4 + tt) % 2

                        def evfn(s, sb_i=sb_i, ps_=ps_):
                            p.wait(s, "scalar", stg_free_da[sb_i])
                            return s.copy(stg[sb_i][:], ps_[:, :])
                        tke = _chain(p, [("tensor", mmfn, []), ("scalar", evfn, [])])
                        pool.release(bnk, tke)

                        @p.block.gpsimd
                        def _(g, og=og, ol=ol, tt=tt, sb_i=sb_i, tke=tke):
                            p.wait(g, "gpsimd", tke, p.tok("cpacc"), acc_base)
                            e = g.dma_start(out=xacc[og * 512 + ol * 128: og * 512 + ol * 128 + 128, tt * 512:(tt + 1) * 512],
                                            in_=stg[sb_i][:], accum_op=ALU.add)
                            stg_free_da[sb_i] = p.inc(e, "acc", 16)
                prev_og_mm = p.tok("cht")

    @p.block.sync
    def _(sy):
        p.final_wait(sy, "sync")


stg_free_da = [None, None]


def build_l0_attn():
    global stg_free_da
    stg_free_da = [None, None]
    p = Prog("l0attn")
    with p.es:
        csb, cemit = load_consts(p, _CN["l0attn"])
        with p.nc.Block() as blk:
            p.block = blk

            @blk.gpsimd
            def _(g):
                cemit(g)
            emit_l0_attn(p, csb)
    return p


def emit_ln_fm(p, csb):
    xacc_in = p.dram("xacc_in", [D, T], F32, kind="ExternalInput")
    xn1 = p.dram("xn1", [D, T], BF16, kind="ExternalOutput")
    onesf_row = csb["onesf_row"]
    onescol_b = csb["onescol_b"]
    p.cscalar(1e-5)
    p.cscalar(-0.5)

    xbf = p.sbuf("lf_xbf", [128, 16 * T], BF16)
    xf32 = [p.sbuf(f"lf_x{i}", [128, T], F32) for i in range(2)]
    sq = p.sbuf("lf_sq", [128, 512], BF16)
    rows = p.sbuf("lf_rows", [1, 2 * T], F32)
    rscr = p.sbuf("lf_rscr", [1, 2 * 512], F32)
    tscr = p.sbuf("lf_t", [128, 512], F32)
    xnb = [p.sbuf(f"lf_xn{i}", [128, 512], BF16) for i in range(2)]
    pst = [p.psum(f"lf_pst{i}", [1, 512], F32) for i in range(2)]
    prep = [p.psum(f"lf_prep{i}", [128, 512], F32) for i in range(2)]

    for dc in range(16):
        b = dc % 2

        @p.block.gpsimd
        def _(g, dc=dc, b=b):
            p.wait_snap(g, "gpsimd", dc - 2, ["chv"])
            e = g.dma_start(out=xf32[b][:], in_=xacc_in[dc * 128:(dc + 1) * 128, :])
            p.inc(e, f"lfld{b}", 16)
        _chain(p, [
            ("vector", lambda v, dc=dc, b=b: v.tensor_copy(xbf[:, dc * T:(dc + 1) * T], xf32[b][:]), [p.tok(f"lfld{b}")]),
        ])
        p.snapshot(dc)

    for tt in range(4):
        for dc in range(16):
            _chain(p, [
                ("scalar", lambda s, dc=dc, tt=tt: s.activation(sq[:, :], xbf[:, dc * T + tt * 512: dc * T + tt * 512 + 512], AF.Square), []),
                ("tensor", lambda t, dc=dc, tt=tt: t.matmul(pst[0][0:1, :], onescol_b[:, :],
                                                            xbf[:, dc * T + tt * 512: dc * T + tt * 512 + 512],
                                                            start=(dc == 0), stop=(dc == 15)), []),
                ("tensor", lambda t, dc=dc: t.matmul(pst[1][0:1, :], onescol_b[:, :], sq[:, :],
                                                     start=(dc == 0), stop=(dc == 15)), []),
            ])
        _chain(p, [
            ("vector", lambda v, tt=tt: v.tensor_scalar_mul(rows[0:1, tt * 512:(tt + 1) * 512], pst[0][0:1, :], -1.0 / D), []),
            ("vector", lambda v, tt=tt: v.tensor_scalar_mul(rscr[0:1, 0:512], pst[1][0:1, :], 1.0 / D), []),
            ("vector", lambda v, tt=tt: v.tensor_mul(rscr[0:1, 512:1024], rows[0:1, tt * 512:(tt + 1) * 512], rows[0:1, tt * 512:(tt + 1) * 512]), []),
            ("vector", lambda v: v.tensor_sub(rscr[0:1, 0:512], rscr[0:1, 0:512], rscr[0:1, 512:1024]), []),
            ("scalar", lambda s: s.activation(rscr[0:1, 0:512], rscr[0:1, 0:512], AF.Ln, bias=1e-5), []),
            ("scalar", lambda s, tt=tt: s.activation(rows[0:1, T + tt * 512: T + (tt + 1) * 512], rscr[0:1, 0:512], AF.Exp, scale=-0.5), []),
        ])
        _chain(p, [
            ("tensor", lambda t, tt=tt: t.matmul(prep[0][:, :], onesf_row[:, :], rows[0:1, tt * 512:(tt + 1) * 512], start=True, stop=True), []),
            ("tensor", lambda t, tt=tt: t.matmul(prep[1][:, :], onesf_row[:, :], rows[0:1, T + tt * 512: T + (tt + 1) * 512], start=True, stop=True), []),
        ])
        for dc in range(16):
            b = dc % 2

            def xfn1(v, dc=dc, tt=tt, b=b):
                p.wait(v, "vector", p.tok(f"lfst{b}"))
                return v.tensor_tensor(tscr[:, :], xbf[:, dc * T + tt * 512: dc * T + tt * 512 + 512], prep[0][:, :], ALU.add)
            tk = _chain(p, [
                ("vector", xfn1, []),
                ("vector", lambda v, b=b: v.tensor_tensor(xnb[b][:, :], tscr[:, :], prep[1][:, :], ALU.mult), []),
            ])

            @p.block.gpsimd
            def _(g, dc=dc, tt=tt, b=b, tk=tk):
                p.wait(g, "gpsimd", tk)
                e = g.dma_start(out=xn1[dc * 128:(dc + 1) * 128, tt * 512:(tt + 1) * 512], in_=xnb[b][:, :])
                p.inc(e, f"lfst{b}", 16)

    @p.block.sync
    def _(sy):
        p.final_wait(sy, "sync")


def build_ln_fm():
    p = Prog("lnfm")
    with p.es:
        csb, cemit = load_consts(p, ["onesf_row", "onescol_b"])
        with p.nc.Block() as blk:
            p.block = blk

            @blk.gpsimd
            def _(g):
                cemit(g)
            emit_ln_fm(p, csb)
    return p


def emit_l1_attn(p, csb):
    DI, DS, NH, HD = 4096, 128, 64, 64
    CVD = DI + 2 * DS
    DIP = 2 * DI + 2 * DS + NH
    xn1 = p.dram("xn1_in", [D, T], BF16, kind="ExternalInput")
    xacc_in = p.dram("xacc_in", [D, T], F32, kind="ExternalInput")
    xacc = p.dram("xacc", [D, T], F32, kind="ExternalOutput")
    in_proj_w = p.dram("in_proj_w", [DIP, D], F32, kind="ExternalInput")
    conv_w = p.dram("conv_w", [CVD, 4], F32, kind="ExternalInput")
    conv_b = p.dram("conv_b", [CVD], F32, kind="ExternalInput")
    dt_bias = p.dram("dt_bias", [NH], F32, kind="ExternalInput")
    A_log = p.dram("A_log", [NH], F32, kind="ExternalInput")
    Dvec = p.dram("Dvec", [NH], F32, kind="ExternalInput")
    norm_weight = p.dram("norm_weight", [DI], F32, kind="ExternalInput")
    out_proj_w = p.dram("out_proj_w", [D, DI], F32, kind="ExternalInput")

    wip_bf = p.dram("wip_bf", [8704, D], BF16)
    wop_bf = p.dram("wop_bf", [D, DI], BF16)
    z_sp = p.dram("z_sp", [DI, T], BF16)
    xc_sp = p.dram("xc_sp", [CVD, T], BF16)
    yn_sp = p.dram("yn_sp", [DI, T], BF16)

    identf = csb["identf"]
    identb = csb["identb"]
    onesf_row = csb["onesf_row"]
    m_ui = csb["m_upincl_neg"]
    for v_ in (1e-5, 1.0 / DI, -0.5, -1.0, 1.0, 0.0):
        p.cscalar(v_)

    xn_sb = p.sbuf("ma_xn", [128, 16 * T], BF16)
    wob = xn_sb[:, 0:16384]
    ynt = xn_sb[:, 16384:32768]
    stg = [p.sbuf(f"ma_stg{i}", [128, 512], F32) for i in range(2)]
    y_TM = p.sbuf("ma_y", [128, 4096], F32)
    dt_raw = y_TM[0:64, 0:T]
    dscr = y_TM[0:64, T:2 * T]
    mrow1 = p.sbuf("ma_mrow1", [128, T], F32)
    mrow2 = p.sbuf("ma_mrow2", [128, T], F32)
    cols_m = p.sbuf("ma_cols", [128, 16 * 256], F32)
    wipb = cols_m[:, :].bitcast(BF16)
    rows_all = p.sbuf("ma_rows", [1, 16 * 128], F32)
    cw_sb = p.sbuf("ma_cw", [128, 34 * 4], F32)
    cb_sb = p.sbuf("ma_cb", [128, 34], F32)
    cbT_sb = p.sbuf("ma_cbT", [34, 128], F32)
    misc = p.sbuf("ma_misc", [128, 16], F32)
    nw_rep = p.sbuf("ma_nw", [128, 4096], BF16)
    nwrow = p.sbuf("ma_nwrow", [1, 512], F32)
    Drep = p.sbuf("ma_Drep", [128, NH], F32)
    Drow = p.sbuf("ma_Drow", [1, NH], F32)
    xch = p.sbuf("ma_xch", [128, 32 * 128], BF16)
    zch = p.sbuf("ma_zch", [128, 32 * 128], BF16)
    xcraw = xch[:, 0:T + 4]
    xcv = zch[:, 0:T]
    bc_sb = p.sbuf("ma_bc", [128, 256], BF16)
    BC_sb = p.sbuf("ma_BC", [128, 128], F32)
    BTM_sb = p.sbuf("ma_BTM", [128, 128], BF16)
    NLANE = 6
    Btil = [p.sbuf(f"ma_Btil{i}", [128, 128], BF16) for i in range(NLANE)]
    MT_sb = [p.sbuf(f"ma_MT{i}", [128, 128], BF16) for i in range(NLANE)]
    Gm = [p.sbuf(f"ma_Gm{i}", [128, 128], F32) for i in range(NLANE)]
    Xdt_sb = [p.sbuf(f"ma_Xdt{i}", [128, 64], BF16) for i in range(NLANE)]
    Ydsb = [p.sbuf(f"ma_Yd{i}", [128, 64], F32) for i in range(NLANE)]
    t5 = [p.sbuf(f"ma_t5{i}", [128, 64], F32) for i in range(NLANE)]
    zT = p.sbuf("ma_zT", [128, 4096], BF16)
    yn_TM = p.sbuf("ma_ynTM", [128, 4096], BF16)
    ynF = zT
    state = p.sbuf("ma_state", [128, NH * HD], BF16)

    pool = PsumPool(p, "ma_pp", 4)
    mb0 = p.psum("ma_mb0", [128, 512], F32)
    mb1 = p.psum("ma_mb1", [128, 512], F32)
    mb2 = p.psum("ma_mb2", [128, 512], F32)
    mb3 = p.psum("ma_mb3", [128, 512], F32)
    lane_bank = [mb1, mb2, mb3, pool.tiles[0], pool.tiles[1], pool.tiles[2]]

    @p.block.gpsimd
    def _(g):
        emit_prep_cast(p, g, in_proj_w, wip_bf[0:DIP, :], "pw_ip")
        emit_prep_cast(p, g, out_proj_w, wop_bf, "pw_op")
        e = g.memset(xch[:, 0:D], 0.0)
        p.inc(e, "mld", 1)
        g.wait_ge(p.sem("mld"), 1)
        e = g.dma_start(out=wip_bf[8512:8640, :], in_=xch[:, 0:D])
        p.inc(e, "pw_ip", 16)
        e = g.dma_start(out=wip_bf[8640:8704, :], in_=xch[0:64, 0:D])
        p.inc(e, "pw_ip", 16)
        e = g.dma_start(out=cw_sb[:, :].rearrange("p (c j) -> p c j", j=4),
                        in_=conv_w[0:4352, :].rearrange("(c q) j -> q c j", q=128))
        p.inc(e, "mld", 16)
        e = g.dma_start(out=cbT_sb[0:34, :], in_=conv_b[:].rearrange("(c q) -> c q", q=128))
        p.inc(e, "mld", 16)
        e = g.dma_start(out=misc[0:NH, 0:1], in_=dt_bias[:, None])
        p.inc(e, "mld", 16)
        e = g.dma_start(out=misc[0:NH, 1:2], in_=A_log[:, None])
        p.inc(e, "mld", 16)
        e = g.dma_start(out=Drow[0:1, :], in_=Dvec[None, :])
        p.inc(e, "mld", 16)
        for i in range(4):
            e = g.dma_start(out=xacc[i * 512:(i + 1) * 512, :], in_=xacc_in[i * 512:(i + 1) * 512, :])
            p.inc(e, "cpacc", 16)

    @p.block.sync
    def _(sy):
        emit_load_xn(p, sy, xn1, xn_sb, "xnld", tm=False)

    _chain(p, [
        ("tensor", lambda t: t.transpose(mb0[:, 0:34], cbT_sb[0:34, :], identf[0:34, 0:34]), [p.tok("mld"), p.tok("cld")]),
        ("vector", lambda v: v.tensor_copy(cb_sb[:, :], mb0[:, 0:34]), []),
    ])

    for a in range(8):
        @p.block.gpsimd
        def _(g, a=a):
            p.wait(g, "gpsimd", p.tok("cht"))
            e = g.dma_start(out=nwrow[0:1, :], in_=norm_weight[a * 512:(a + 1) * 512][None, :])
            p.inc(e, "mld2", 16)
        _chain(p, [
            ("tensor", lambda t, a=a: t.matmul(mb0[:, :], onesf_row[:, :], nwrow[0:1, :], start=True, stop=True), [p.tok("mld2"), p.tok("cld")]),
            ("vector", lambda v, a=a: v.tensor_copy(nw_rep[:, a * 512:(a + 1) * 512], mb0[:, :]), []),
        ])
    _chain(p, [
        ("tensor", lambda t: t.matmul(mb0[:, 0:NH], onesf_row[:, :], Drow[0:1, :], start=True, stop=True), [p.tok("mld")]),
        ("vector", lambda v: v.tensor_copy(Drep[:, :], mb0[:, 0:NH]), []),
    ])

    last_og_mm = None
    for og in range(17):
        @p.block.sync
        def _(sy, og=og):
            p.wait(sy, "sync", p.tok("pw_ip"))
            if last_og_mm is not None:
                p.wait(sy, "sync", last_og_mm)
            for kcc in range(16):
                e = sy.dma_start(out=wipb[:, kcc * 512:(kcc + 1) * 512],
                                 in_=wip_bf[og * 512:(og + 1) * 512, kcc * 128:(kcc + 1) * 128], transpose=True)
                p.inc(e, "wipld", 16)

        for ol in range(4):
            ocg = og * 4 + ol
            if ocg >= 67:
                continue
            is_dt = (ocg == 66)
            is_z = (ocg < 32)
            cc = ocg - 32 if (not is_z and not is_dt) else None
            for tt in range(4):
                bnk, ps_, free = pool.acquire()

                def mmfn(t, ps_=ps_, free=free, og=og, ol=ol, tt=tt):
                    p.wait(t, "tensor", free, p.tok("wipld"), p.tok("xnld"), p.tok("mld"))
                    e = None
                    for kcc in range(16):
                        e = t.matmul(ps_[:, :], wipb[:, kcc * 512 + ol * 128: kcc * 512 + ol * 128 + 128],
                                     xn_sb[:, kcc * T + tt * 512: kcc * T + tt * 512 + 512],
                                     start=(kcc == 0), stop=(kcc == 15))
                    return e
                if is_z:
                    def evfn(s, ps_=ps_, ocg=ocg, tt=tt):
                        p.wait(s, "scalar", stg_free_ma[tt % 2])
                        return s.activation(stg[tt % 2][:, :].bitcast(BF16)[:, 0:512], ps_[:, :], AF.Silu)
                    tk = _chain(p, [("tensor", mmfn, []), ("scalar", evfn, [])])
                    pool.release(bnk, tk)

                    @p.block.gpsimd
                    def _(g, ocg=ocg, tt=tt, tk=tk):
                        p.wait(g, "gpsimd", tk)
                        e = g.dma_start(out=z_sp[ocg * 128:(ocg + 1) * 128, tt * 512:(tt + 1) * 512],
                                        in_=stg[tt % 2][:, :].bitcast(BF16)[:, 0:512])
                        stg_free_ma[tt % 2] = p.inc(e, "zspst", 16)
                elif is_dt:
                    tk = _chain(p, [
                        ("tensor", mmfn, []),
                        ("vector", lambda v, ps_=ps_, tt=tt: v.tensor_copy(dt_raw[:, tt * 512:(tt + 1) * 512], ps_[0:64, :]), []),
                    ])
                    pool.release(bnk, tk)
                else:
                    tk = _chain(p, [
                        ("tensor", mmfn, []),
                        ("vector", lambda v, ps_=ps_, tt=tt: v.tensor_copy(xcraw[:, 4 + tt * 512: 4 + (tt + 1) * 512], ps_[:, :]), []),
                    ])
                    pool.release(bnk, tk)
            if cc is not None:
                steps = [("gpsimd", lambda g: g.memset(xcraw[:, 0:4], 0.0), [])]
                steps.append(("vector", lambda v, cc=cc: v.tensor_scalar(xcv[:, :], xcraw[:, 1:1 + T], cw_sb[:, cc * 4: cc * 4 + 1], None, ALU.mult), []))
                for j in range(1, 4):
                    steps.append(("vector", lambda v, j=j, cc=cc: v.scalar_tensor_tensor(
                        xcv[:, :], xcraw[:, 1 + j:1 + j + T], cw_sb[:, cc * 4 + j: cc * 4 + j + 1], xcv[:, :], ALU.mult, ALU.add), []))
                steps.append(("vector", lambda v, cc=cc: v.tensor_scalar(xcv[:, :], xcv[:, :], cb_sb[:, cc:cc + 1], None, ALU.add), []))
                steps.append(("scalar", lambda s, cc=cc: s.activation(xcv[:, :], xcv[:, :], AF.Silu), []))
                tk = _chain(p, steps)

                @p.block.gpsimd
                def _(g, cc=cc, tk=tk):
                    p.wait(g, "gpsimd", tk)
                    e = g.dma_start(out=xc_sp[cc * 128:(cc + 1) * 128, :], in_=xcv[:, :])
                    p.inc(e, "xcspst", 16)
        last_og_mm = p.tok("cht")

    _chain(p, [
        ("scalar", lambda s: s.activation(dscr[:, :], dt_raw[:, :], AF.Abs, bias=misc[0:NH, 0:1]), [p.tok("xcspst")]),
        ("scalar", lambda s: s.activation(dscr[:, :], dscr[:, :], AF.Exp, scale=-1.0), []),
        ("scalar", lambda s: s.activation(dscr[:, :], dscr[:, :], AF.Ln, bias=1.0), []),
        ("vector", lambda v: v.tensor_scalar(dt_raw[:, :], dt_raw[:, :], misc[0:NH, 0:1], 0.0, ALU.add, ALU.max), []),
        ("vector", lambda v: v.tensor_add(dt_raw[:, :], dt_raw[:, :], dscr[:, :]), []),
        ("vector", lambda v: v.tensor_copy(mrow1[0:64, :], dt_raw[:, :]), []),
        ("scalar", lambda s: s.activation(misc[0:NH, 2:3], misc[0:NH, 1:2], AF.Exp), []),
        ("vector", lambda v: v.tensor_scalar_mul(misc[0:NH, 3:4], misc[0:NH, 2:3], -1.0), []),
        ("vector", lambda v: v.tensor_scalar(dscr[:, :], dt_raw[:, :], misc[0:NH, 3:4], None, ALU.mult), []),
    ])
    for c in range(16):
        sl = slice(c * 128, (c + 1) * 128)
        _chain(p, [
            ("vector", lambda v, sl=sl: v.tensor_tensor_scan(mrow1[64:128, sl], dscr[:, sl], dscr[:, sl], 0.0, ALU.add, ALU.bypass), []),
            ("vector", lambda v, sl=sl, c=c: v.tensor_scalar(dt_raw[:, sl], mrow1[64:128, sl], -1.0, mrow1[64:128, c * 128 + 127:c * 128 + 128], ALU.mult, ALU.add), []),
            ("scalar", lambda s, sl=sl: s.activation(mrow2[0:64, sl], dt_raw[:, sl], AF.Exp), []),
        ])
    _chain(p, [
        ("scalar", lambda s: s.activation(mrow2[64:128, :], mrow1[64:128, :], AF.Exp), []),
    ])
    for c in range(16):
        _chain(p, [
            ("tensor", lambda t, c=c: t.transpose(mb0[:, 0:128], mrow1[:, c * 128:(c + 1) * 128], identf[:, :]), []),
            ("vector", lambda v, c=c: v.tensor_copy(cols_m[:, c * 256: c * 256 + 128], mb0[:, 0:128]), []),
            ("tensor", lambda t, c=c: t.transpose(mb0[:, 128:256], mrow2[:, c * 128:(c + 1) * 128], identf[:, :]), []),
            ("vector", lambda v, c=c: v.tensor_copy(cols_m[:, c * 256 + 128: c * 256 + 256], mb0[:, 128:256]), []),
        ])
    _chain(p, [("vector", lambda v: v.memset(state[:, :], 0.0), [])])

    def mcol(q, h, c):
        off = {"dt": 0, "lA": 64, "ds": 128, "elA": 192}[q]
        return cols_m[:, c * 256 + off + h: c * 256 + off + h + 1]

    for c in range(16):
        @p.block.sync
        def _(sy, c=c):
            p.wait(sy, "sync", p.tok("xcspst"), p.tok("zspst"))
            p.wait(sy, "sync", p.tok("cht"), p.tok("chv"), p.tok("chs"))
            e = sy.dma_start(out=xch[:, :].rearrange("p (a t) -> p a t", t=128),
                             in_=xc_sp[0:DI, c * 128:(c + 1) * 128].rearrange("(a p) t -> p a t", p=128))
            p.inc(e, "chld", 16)
            e = sy.dma_start(out=zch[:, :].rearrange("p (a t) -> p a t", t=128),
                             in_=z_sp[0:DI, c * 128:(c + 1) * 128].rearrange("(a p) t -> p a t", p=128))
            p.inc(e, "chld", 16)
            e = sy.dma_start(out=bc_sb[:, :].rearrange("p (a t) -> p a t", t=128),
                             in_=xc_sp[DI:CVD, c * 128:(c + 1) * 128].rearrange("(a p) t -> p a t", p=128))
            p.inc(e, "chld", 16)


        _chain(p, ([("scalar", lambda s: preload_set6(s), [])] if (c == 0 and USE_PRELOAD) else []) + [
            ("tensor", lambda t: t.matmul(mb0[:, 0:128], bc_sb[:, 0:128], bc_sb[:, 128:256], start=True, stop=True), [p.tok("chld")]),
            ("vector", lambda v: v.tensor_copy(BC_sb[:, :], mb0[:, 0:128]), []),
            ("tensor", lambda t: t.transpose(mb0.bitcast(BF16)[:, 512:640], bc_sb[:, 0:128], identb[:, :]), []),
            ("vector", lambda v: v.tensor_copy(BTM_sb[:, :], mb0.bitcast(BF16)[:, 512:640]), []),
        ])
        for hb, g0 in [(hb, g0) for hb in range(0, NH, 16) for g0 in range(0, 16, 6)]:
            if g0 == 0:
                @p.block.gpsimd
                def _(g, c=c, hb=hb):
                    p.wait(g, "gpsimd", p.tok("cht"), p.tok("chv"))
                    e = g.dma_start(out=rows_all[0:1, :], in_=mrow1[64 + hb:64 + hb + 16, c * 128:(c + 1) * 128])
                    p.inc(e, "rowld", 16)
            chains = []
            for li, h in enumerate(range(hb + g0, min(hb + g0 + 6, hb + 16))):
                a = h // 2
                pb = (h % 2) * 64
                dt_c, lA_c = mcol("dt", h, c), mcol("lA", h, c)
                ds_c, elA_c = mcol("ds", h, c), mcol("elA", h, c)
                kb = lane_bank[li]

                def step(engname, fn, extra=()):
                    return (engname, fn, list(extra))

                def mk(li=li, h=h, a=a, pb=pb, dt_c=dt_c, lA_c=lA_c, ds_c=ds_c, elA_c=elA_c, kb=kb, c=c):
                    kbb = kb.bitcast(BF16)
                    return [
                        step("tensor", lambda t: t.transpose(kbb[:, 640:704], xch[pb:pb + 64, a * 128:(a + 1) * 128], identb[pb:pb + 64, pb:pb + 64]), [p.tok("rowld")]),
                        step("vector", lambda v: v.tensor_scalar(Xdt_sb[li][:, :], kbb[:, 640:704], dt_c, None, ALU.mult)),
                        step("tensor", lambda t: t.matmul(kb[:, 0:128], onesf_row[:, :], rows_all[0:1, (h % 16) * 128:(h % 16) * 128 + 128], start=True, stop=True)),
                        step("vector", lambda v: v.scalar_tensor_tensor(Gm[li][:, :], kb[:, 0:128], lA_c, m_ui[:, :], ALU.subtract, ALU.add)),
                        step("scalar", lambda s: s.activation(Gm[li][:, :], Gm[li][:, :], AF.Exp)),
                        step("vector", lambda v: v.tensor_tensor(MT_sb[li][:, :], Gm[li][:, :], BC_sb[:, :], ALU.mult)),
                        step("tensor", lambda t: t.matmul(kb[:, 128:192], MT_sb[li][:, :], Xdt_sb[li][:, :], start=True, stop=True)),
                        step("scalar", lambda s: s.copy(Ydsb[li][:, :], kb[:, 128:192])),
                        step("tensor", lambda t, h=h: t.matmul(kb[:, 192:256], bc_sb[:, 128:256], state[:, h * 64:(h + 1) * 64], start=True, stop=True)),
                        step("vector", lambda v: v.scalar_tensor_tensor(t5[li][:, :], kb[:, 192:256], elA_c, Ydsb[li][:, :], ALU.mult, ALU.add)),
                        step("vector", lambda v, h=h: v.scalar_tensor_tensor(y_TM[:, h * 64:(h + 1) * 64], kbb[:, 640:704], Drep[:, h:h + 1], t5[li][:, :], ALU.mult, ALU.add)),
                        step("vector", lambda v: v.tensor_scalar(Btil[li][:, :], BTM_sb[:, :], ds_c, None, ALU.mult)),
                        step("tensor", lambda t: t.matmul(kb[:, 256:320], Btil[li][:, :], Xdt_sb[li][:, :], start=True, stop=True)),
                        step("vector", lambda v: v.tensor_scalar(misc[:, 4 + li:5 + li], ds_c, elA_c, None, ALU.mult)),
                        step("vector", lambda v, h=h: v.scalar_tensor_tensor(state[:, h * 64:(h + 1) * 64], state[:, h * 64:(h + 1) * 64], misc[:, 4 + li:5 + li], kb[:, 256:320], ALU.mult, ALU.add)),
                    ]
                chains.append(mk())
            _chains_zip(p, chains)

        for a in range(32):
            _chain(p, [
                ("tensor", lambda t, a=a: t.transpose(mb0.bitcast(BF16)[:, 0:128], zch[:, a * 128:(a + 1) * 128], identb[:, :]), []),
                ("vector", lambda v, a=a: v.tensor_copy(zT[:, a * 128:(a + 1) * 128], mb0.bitcast(BF16)[:, 0:128]), []),
            ])
        _chain(p, [
            ("vector", lambda v: v.tensor_tensor(y_TM[:, :], y_TM[:, :], zT[:, :], ALU.mult), []),
            ("scalar", lambda s: s.activation(zT[:, :], y_TM[:, :], AF.Square, accum_out=misc[:, 5:6]), []),
            ("vector", lambda v: v.tensor_scalar(misc[:, 6:7], misc[:, 5:6], 1.0 / DI, 1e-5, ALU.mult, ALU.add), []),
            ("scalar", lambda s: s.activation(misc[:, 6:7], misc[:, 6:7], AF.Ln), []),
            ("vector", lambda v: v.tensor_scalar_mul(misc[:, 7:8], misc[:, 6:7], -0.5), []),
            ("scalar", lambda s: s.activation(misc[:, 7:8], misc[:, 7:8], AF.Exp), []),
            ("vector", lambda v: v.scalar_tensor_tensor(yn_TM[:, :], y_TM[:, :], misc[:, 7:8], nw_rep[:, :], ALU.mult, ALU.mult), []),
        ])
        for a in range(32):
            _chain(p, [
                ("tensor", lambda t, a=a: t.transpose(mb0.bitcast(BF16)[:, 0:128], yn_TM[:, a * 128:(a + 1) * 128], identb[:, :]), []),
                ("vector", lambda v, a=a: v.tensor_copy(ynF[:, a * 128:(a + 1) * 128], mb0.bitcast(BF16)[:, 0:128]), []),
            ])
        tk = p.tok("chv")

        @p.block.gpsimd
        def _(g, c=c, tk=tk):
            p.wait(g, "gpsimd", tk)
            e = g.dma_start(out=yn_sp[0:DI, c * 128:(c + 1) * 128].rearrange("(a p) t -> p a t", p=128),
                            in_=ynF[:, :].rearrange("p (a t) -> p a t", t=128))
            p.inc(e, "ynspst", 16)
    yn_ready = p.tok("ynspst")
    prev_og = None
    for og in range(4):
        @p.block.sync
        def _(sy, og=og):
            p.wait(sy, "sync", p.tok("pw_op"))
            if prev_og is not None:
                p.wait(sy, "sync", prev_og)
            for kcc in range(32):
                e = sy.dma_start(out=wob[:, kcc * 512:(kcc + 1) * 512],
                                 in_=wop_bf[og * 512:(og + 1) * 512, kcc * 128:(kcc + 1) * 128], transpose=True)
                p.inc(e, "wopld", 16)

        for tt in range(4):
            @p.block.sync
            def _(sy, tt=tt, og=og):
                p.wait(sy, "sync", p.tok("ynspst"))
                if og > 0 or tt > 0:
                    p.wait(sy, "sync", p.tok("cht"))
                e = sy.dma_start(out=ynt[:, :].rearrange("p (a t) -> p a t", t=512),
                                 in_=yn_sp[0:DI, tt * 512:(tt + 1) * 512].rearrange("(a p) t -> p a t", p=128))
                p.inc(e, "yntld", 16)

            for ol in range(4):
                bnk, ps_, free = pool.acquire()

                def mmfn(t, ps_=ps_, free=free, og=og, ol=ol, tt=tt):
                    p.wait(t, "tensor", free, p.tok("wopld"), p.tok("yntld"))
                    e = None
                    for kcc in range(32):
                        e = t.matmul(ps_[:, :], wob[:, kcc * 512 + ol * 128: kcc * 512 + ol * 128 + 128],
                                     ynt[:, kcc * 512:(kcc + 1) * 512],
                                     start=(kcc == 0), stop=(kcc == 31))
                    return e

                def evfn(s, ps_=ps_, ol=ol, tt=tt):
                    p.wait(s, "scalar", stg_free_ma[(ol + tt) % 2])
                    return s.copy(stg[(ol + tt) % 2][:, :], ps_[:, :])
                tk2 = _chain(p, [("tensor", mmfn, []), ("scalar", evfn, [])])
                pool.release(bnk, tk2)

                @p.block.gpsimd
                def _(g, og=og, ol=ol, tt=tt, tk2=tk2):
                    p.wait(g, "gpsimd", tk2, p.tok("cpacc"))
                    e = g.dma_start(out=xacc[og * 512 + ol * 128: og * 512 + ol * 128 + 128, tt * 512:(tt + 1) * 512],
                                    in_=stg[(ol + tt) % 2][:, :], accum_op=ALU.add)
                    stg_free_ma[(ol + tt) % 2] = p.inc(e, "acc", 16)
        prev_og = p.tok("cht")

    @p.block.sync
    def _(sy):
        p.final_wait(sy, "sync")


stg_free_ma = [None, None]


def build_l1_attn():
    global stg_free_ma
    stg_free_ma = [None, None]
    p = Prog("l1attn")
    with p.es:
        csb, cemit = load_consts(p, ["identf", "identb", "onesf_row", "m_upincl_neg"])
        with p.nc.Block() as blk:
            p.block = blk

            @blk.gpsimd
            def _(g):
                cemit(g)
            emit_l1_attn(p, csb)
    return p


def emit_l1_ffn(p, csb):
    INTER = 8192
    xn1 = p.dram("xn1_in", [D, T], BF16, kind="ExternalInput")
    xacc_in = p.dram("xacc_in", [D, T], F32, kind="ExternalInput")
    xacc = p.dram("xacc", [D, T], F32, kind="ExternalOutput")
    SU_ug = p.dram("SU_ug", [D], F32, kind="ExternalInput")
    SU_dp = p.dram("SU_dp", [INTER], F32, kind="ExternalInput")
    Wscale_ug = p.dram("Wscale_ug", [2 * INTER], F32, kind="ExternalInput")
    Wscale_dp = p.dram("Wscale_dp", [D], F32, kind="ExternalInput")
    W_up = p.dram("W_up", [INTER, D], F32, kind="ExternalInput")
    W_gate = p.dram("W_gate", [INTER, D], F32, kind="ExternalInput")
    W_down = p.dram("W_down", [D, INTER], F32, kind="ExternalInput")

    wu_bf = p.dram("wu_bf", [INTER, D], BF16)
    wg_bf = p.dram("wg_bf", [INTER, D], BF16)
    wd_bf = p.dram("wd_bf", [D, INTER], BF16)
    hadA1 = p.dram("hadA1", [D, T], BF16)
    hadB1 = p.dram("hadB1", [D, T], BF16)
    hadA2 = p.dram("hadA2", [INTER, T], BF16)
    hadB2 = p.dram("hadB2", [INTER, T], BF16)

    identf = csb["identf"]
    IH16 = csb["IH16"]
    IH64 = csb["IH64"]
    H128 = csb["H128"]
    RS2 = 1.0 / float(np.sqrt(2048.0))
    RS8 = 1.0 / float(np.sqrt(8192.0))

    xn_sb = p.sbuf("qf_xn", [128, 16 * T], BF16)
    y2t = xn_sb[:, 0:32768]
    wgb = p.sbuf("qf_wgb", [128, 16 * 512], BF16)
    wub = p.sbuf("qf_wub", [128, 16 * 512], BF16)
    wdob = p.sbuf("qf_wdob", [128, 64 * 256], BF16)
    hbuf = p.sbuf("qf_hbuf", [128, T], BF16)
    tabT = p.sbuf("qf_tabT", [128, 128], F32)
    su_ug = p.sbuf("qf_suug", [128, 16], F32)
    su_dp = p.sbuf("qf_sudp", [128, 64], F32)
    wsc = p.sbuf("qf_wsc", [128, 128], F32)
    wsc_dp = p.sbuf("qf_wscdp", [128, 16], F32)
    sup = p.sbuf("qf_sup", [128, 64], F32)
    sgt = p.sbuf("qf_sgt", [128, 64], F32)
    sdp = p.sbuf("qf_sdp", [128, 16], F32)
    sgs = [p.sbuf(f"qf_sg{i}", [128, 512], F32) for i in range(2)]
    stg = [p.sbuf(f"qf_stg{i}", [128, 512], F32) for i in range(2)]
    evb = [p.sbuf(f"qf_ev{i}", [128, 512], BF16) for i in range(2)]

    pool = PsumPool(p, "qf_pp", 6)

    @p.block.gpsimd
    def _(g):
        emit_prep_cast(p, g, W_up, wu_bf, "pw_u")
        emit_prep_cast(p, g, W_gate, wg_bf, "pw_g")
        emit_prep_cast(p, g, W_down, wd_bf, "pw_d")
        e = g.dma_start(out=tabT[0:16, 0:128], in_=SU_ug[:].rearrange("(c q) -> c q", q=128))
        p.inc(e, "tld1", 16)
        for i in range(4):
            e = g.dma_start(out=xacc[i * 512:(i + 1) * 512, :], in_=xacc_in[i * 512:(i + 1) * 512, :])
            p.inc(e, "cpacc", 16)

    bnk0, ps0, _f = pool.acquire()
    _chain(p, [
        ("tensor", lambda t: t.transpose(ps0[:, 0:16], tabT[0:16, 0:128], identf[0:16, 0:16]), [p.tok("tld1"), p.tok("cld")]),
        ("vector", lambda v: v.tensor_copy(su_ug[:, :], ps0[:, 0:16]), []),
    ])

    @p.block.gpsimd
    def _(g):
        p.wait(g, "gpsimd", p.tok("chv"))
        e = g.dma_start(out=tabT[0:64, 0:128], in_=SU_dp[:].rearrange("(c q) -> c q", q=128))
        p.inc(e, "tld2", 16)
    _chain(p, [
        ("tensor", lambda t: t.transpose(ps0[:, 0:64], tabT[0:64, 0:128], identf[0:64, 0:64]), [p.tok("tld2")]),
        ("vector", lambda v: v.tensor_copy(su_dp[:, :], ps0[:, 0:64]), []),
    ])

    @p.block.gpsimd
    def _(g):
        p.wait(g, "gpsimd", p.tok("chv"))
        e = g.dma_start(out=tabT[0:128, 0:128], in_=Wscale_ug[:].rearrange("(c q) -> c q", q=128))
        p.inc(e, "tld3", 16)
    _chain(p, [
        ("tensor", lambda t: t.transpose(ps0[:, 0:128], tabT[:, 0:128], identf[:, :]), [p.tok("tld3")]),
        ("vector", lambda v: v.tensor_copy(wsc[:, :], ps0[:, 0:128]), []),
    ])

    @p.block.gpsimd
    def _(g):
        p.wait(g, "gpsimd", p.tok("chv"))
        e = g.dma_start(out=tabT[0:16, 0:128], in_=Wscale_dp[:].rearrange("(c q) -> c q", q=128))
        p.inc(e, "tld4", 16)
    _chain(p, [
        ("tensor", lambda t: t.transpose(ps0[:, 0:16], tabT[0:16, 0:128], identf[0:16, 0:16]), [p.tok("tld4")]),
        ("vector", lambda v: v.tensor_copy(wsc_dp[:, :], ps0[:, 0:16]), []),
        ("vector", lambda v: v.tensor_mul(sup[:, :], wsc[:, 0:64], su_dp[:, :]), []),
        ("vector", lambda v: v.tensor_scalar_mul(sup[:, :], sup[:, :], RS2), []),
        ("vector", lambda v: v.tensor_scalar_mul(sgt[:, :], wsc[:, 64:128], RS2), []),
        ("vector", lambda v: v.tensor_scalar_mul(sdp[:, :], wsc_dp[:, :], RS8), []),
    ])
    pool.release(bnk0, p.tok("chv"))

    @p.block.sync
    def _(sy):
        emit_load_xn(p, sy, xn1, xn_sb, "xnld", tm=False)

    for dc in range(16):
        _chain(p, [
            ("vector", lambda v, dc=dc: v.tensor_scalar(xn_sb[:, dc * T:(dc + 1) * T], xn_sb[:, dc * T:(dc + 1) * T],
                                                        su_ug[:, dc:dc + 1], None, ALU.mult), [p.tok("xnld")]),
        ])
        for tt in range(4):
            bnk, ps_, free = pool.acquire()
            b = (dc * 4 + tt) % 2

            def mmfn(t, dc=dc, tt=tt, ps_=ps_, free=free):
                p.wait(t, "tensor", free)
                return t.matmul(ps_[:, :], IH16[:, :], xn_sb[:, dc * T + tt * 512: dc * T + tt * 512 + 512],
                                start=True, stop=True)

            def evfn(v, ps_=ps_, b=b):
                p.wait(v, "vector", evfree_qf[b])
                return v.tensor_copy(evb[b][:, :], ps_[:, :])
            tk = _chain(p, [("tensor", mmfn, []), ("vector", evfn, [])])
            pool.release(bnk, tk)

            @p.block.gpsimd
            def _(g, dc=dc, tt=tt, b=b, tk=tk):
                p.wait(g, "gpsimd", tk)
                e = g.dma_start(out=hadA1[dc * 128:(dc + 1) * 128, tt * 512:(tt + 1) * 512], in_=evb[b][:, :])
                evfree_qf[b] = p.inc(e, "hA1", 16)

    vq_mm_tok = [None, None]
    for it_ in range(64):
        q, tt = it_ // 4, it_ % 4
        bnk, ps_, free = pool.acquire()
        b = it_ % 2

        @p.block.sync
        def _(sy, q=q, tt=tt, b=b):
            p.wait(sy, "sync", p.tok("hA1"), vq_mm_tok[b])
            e = sy.dma_start(out=vq_qf[b][:, :], in_=hadA1[q:D:16, tt * 512:(tt + 1) * 512])
            p.inc(e, "b1ld", 16)

        def mmfn(t, ps_=ps_, free=free, b=b):
            p.wait(t, "tensor", free, p.tok("b1ld"))
            return t.matmul(ps_[:, :], H128[:, :], vq_qf[b][:, :], start=True, stop=True)

        def evfn(v, ps_=ps_, b=b):
            p.wait(v, "vector", evfree_qf[b])
            return v.tensor_copy(evb[b][:, :], ps_[:, :])
        tkm = _chain(p, [("tensor", mmfn, [])], nobar=True)
        vq_mm_tok[b] = tkm
        tk = _chain(p, [("vector", evfn, [tkm])], nobar=True)
        pool.release(bnk, tk)

        @p.block.gpsimd
        def _(g, q=q, tt=tt, b=b, tk=tk):
            p.wait(g, "gpsimd", tk)
            e = g.dma_start(out=hadB1[q:D:16, tt * 512:(tt + 1) * 512], in_=evb[b][:, :])
            evfree_qf[b] = p.inc(e, "hB1", 16)

    @p.block.sync
    def _(sy):
        p.wait(sy, "sync", p.tok("hB1"), p.tok("cht"), p.tok("chv"))
        for dc in range(16):
            e = sy.dma_start(out=xn_sb[:, dc * T:(dc + 1) * T], in_=hadB1[dc * 128:(dc + 1) * 128, :])
            p.inc(e, "y1ld", 16)

    last_og = None
    for og in range(16):
        @p.block.sync
        def _(sy, og=og):
            p.wait(sy, "sync", p.tok("pw_u"), p.tok("pw_g"))
            if last_og is not None:
                p.wait(sy, "sync", last_og)
            for kcc in range(16):
                e = sy.dma_start(out=wgb[:, kcc * 512:(kcc + 1) * 512],
                                 in_=wg_bf[og * 512:(og + 1) * 512, kcc * 128:(kcc + 1) * 128], transpose=True)
                p.inc(e, "wugld", 16)
                e = sy.dma_start(out=wub[:, kcc * 512:(kcc + 1) * 512],
                                 in_=wu_bf[og * 512:(og + 1) * 512, kcc * 128:(kcc + 1) * 128], transpose=True)
                p.inc(e, "wugld", 16)

        for ol in range(4):
            oc = og * 4 + ol
            for tt in range(4):
                bg, gps, gfree = pool.acquire()
                bu, ups, ufree = pool.acquire()

                def mmfn(t, gps=gps, ups=ups, gfree=gfree, ufree=ufree, ol=ol, tt=tt):
                    p.wait(t, "tensor", gfree, ufree, p.tok("wugld"), p.tok("y1ld"))
                    e = None
                    for kcc in range(16):
                        e = t.matmul(gps[:, :], wgb[:, kcc * 512 + ol * 128: kcc * 512 + ol * 128 + 128],
                                     xn_sb[:, kcc * T + tt * 512: kcc * T + tt * 512 + 512],
                                     start=(kcc == 0), stop=(kcc == 15))
                    globals()["_qf_g"] = p.inc(e, "cht", 1)
                    for kcc in range(16):
                        e = t.matmul(ups[:, :], wub[:, kcc * 512 + ol * 128: kcc * 512 + ol * 128 + 128],
                                     xn_sb[:, kcc * T + tt * 512: kcc * T + tt * 512 + 512],
                                     start=(kcc == 0), stop=(kcc == 15))
                    return e
                b = tt % 2

                def prescale(v, gps=gps, oc=oc, b=b):
                    p.wait(v, "vector", sgfree_qf[b])
                    return v.tensor_scalar(sgs[b][:, :], gps[:, :], sgt[:, oc % 64: oc % 64 + 1], None, ALU.mult)

                def sgfn(s, b=b):
                    return s.activation(sgs[b][:, :], sgs[b][:, :], AF.Silu)

                def hfn(v, ups=ups, oc=oc, tt=tt, b=b):
                    return v.scalar_tensor_tensor(hbuf[:, tt * 512:(tt + 1) * 512], ups[:, :],
                                                  sup[:, oc % 64: oc % 64 + 1], sgs[b][:, :], ALU.mult, ALU.mult)
                tk = _chain(p, [("tensor", mmfn, []), ("vector", prescale, []), ("scalar", sgfn, []), ("vector", hfn, [])])
                pool.release(bg, tk)
                pool.release(bu, tk)
                sgfree_qf[b] = tk
            for tt in range(4):
                bnk, ps_, free = pool.acquire()
                b = tt % 2

                def mmfn2(t, tt=tt, ps_=ps_, free=free):
                    p.wait(t, "tensor", free)
                    return t.matmul(ps_[:, :], IH64[:, :], hbuf[:, tt * 512:(tt + 1) * 512], start=True, stop=True)

                def evfn2(v, ps_=ps_, b=b):
                    p.wait(v, "vector", evfree_qf[b])
                    return v.tensor_copy(evb[b][:, :], ps_[:, :])
                tk = _chain(p, [("tensor", mmfn2, []), ("vector", evfn2, [])])
                pool.release(bnk, tk)

                @p.block.gpsimd
                def _(g, oc=oc, tt=tt, b=b, tk=tk):
                    p.wait(g, "gpsimd", tk)
                    e = g.dma_start(out=hadA2[oc * 128:(oc + 1) * 128, tt * 512:(tt + 1) * 512], in_=evb[b][:, :])
                    evfree_qf[b] = p.inc(e, "hA2", 16)
        last_og = p.tok("cht")

    vq_mm_tok = [None, None]
    first_b2 = [True]
    for it_ in range(256):
        q, tt = it_ // 4, it_ % 4
        bnk, ps_, free = pool.acquire()
        b = it_ % 2

        @p.block.sync
        def _(sy, q=q, tt=tt, b=b):
            p.wait(sy, "sync", p.tok("hA2"), vq_mm_tok[b])
            if first_b2[0]:
                p.wait(sy, "sync", p.tok("cht"))
                first_b2[0] = False
            e = sy.dma_start(out=vq_qf[b][:, :], in_=hadA2[q:INTER:64, tt * 512:(tt + 1) * 512])
            p.inc(e, "b2ld", 16)

        def mmfn(t, ps_=ps_, free=free, b=b):
            p.wait(t, "tensor", free, p.tok("b2ld"))
            return t.matmul(ps_[:, :], H128[:, :], vq_qf[b][:, :], start=True, stop=True)

        def evfn(v, ps_=ps_, b=b):
            p.wait(v, "vector", evfree_qf[b])
            return v.tensor_copy(evb[b][:, :], ps_[:, :])
        tkm = _chain(p, [("tensor", mmfn, [])], nobar=True)
        vq_mm_tok[b] = tkm
        tk = _chain(p, [("vector", evfn, [tkm])], nobar=True)
        pool.release(bnk, tk)

        @p.block.gpsimd
        def _(g, q=q, tt=tt, b=b, tk=tk):
            p.wait(g, "gpsimd", tk)
            e = g.dma_start(out=hadB2[q:INTER:64, tt * 512:(tt + 1) * 512], in_=evb[b][:, :])
            evfree_qf[b] = p.inc(e, "hB2", 16)

    acc_base = p.tok("acc")
    prev_og = None
    for og in range(8):
        @p.block.sync
        def _(sy, og=og):
            p.wait(sy, "sync", p.tok("pw_d"))
            if prev_og is not None:
                p.wait(sy, "sync", prev_og)
            for kcc in range(64):
                e = sy.dma_start(out=wdob[:, kcc * 256:(kcc + 1) * 256],
                                 in_=wd_bf[og * 256:(og + 1) * 256, kcc * 128:(kcc + 1) * 128], transpose=True)
                p.inc(e, "wdld", 16)

        for tt in range(4):
            @p.block.sync
            def _(sy, tt=tt, og=og):
                p.wait(sy, "sync", p.tok("hB2"), p.tok("cht"))
                for kcc in range(64):
                    e = sy.dma_start(out=y2t[:, kcc * 512:(kcc + 1) * 512],
                                     in_=hadB2[kcc * 128:(kcc + 1) * 128, tt * 512:(tt + 1) * 512])
                    p.inc(e, "y2ld", 16)

            for ol in range(2):
                oc = og * 2 + ol
                bnk, ps_, free = pool.acquire()

                def mmfn(t, ps_=ps_, free=free, ol=ol):
                    p.wait(t, "tensor", free, p.tok("wdld"), p.tok("y2ld"))
                    e = None
                    for kcc in range(64):
                        e = t.matmul(ps_[:, :], wdob[:, kcc * 256 + ol * 128: kcc * 256 + ol * 128 + 128],
                                     y2t[:, kcc * 512:(kcc + 1) * 512],
                                     start=(kcc == 0), stop=(kcc == 63))
                    return e
                sb_i = (tt + ol) % 2

                def evfn(v, ps_=ps_, oc=oc, sb_i=sb_i):
                    p.wait(v, "vector", dnfree_qf[sb_i])
                    return v.tensor_scalar(stg[sb_i][:, :], ps_[:, :], sdp[:, oc % 16: oc % 16 + 1], None, ALU.mult)
                tk = _chain(p, [("tensor", mmfn, []), ("vector", evfn, [])])
                pool.release(bnk, tk)

                @p.block.gpsimd
                def _(g, oc=oc, tt=tt, sb_i=sb_i, tk=tk):
                    p.wait(g, "gpsimd", tk, p.tok("cpacc"), acc_base)
                    e = g.dma_start(out=xacc[oc * 128:(oc + 1) * 128, tt * 512:(tt + 1) * 512],
                                    in_=stg[sb_i][:, :], accum_op=ALU.add)
                    dnfree_qf[sb_i] = p.inc(e, "acc", 16)
        prev_og = p.tok("cht")

    @p.block.sync
    def _(sy):
        p.final_wait(sy, "sync")


evfree_qf = [None, None]
sgfree_qf = [None, None]
dnfree_qf = [None, None]
vq_qf = [None, None]


def build_l1_ffn():
    global evfree_qf, sgfree_qf, dnfree_qf
    evfree_qf = [None, None]
    sgfree_qf = [None, None]
    dnfree_qf = [None, None]
    p = Prog("l1ffn")
    with p.es:
        csb, cemit = load_consts(p, ["identf", "IH16", "IH64", "H128"])
        vq_qf[0] = p.sbuf("qf_vq", [128, 512], BF16)
        vq_qf[1] = p.sbuf("qf_vq2", [128, 512], BF16)
        with p.nc.Block() as blk:
            p.block = blk

            @blk.gpsimd
            def _(g):
                cemit(g)
            emit_l1_ffn(p, csb)
    return p


_CN = {
    "ln0": ["identf"],
    "l0attn": ["identf", "identb", "ones_row", "onesf_row", "negonesf_row", "onescol_b",
               "m_upincl_neg", "m_upstrict_neg", "m_lowstrict_neg", "twoI"],
    "l0moe": ["identf", "ones_row"],
    "lnfm": ["onesf_row", "onescol_b"],
    "l1attn": ["identf", "identb", "onesf_row", "m_upincl_neg"],
    "l1ffn": ["identf", "IH16", "IH64", "H128"],
    "final": ["identf"],
}

ALL_CONSTS = ["identf", "identb", "ones_row", "onesf_row", "negonesf_row", "onescol_b",
              "m_upincl_neg", "m_upstrict_neg", "m_lowstrict_neg", "twoI",
              "IH16", "IH64", "H128"]


def build_fused():
    global stg_free_da, stg_free_ma, evfree_qf, sgfree_qf, dnfree_qf
    p = Prog("fused")
    with p.es:
        csb, cemit = load_consts(p, ALL_CONSTS)
        vq_qf[0] = p.es.enter_context(p.nc.sbuf_tensor("qf_vq", [128, 512], BF16))
        vq_qf[1] = p.es.enter_context(p.nc.sbuf_tensor("qf_vq2", [128, 512], BF16))
        x_in = p.dram("g_x_in", [T, D], F32, kind="ExternalInput")
        y_out = p.dram("g_y_out", [T, D], F32, kind="ExternalOutput")
        xaccA = p.dram("g_xaccA", [D, T], F32)
        xaccB = p.dram("g_xaccB", [D, T], F32)
        xn0t = p.dram("g_xn0", [T, D], BF16)
        xn1t = p.dram("g_xn1", [D, T], BF16)
        W = {}
        wspecs = [
            ("L0_attn_W_qkv", [8192, D]), ("L0_attn_W_z", [4096, D]),
            ("L0_attn_W_b", [32, D]), ("L0_attn_W_a", [32, D]),
            ("L0_attn_conv_w", [8192, 4]), ("L0_attn_dt_bias", [32]),
            ("L0_attn_A_log", [32]), ("L0_attn_norm_w", [128]),
            ("L0_attn_W_out", [D, 4096]),
            ("L0_ffn_gate_w", [8, D]), ("L0_ffn_w_gate", [8, 1024, D]),
            ("L0_ffn_w_up", [8, 1024, D]), ("L0_ffn_w_down", [8, D, 1024]),
            ("L0_ffn_sh_gate", [D, D]), ("L0_ffn_sh_up", [D, D]),
            ("L0_ffn_sh_down", [D, D]), ("L0_ffn_se_gate_w", [1, D]),
            ("L1_attn_in_proj_w", [8512, D]), ("L1_attn_conv_w", [4352, 4]),
            ("L1_attn_conv_b", [4352]), ("L1_attn_dt_bias", [64]),
            ("L1_attn_A_log", [64]), ("L1_attn_D", [64]),
            ("L1_attn_norm_weight", [4096]), ("L1_attn_out_proj_w", [D, 4096]),
            ("L1_ffn_SU_ug", [D]), ("L1_ffn_SU_dp", [8192]),
            ("L1_ffn_Wscale_ug", [16384]), ("L1_ffn_Wscale_dp", [D]),
            ("L1_ffn_W_up", [8192, D]), ("L1_ffn_W_gate", [8192, D]),
            ("L1_ffn_W_down", [D, 8192]),
        ]
        for n, shp in wspecs:
            W[n] = p.dram(n, shp, F32, kind="ExternalInput")

        with p.nc.Block() as blk:
            p.block = blk

            @blk.gpsimd
            def _(g):
                cemit(g)

            def phase(pfx, fn, io):
                p.push_frame(pfx)
                p.io = io
                fn(p, csb)
                p.io = {}
                p.barrier()
                p.pop_frame()

            phase("P0_", emit_ln0, {"x_in": x_in, "xacc": xaccA, "xn0": xn0t})
            stg_free_da = [None, None]
            phase("P1_", emit_l0_attn, {
                "xn0_in": xn0t, "xacc_in": xaccA, "xacc": xaccB,
                "W_qkv": W["L0_attn_W_qkv"], "W_z": W["L0_attn_W_z"],
                "W_b": W["L0_attn_W_b"], "W_a": W["L0_attn_W_a"],
                "conv_w": W["L0_attn_conv_w"], "dt_bias": W["L0_attn_dt_bias"],
                "A_log": W["L0_attn_A_log"], "norm_w": W["L0_attn_norm_w"],
                "W_out": W["L0_attn_W_out"]})
            phase("P2_", emit_l0_moe, {
                "xn0_in": xn0t, "xacc_in": xaccB, "xacc": xaccA,
                "gate_w": W["L0_ffn_gate_w"], "w_gate": W["L0_ffn_w_gate"],
                "w_up": W["L0_ffn_w_up"], "w_down": W["L0_ffn_w_down"],
                "sh_gate": W["L0_ffn_sh_gate"], "sh_up": W["L0_ffn_sh_up"],
                "sh_down": W["L0_ffn_sh_down"], "se_gate_w": W["L0_ffn_se_gate_w"]})
            phase("P3_", emit_ln_fm, {"xacc_in": xaccA, "xn1": xn1t})
            stg_free_ma = [None, None]
            phase("P4_", emit_l1_attn, {
                "xn1_in": xn1t, "xacc_in": xaccA, "xacc": xaccB,
                "in_proj_w": W["L1_attn_in_proj_w"], "conv_w": W["L1_attn_conv_w"],
                "conv_b": W["L1_attn_conv_b"], "dt_bias": W["L1_attn_dt_bias"],
                "A_log": W["L1_attn_A_log"], "Dvec": W["L1_attn_D"],
                "norm_weight": W["L1_attn_norm_weight"], "out_proj_w": W["L1_attn_out_proj_w"]})
            evfree_qf = [None, None]
            sgfree_qf = [None, None]
            dnfree_qf = [None, None]
            phase("P5_", emit_l1_ffn, {
                "xn1_in": xn1t, "xacc_in": xaccB, "xacc": xaccA,
                "SU_ug": W["L1_ffn_SU_ug"], "SU_dp": W["L1_ffn_SU_dp"],
                "Wscale_ug": W["L1_ffn_Wscale_ug"], "Wscale_dp": W["L1_ffn_Wscale_dp"],
                "W_up": W["L1_ffn_W_up"], "W_gate": W["L1_ffn_W_gate"],
                "W_down": W["L1_ffn_W_down"]})
            phase("P6_", emit_final, {"xacc_in": xaccA, "y_out": y_out})
    return p


FUSED = True
USE_PRELOAD = False

_PROGS = {}


def _prog(name):
    if name not in _PROGS:
        _PROGS[name] = {
            "ln0": build_ln0, "l0attn": build_l0_attn, "l0moe": build_l0_moe,
            "lnfm": build_ln_fm, "l1attn": build_l1_attn, "l1ffn": build_l1_ffn,
            "final": build_final, "fused": build_fused,
        }[name]()
    return _PROGS[name]


def kernel(**inputs):
    cores = list(range(NCORE))
    x = np.ascontiguousarray(np.asarray(inputs["L0_attn_x"], dtype=np.float32))

    if FUSED:
        p = _prog("fused")
        ci = const_inputs(p, ALL_CONSTS)
        in_maps = []
        for b in range(NCORE):
            m = dict(ci)
            m["g_x_in"] = x[b]
            for k in inputs:
                if k != "L0_attn_x":
                    m[k] = inputs[k]
            in_maps.append(m)
        res = run_bass_kernel_spmd(p.nc, in_maps, core_ids=cores)
        return np.stack([np.asarray(res.results[b]["g_y_out"], dtype=np.float32)
                         for b in range(NCORE)], axis=0)

    def run(name, per_core, shared):
        p = _prog(name)
        ci = const_inputs(p, _CN[name])
        in_maps = []
        for b in range(NCORE):
            m = dict(shared)
            m.update(ci)
            m.update({k: v[b] for k, v in per_core.items()})
            in_maps.append(m)
        res = run_bass_kernel_spmd(p.nc, in_maps, core_ids=cores)
        return res.results

    r = run("ln0", {"x_in": [x[b] for b in range(NCORE)]}, {})
    xacc = [r[b]["xacc"] for b in range(NCORE)]
    xn0 = [r[b]["xn0"] for b in range(NCORE)]

    r = run("l0attn", {"xn0_in": xn0, "xacc_in": xacc}, {
        "W_qkv": inputs["L0_attn_W_qkv"], "W_z": inputs["L0_attn_W_z"],
        "W_b": inputs["L0_attn_W_b"], "W_a": inputs["L0_attn_W_a"],
        "conv_w": inputs["L0_attn_conv_w"], "dt_bias": inputs["L0_attn_dt_bias"],
        "A_log": inputs["L0_attn_A_log"], "norm_w": inputs["L0_attn_norm_w"],
        "W_out": inputs["L0_attn_W_out"]})
    xacc = [r[b]["xacc"] for b in range(NCORE)]

    r = run("l0moe", {"xn0_in": xn0, "xacc_in": xacc}, {
        "gate_w": inputs["L0_ffn_gate_w"], "w_gate": inputs["L0_ffn_w_gate"],
        "w_up": inputs["L0_ffn_w_up"], "w_down": inputs["L0_ffn_w_down"],
        "sh_gate": inputs["L0_ffn_sh_gate"], "sh_up": inputs["L0_ffn_sh_up"],
        "sh_down": inputs["L0_ffn_sh_down"], "se_gate_w": inputs["L0_ffn_se_gate_w"]})
    xacc = [r[b]["xacc"] for b in range(NCORE)]

    r = run("lnfm", {"xacc_in": xacc}, {})
    xn1 = [r[b]["xn1"] for b in range(NCORE)]

    r = run("l1attn", {"xn1_in": xn1, "xacc_in": xacc}, {
        "in_proj_w": inputs["L1_attn_in_proj_w"], "conv_w": inputs["L1_attn_conv_w"],
        "conv_b": inputs["L1_attn_conv_b"], "dt_bias": inputs["L1_attn_dt_bias"],
        "A_log": inputs["L1_attn_A_log"], "Dvec": inputs["L1_attn_D"],
        "norm_weight": inputs["L1_attn_norm_weight"], "out_proj_w": inputs["L1_attn_out_proj_w"]})
    xacc = [r[b]["xacc"] for b in range(NCORE)]

    r = run("l1ffn", {"xn1_in": xn1, "xacc_in": xacc}, {
        "SU_ug": inputs["L1_ffn_SU_ug"], "SU_dp": inputs["L1_ffn_SU_dp"],
        "Wscale_ug": inputs["L1_ffn_Wscale_ug"], "Wscale_dp": inputs["L1_ffn_Wscale_dp"],
        "W_up": inputs["L1_ffn_W_up"], "W_gate": inputs["L1_ffn_W_gate"],
        "W_down": inputs["L1_ffn_W_down"]})
    xacc = [r[b]["xacc"] for b in range(NCORE)]

    r = run("final", {"xacc_in": xacc}, {})
    out = np.stack([np.asarray(r[b]["y_out"], dtype=np.float32) for b in range(NCORE)], axis=0)
    return out
